# version 02
# baseline (speedup 1.0000x reference)
import contextlib
import numpy as np
import ml_dtypes
import concourse.bass as bass
import concourse.mybir as mybir
from concourse.bass_utils import run_bass_kernel_spmd
from concourse.bass_core_rust import IndirectOffsetOnAxis

F32 = mybir.dt.float32
BF16 = mybir.dt.bfloat16
I32 = mybir.dt.int32
AF = mybir.ActivationFunctionType
ALU = mybir.AluOpType
BF = ml_dtypes.bfloat16

B, S, D = 2, 8192, 2048
DEBUG_B0 = False
T = 2048
NCORE = 8
KTMAX = [28, 32, 60, 64]
KT2 = [k // 2 for k in KTMAX]
NJOB2 = sum(KT2)
EXP_SHIFT = -30.0
SCALE_QK = float(1.0 / np.sqrt(128.0))
NEG_BIG = -1e30
EPS = 1e-5


def core_blocks(c):
    j = c % 4
    return [j, 7 - j]


def qt_q0(c, qt):
    blkA, blkB = core_blocks(c)
    blk = blkA if qt < 2 else blkB
    return 1024 * blk + 512 * (qt % 2)


def job2_col(qt, kp):
    return sum(KT2[:qt]) + kp



def host_causal_masks_paired():
    out = np.zeros((128, 2 * 1024), dtype=np.float32)
    k = np.arange(128)[:, None]
    q = np.arange(512)[None, :]
    for slot, (d0, d1) in enumerate([(0, 128), (256, 384)]):
        out[:, slot * 1024:slot * 1024 + 512] = (k + d0 <= q)
        out[:, slot * 1024 + 512:(slot + 1) * 1024] = (k + d1 <= q)
    return out.astype(BF)


def host_b0_tables(c):
    bias = np.zeros((128, NJOB2), dtype=np.float32)
    kdiag = np.zeros((128, 4), dtype=np.int32)
    for qt in range(4):
        p0 = (qt_q0(c, qt) + 512) // 128
        kdiag[:, qt] = np.arange(128) * 64 + (p0 - 4)
        for pi in range(KT2[qt]):
            col = job2_col(qt, pi)
            if pi < 2:
                bias[:, col] = EXP_SHIFT
            else:
                bias[:, col] = EXP_SHIFT if (2 * (pi - 2) + 1) < (p0 - 4) else NEG_BIG
    return bias, kdiag


def host_scl():
    scl = np.zeros((128, 8), dtype=np.float32)
    scl[:, 0] = SCALE_QK
    scl[:, 1] = EPS
    scl[:, 2] = 1.0 / 8192.0
    scl[:, 3] = -128.0
    return scl



class Ctx:
    def __init__(self, nc):
        self.nc = nc
        self._cms = []
        self.h = {}
        self.n = {}
        for nm in ["ds", "dg", "dk", "cc", "pe", "act", "dve", "st"]:
            cm = nc.semaphore(nm)
            self.h[nm] = cm.__enter__()
            self._cms.append(cm)
            self.n[nm] = 0

    def close(self):
        for cm in reversed(self._cms):
            cm.__exit__(None, None, None)


def ln_vec_emit(ctx, v, x_ap, ntt, stats, aggr, rstd, dve0, act0, out_ap=None,
                dg_wait=None, scratch32=None):
    if out_ap is None:
        out_ap = x_ap
    if dg_wait is not None:
        v.wait_ge(ctx.h[dg_wait[0]], dg_wait[1])
    for t in range(ntt):
        for ch in range(4):
            src = x_ap[:, t * 2048 + 512 * ch: t * 2048 + 512 * (ch + 1)]
            if scratch32 is not None:
                v.tensor_copy(scratch32[:, 0:512], src)
                v.drain()
                src = scratch32[:, 0:512]
            v.bn_stats(stats[:, t * 24 + 6 * ch: t * 24 + 6 * (ch + 1)], src)
            v.drain()
        v.bn_aggr(aggr[:, 2 * t:2 * t + 2], stats[:, t * 24:(t + 1) * 24])
        v.drain()
    v.drain().then_inc(ctx.h["dve"], 1)
    v.wait_ge(ctx.h["act"], act0 + 1)
    for t in range(ntt):
        v.reciprocal(rstd[:, t:t + 1], rstd[:, t:t + 1])
    v.drain()
    for t in range(ntt):
        v.tensor_scalar(out_ap[:, t * 2048:(t + 1) * 2048],
                        x_ap[:, t * 2048:(t + 1) * 2048],
                        aggr[:, 2 * t:2 * t + 1], rstd[:, t:t + 1],
                        ALU.subtract, ALU.mult)
    v.drain().then_inc(ctx.h["dve"], 1)


def ln_scl_emit(ctx, s, ntt, aggr, rstd, eps_col, dve0, act0):
    s.wait_ge(ctx.h["dve"], dve0 + 1)
    for t in range(ntt):
        s.activation(rstd[:, t:t + 1], aggr[:, 2 * t + 1:2 * t + 2], AF.Sqrt,
                     bias=eps_col, scale=1.0)
    s.drain().then_inc(ctx.h["act"], 1)


def emit_ln_block(ctx, nc, x_ap, ntt, stats, aggr, rstd, eps_col, out_ap=None,
                  dg_wait=None, extra_gp=None, scratch32=None):
    dve0, act0 = ctx.n["dve"], ctx.n["act"]
    with nc.Block() as blk:
        @blk.vector
        def _(v):
            ln_vec_emit(ctx, v, x_ap, ntt, stats, aggr, rstd, dve0, act0,
                        out_ap=out_ap, dg_wait=dg_wait, scratch32=scratch32)

        @blk.scalar
        def _(s):
            ln_scl_emit(ctx, s, ntt, aggr, rstd, eps_col, dve0, act0)

        if extra_gp is not None:
            @blk.gpsimd
            def _(g):
                extra_gp(g)
    ctx.n["dve"] = dve0 + 2
    ctx.n["act"] = act0 + 1


def emit_transpose(ctx, nc, pairs, ident, evac_list, pre_waits=()):
    pe0, act0 = ctx.n["pe"], ctx.n["act"]
    ngrp = len(evac_list)
    with nc.Block() as blk:
        @blk.tensor
        def _(t):
            for sem, val in pre_waits:
                t.wait_ge(ctx.h[sem], val)
            for gi, (grp, _dst) in enumerate(zip(pairs, evac_list)):
                if gi >= 2:
                    t.wait_ge(ctx.h["act"], act0 + gi - 1)
                for (dst_ps, src) in grp:
                    t.matmul(dst_ps, src, ident, start=True, stop=True)
                t.drain().then_inc(ctx.h["pe"], 1)

        @blk.scalar
        def _(s):
            for gi, (dst, ps_ap) in enumerate(evac_list):
                s.wait_ge(ctx.h["pe"], pe0 + gi + 1)
                s.copy(dst, ps_ap)
                s.drain().then_inc(ctx.h["act"], 1)
    ctx.n["pe"] = pe0 + ngrp
    ctx.n["act"] = act0 + ngrp


def emit_tm2fm(ctx, nc, src_tm, dst_fm, ident, ps_tp, ps_tp2, ntt=4):
    pairs, evacs = [], []
    for dt in range(16):
        ps = ps_tp if dt % 2 == 0 else ps_tp2
        grp = []
        for tt in range(ntt):
            grp.append((ps[:, tt * 128:(tt + 1) * 128],
                        src_tm[:, tt * 2048 + dt * 128: tt * 2048 + (dt + 1) * 128]))
        pairs.append(grp)
        evacs.append((dst_fm[:, dt * ntt * 128:(dt + 1) * ntt * 128],
                      ps[:, 0:ntt * 128]))
    emit_transpose(ctx, nc, pairs, ident, evacs)


def emit_proj(ctx, nc, wpool, w_off, x_fm, out_sb, nout, psA, psB,
              bias_sb=None, act_func=None, pre_waits=(), ndm=16, xstride=512):
    pe0, act0 = ctx.n["pe"], ctx.n["act"]
    with nc.Block() as blk:
        @blk.tensor
        def _(t):
            for sem, val in pre_waits:
                t.wait_ge(ctx.h[sem], val)
            for u in range(nout):
                ps = psA if u % 2 == 0 else psB
                if u >= 2:
                    t.wait_ge(ctx.h["act"], act0 + u - 1)
                ins = None
                for dm in range(ndm):
                    ins = t.matmul(ps[:, :],
                                   wpool[:, w_off + dm * 2048 + u * 128: w_off + dm * 2048 + (u + 1) * 128],
                                   x_fm[:, dm * xstride:dm * xstride + 512],
                                   start=(dm == 0), stop=(dm == ndm - 1))
                ins.then_inc(ctx.h["pe"], 1)

        @blk.scalar
        def _(s):
            for u in range(nout):
                ps = psA if u % 2 == 0 else psB
                s.wait_ge(ctx.h["pe"], pe0 + u + 1)
                if act_func is not None:
                    s.activation(out_sb[:, u * 512:(u + 1) * 512], ps[:, :],
                                 act_func,
                                 bias=(bias_sb[:, u:u + 1] if bias_sb is not None else 0.0),
                                 scale=1.0)
                else:
                    s.copy(out_sb[:, u * 512:(u + 1) * 512], ps[:, :])
                s.drain().then_inc(ctx.h["act"], 1)
    ctx.n["pe"] = pe0 + nout
    ctx.n["act"] = act0 + nout


def st_grp(st0, g):
    tot = st0
    for i in range(4 * (g + 1)):
        tot += 16 if i < 16 else 64
    return tot




def emit_tm2fm_gen(ctx, nc, src_tm, dst_fn, ident, ps_tp, ps_tp2, ntt=4):
    pairs, evacs = [], []
    for dt in range(16):
        ps = ps_tp if dt % 2 == 0 else ps_tp2
        grp = [(ps[:, tt * 128:(tt + 1) * 128],
                src_tm[:, tt * 2048 + dt * 128: tt * 2048 + (dt + 1) * 128])
               for tt in range(ntt)]
        pairs.append(grp)
        evacs.append((dst_fn(dt), ps[:, 0:ntt * 128]))
    emit_transpose(ctx, nc, pairs, ident, evacs)


def emit_xln_phase(ctx, nc, resid_dram, row0, nrows, xfm_dst_fn, rbuf, tbuf,
                   stats, aggr, rstd, eps_col, ident, ps_tp, ps_tp2, ntt=4):
    for g in range(nrows // (128 * ntt)):
        ds0 = ctx.n["ds"] + ntt * 16
        with nc.Block() as blk:
            @blk.sync
            def _(sy, g=g):
                for tt in range(ntt):
                    r = row0 + g * 128 * ntt + tt * 128
                    sy.dma_start(rbuf[:, tt * 2048:(tt + 1) * 2048],
                                 resid_dram[r:r + 128, :]).then_inc(ctx.h["ds"], 16)
        ctx.n["ds"] = ds0
        emit_ln_block(ctx, nc, rbuf, ntt, stats, aggr, rstd, eps_col,
                      out_ap=tbuf, dg_wait=("ds", ds0))
        emit_tm2fm_gen(ctx, nc, tbuf, lambda dt, g=g: xfm_dst_fn(g, dt),
                       ident, ps_tp, ps_tp2, ntt=ntt)


def emit_epilogue_chunk(ctx, nc, yfm, resid_in, resid_out, row0,
                        xraw, ytm, lny, stats, aggr, rstd, eps_col, ctmp,
                        ps_tp, ps_tp2, ident):
    for tt in range(4):
        pairs, evacs = [], []
        for g in range(4):
            ps = ps_tp if g % 2 == 0 else ps_tp2
            grp = []
            for k4 in range(4):
                dt = g * 4 + k4
                grp.append((ps[:, k4 * 128:(k4 + 1) * 128],
                            yfm[:, dt * 512 + tt * 128: dt * 512 + (tt + 1) * 128]))
            pairs.append(grp)
            evacs.append((ytm[:, g * 512:(g + 1) * 512], ps[:, :]))
        emit_transpose(ctx, nc, pairs, ident, evacs)
        ds_xr = ctx.n["ds"] + 16
        dve0e, act0e = ctx.n["dve"], ctx.n["act"]
        with nc.Block() as blk:
            @blk.vector
            def _(v, tt=tt, dve0e=dve0e, act0e=act0e, ds_xr=ds_xr):
                ln_vec_emit(ctx, v, ytm[:, 0:2048], 1, stats, aggr, rstd,
                            dve0e, act0e, out_ap=lny[:, 0:2048], scratch32=ctmp)
                v.wait_ge(ctx.h["ds"], ds_xr)
                v.tensor_add(xraw[:, :], xraw[:, :], lny[:, 0:2048])
                v.drain().then_inc(ctx.h["dve"], 1)

            @blk.scalar
            def _(s, dve0e=dve0e, act0e=act0e):
                ln_scl_emit(ctx, s, 1, aggr, rstd, eps_col, dve0e, act0e)

            @blk.sync
            def _(sy, tt=tt, dve0e=dve0e):
                r = row0 + tt * 128
                sy.dma_start(xraw[:, :], resid_in[r:r + 128, :]).then_inc(ctx.h["ds"], 16)
                sy.wait_ge(ctx.h["dve"], dve0e + 3)
                sy.dma_start(resid_out[r:r + 128, :], xraw[:, :]).then_inc(ctx.h["st"], 16)
                sy.wait_ge(ctx.h["st"], ctx.n["st"] + 16)
        ctx.n["ds"] = ds_xr
        ctx.n["dve"] = dve0e + 3
        ctx.n["act"] = act0e + 1
        ctx.n["st"] += 16


def host_swa_masks():
    out = np.zeros((128, 8 * 512), dtype=np.float32)
    k = np.arange(128)[:, None]
    q = np.arange(512)[None, :]
    for kt in range(8):
        dlt = -512 + 128 * kt
        m = ((k + dlt <= q) & (k + dlt >= q - 512))
        out[:, kt * 512:(kt + 1) * 512] = m
    return out.astype(BF)


def host_b2_bias(c):
    bias = np.full((128, 16), EXP_SHIFT, dtype=np.float32)
    blkA, blkB = core_blocks(c)
    for run, blk in enumerate([blkA, blkB]):
        for pos in range(2):
            gchunk = 2 * blk + pos
            if gchunk == 0:
                for kp in range(2):
                    bias[:, (run * 2 + pos) * 4 + kp] = NEG_BIG
    return bias



def _io(io, key, mk):
    return io[key] if io is not None and key in io else mk()


def build_b0(nc=None, ctx=None, pfx="", io=None):
    own = nc is None
    if own:
        nc = bass.Bass("TRN2", target_bir_lowering=False)
        ctx = Ctx(nc)

    xkv = _io(io, "x_full", lambda: nc.dram_tensor("xkv", [S, D], F32, kind="ExternalInput"))
    xown = _io(io, "x_own", lambda: nc.dram_tensor("xown", [T, D], F32, kind="ExternalInput"))
    wq_i = nc.dram_tensor("wq", [D, D], F32, kind="ExternalInput")
    wk_i = nc.dram_tensor("wk", [D, D], F32, kind="ExternalInput")
    wv_i = nc.dram_tensor("wv", [D, D], F32, kind="ExternalInput")
    wo_i = nc.dram_tensor("wo", [D, D], F32, kind="ExternalInput")
    bo_i = nc.dram_tensor("bo_t", [128, 16], F32, kind="ExternalInput")
    ident_i = _io(io, "ident", lambda: nc.dram_tensor("ident", [128, 128], BF16, kind="ExternalInput"))
    masks_i = nc.dram_tensor("masks", [128, 2 * 1024], BF16, kind="ExternalInput")
    bias_i = nc.dram_tensor("bias_tbl", [128, NJOB2], F32, kind="ExternalInput")
    kdiag_i = nc.dram_tensor("kdiag", [128, 4], I32, kind="ExternalInput")
    scl_i = _io(io, "scl", lambda: nc.dram_tensor("scl", [128, 8], F32, kind="ExternalInput"))
    ones_i = _io(io, "ones32", lambda: nc.dram_tensor("ones32", [128, 1], F32, kind="ExternalInput"))
    onesr_i = _io(io, "onesr32", lambda: nc.dram_tensor("onesr32", [1, 128], F32, kind="ExternalInput"))

    kspill = nc.dram_tensor(pfx + "kspill", [16 * 128, 64 * 128], BF16)
    vspill = nc.dram_tensor(pfx + "vspill", [16 * 128, 64 * 128], BF16)
    resid1 = _io(io, "resid_out", lambda: nc.dram_tensor("resid1", [T, D], F32, kind="ExternalOutput"))
    if DEBUG_B0:
        dbg_xfm = nc.dram_tensor("dbg_xfm", [128, 16 * 512], F32, kind="ExternalOutput")
        dbg_q = nc.dram_tensor("dbg_q", [128, 16 * 512], F32, kind="ExternalOutput")
        dbg_ctx = nc.dram_tensor("dbg_ctx", [128, 16 * 512], F32, kind="ExternalOutput")
        dbg_y = nc.dram_tensor("dbg_y", [128, 16 * 512], F32, kind="ExternalOutput")
        dbg_kbuf = nc.dram_tensor("dbg_kbuf", [128, 8192], F32, kind="ExternalOutput")
        dbg_xtm = nc.dram_tensor("dbg_xtm", [128, 8192], F32, kind="ExternalOutput")
        dbg_st = nc.dram_tensor("dbg_st", [128, 112], F32, kind="ExternalOutput")
        dbg_xfm2 = nc.dram_tensor("dbg_xfm2", [128, 2048], F32, kind="ExternalOutput")
        dbg_xfm3 = nc.dram_tensor("dbg_xfm3", [128, 2048], F32, kind="ExternalOutput")
        dbg_vbuf = nc.dram_tensor("dbg_vbuf", [128, 8192], F32, kind="ExternalOutput")

    stack = contextlib.ExitStack()

    def sb(name, shape, dt):
        return stack.enter_context(nc.sbuf_tensor(pfx + name, shape, dt))

    c_ident = sb("c_ident", [128, 128], BF16)
    c_masks = sb("c_masks", [128, 2 * 1024], BF16)
    c_bias = sb("c_bias", [128, NJOB2], F32)
    c_kdiag = sb("c_kdiag", [128, 4], I32)
    c_scl = sb("c_scl", [128, 8], F32)
    c_ones = sb("c_ones", [128, 1], F32)
    c_onesr = sb("c_onesr", [1, 128], F32)
    c_bo = sb("c_bo", [128, 16], F32)
    wpool = sb("wpool", [128, 32 * 2048], BF16)
    xtm = sb("xtm", [128, 8192], BF16)
    xfm = sb("xfm", [128, 16 * 512], BF16)
    qctx = sb("qctx", [128, 16 * 512], BF16)
    yfm = xfm
    ytm = xtm[:, 0:2048]
    lny = xtm[:, 2048:4096]
    xraw = sb("xraw", [128, 2048], F32)
    kvev = sb("kvev", [128, 2048], BF16)
    expp = kvev[:, 0:1024]
    exppm = kvev[:, 1024:2048]
    lpart = sb("lpart", [128, 512], F32)
    linv = lpart[0:1, :]
    ctmp = sb("ctmp", [128, 512], F32)
    stats = sb("stats", [128, 96], F32)
    aggr = sb("aggr", [128, 8], F32)
    rstd = sb("rstd", [128, 4], F32)
    ps_a = stack.enter_context(nc.psum_tensor(pfx + "ps_a", [128, 512], F32))
    ps_b = stack.enter_context(nc.psum_tensor(pfx + "ps_b", [128, 512], F32))
    if True:
        WQO = 0
        KBUF = [16 * 2048, 16 * 2048 + 8192]
        VBUF = [16 * 2048 + 2 * 8192, 16 * 2048 + 3 * 8192]
        eps_col = c_scl[:, 1:2]

        with nc.Block() as blk:
            @blk.sync
            def _(sy):
                for dst, src in [(c_ident, ident_i), (c_masks, masks_i),
                                 (c_bias, bias_i), (c_kdiag, kdiag_i),
                                 (c_scl, scl_i), (c_ones, ones_i),
                                 (c_onesr, onesr_i), (c_bo, bo_i)]:
                    sy.dma_start(dst[:, :], src.ap()).then_inc(ctx.h["ds"], 16)
                ctx.n["ds"] += 8 * 16
                sy.wait_ge(ctx.h["ds"], ctx.n["ds"])

            @blk.gpsimd
            def _(g):
                for t in range(16):
                    g.dma_start(wpool[:, t * 2048:(t + 1) * 2048],
                                wk_i[128 * t:128 * (t + 1), :]).then_inc(ctx.h["dg"], 16)
                    g.dma_start(wpool[:, (16 + t) * 2048:(17 + t) * 2048],
                                wv_i[128 * t:128 * (t + 1), :]).then_inc(ctx.h["dg"], 16)
                for tt in range(4):
                    g.dma_start(xtm[:, tt * 2048:(tt + 1) * 2048],
                                xkv[tt * 128:(tt + 1) * 128, :]).then_inc(ctx.h["dg"], 16)
                ctx.n["dg"] += 36 * 16
                g.wait_ge(ctx.h["dg"], ctx.n["dg"])


        tp_stack = contextlib.ExitStack()
        ps_tp = tp_stack.enter_context(nc.psum_tensor(pfx + "ps_tp_0", [128, 512], F32))
        ps_tp2 = tp_stack.enter_context(nc.psum_tensor(pfx + "ps_tp2_0", [128, 512], F32))
        if True:
            for sc in range(16):
                xb = xtm[:, 0:8192]

                def loadx(g, sc=sc):
                    if sc > 0:
                        for tt in range(4):
                            g.dma_start(xtm[:, tt * 2048:(tt + 1) * 2048],
                                        xkv[sc * 512 + tt * 128: sc * 512 + (tt + 1) * 128, :]
                                        ).then_inc(ctx.h["dg"], 16)

                dgw = ctx.n["dg"] + (4 * 16 if sc > 0 else 0)
                ctx.n["dg"] = dgw
                emit_ln_block(ctx, nc, xb, 4, stats, aggr, rstd, eps_col,
                              dg_wait=("dg", dgw),
                              extra_gp=loadx, scratch32=ctmp)
                emit_tm2fm(ctx, nc, xb, xfm, c_ident[:, :], ps_tp, ps_tp2)

                pe0, act0, st0 = ctx.n["pe"], ctx.n["act"], ctx.n["st"]


                with nc.Block() as blk:
                    @blk.tensor
                    def _(t, sc=sc, pe0=pe0, act0=act0):
                        for i in range(32):
                            ps = ps_a if i % 2 == 0 else ps_b
                            if i >= 2:
                                t.wait_ge(ctx.h["act"], act0 + i - 1)
                            ins = None
                            if i < 16:
                                dk = i
                                for dm in range(16):
                                    ins = t.matmul(ps[:, :],
                                                   wpool[:, dm * 2048 + dk * 128: dm * 2048 + (dk + 1) * 128],
                                                   xfm[:, dm * 512:(dm + 1) * 512],
                                                   start=(dm == 0), stop=(dm == 15))
                            else:
                                tt, dvc = (i - 16) // 4, (i - 16) % 4
                                for dm in range(16):
                                    ins = t.matmul(ps[:, :],
                                                   xfm[:, dm * 512 + tt * 128: dm * 512 + (tt + 1) * 128],
                                                   wpool[:, (16 + dm) * 2048 + dvc * 512: (16 + dm) * 2048 + (dvc + 1) * 512],
                                                   start=(dm == 0), stop=(dm == 15))
                            ins.then_inc(ctx.h["pe"], 1)

                    @blk.scalar
                    def _(s, sc=sc, pe0=pe0, act0=act0, st0=st0):
                        for i in range(32):
                            ps = ps_a if i % 2 == 0 else ps_b
                            s.wait_ge(ctx.h["pe"], pe0 + i + 1)
                            if i >= 4:
                                s.wait_ge(ctx.h["st"], st_grp(st0, (i - 4) // 4))
                            slot = kvev[:, (i % 4) * 512:(i % 4 + 1) * 512]
                            s.copy(slot, ps[:, :])
                            s.drain().then_inc(ctx.h["act"], 1)

                    @blk.sync
                    def _(sy, sc=sc, act0=act0, st0=st0):
                        for g in range(8):
                            sy.wait_ge(ctx.h["act"], act0 + 4 * (g + 1))
                            for i in range(4 * g, 4 * g + 4):
                                slot = kvev[:, (i % 4) * 512:(i % 4 + 1) * 512]
                                if i < 16:
                                    dkk = i
                                    dst = bass.AP(kspill, dkk * 128 * 8192 + 4 * sc * 128,
                                                  [[8192, 128], [1, 512]])
                                    sy.dma_start(dst, slot[:, :]).then_inc(ctx.h["st"], 16)
                                else:
                                    tt, dvc = (i - 16) // 4, (i - 16) % 4
                                    for u in range(4):
                                        hh = 4 * dvc + u
                                        dst = bass.AP(vspill,
                                                      hh * 128 * 8192 + (4 * sc + tt) * 128,
                                                      [[8192, 128], [1, 128]])
                                        sy.dma_start(dst, slot[:, u * 128:(u + 1) * 128]
                                                     ).then_inc(ctx.h["st"], 16)
                        sy.wait_ge(ctx.h["st"], st_grp(st0, 7))
                ctx.n["pe"] = pe0 + 32
                ctx.n["act"] = act0 + 32
                ctx.n["st"] = st_grp(st0, 7)

            for qt in range(4):
                ktm, kpn = KTMAX[qt], KT2[qt]

                dg_wq = ctx.n["dg"] + 20 * 16
                with nc.Block() as blk:
                    @blk.gpsimd
                    def _(g, qt=qt):
                        for t in range(16):
                            g.dma_start(wpool[:, WQO + t * 2048: WQO + (t + 1) * 2048],
                                        wq_i[128 * t:128 * (t + 1), :]).then_inc(ctx.h["dg"], 16)
                        for tt in range(4):
                            g.dma_start(xtm[:, tt * 2048:(tt + 1) * 2048],
                                        xown[qt * 512 + tt * 128: qt * 512 + (tt + 1) * 128, :]
                                        ).then_inc(ctx.h["dg"], 16)
                ctx.n["dg"] = dg_wq

                emit_ln_block(ctx, nc, xtm[:, 0:8192], 4, stats, aggr, rstd,
                              eps_col, dg_wait=("dg", dg_wq), scratch32=ctmp)
                if DEBUG_B0 and qt == 0:
                    with nc.Block() as blk:
                        @blk.vector
                        def _(v):
                            v.tensor_copy(xraw[:, 0:8], aggr[:, :])
                            v.tensor_copy(xraw[:, 8:12], rstd[:, :])
                            v.tensor_copy(xraw[:, 16:112], stats[:, :])
                        @blk.scalar
                        def _(s2):
                            pass
                    with nc.Block() as blk:
                        @blk.sync
                        def _(sy):
                            sy.dma_start(dbg_st.ap(), xraw[:, 0:112]).then_inc(ctx.h["st"], 16)
                            ctx.n["st"] += 16
                            sy.wait_ge(ctx.h["st"], ctx.n["st"])
                    for pp in range(4):
                        with nc.Block() as blk:
                            @blk.vector
                            def _(v, pp=pp):
                                v.tensor_copy(xraw[:, :], xtm[:, pp * 2048:(pp + 1) * 2048])
                            @blk.scalar
                            def _(s2):
                                pass
                        with nc.Block() as blk:
                            @blk.sync
                            def _(sy, pp=pp):
                                sy.dma_start(dbg_xtm.ap()[:, pp * 2048:(pp + 1) * 2048],
                                             xraw[:, :]).then_inc(ctx.h["st"], 16)
                                ctx.n["st"] += 16
                                sy.wait_ge(ctx.h["st"], ctx.n["st"])
                emit_tm2fm(ctx, nc, xtm[:, 0:8192], xfm, c_ident[:, :], ps_tp, ps_tp2)

                if DEBUG_B0 and qt == 0:
                    with nc.Block() as blk:
                        @blk.vector
                        def _(v):
                            v.tensor_copy(xraw[:, :], xfm[:, 0:2048])
                        @blk.scalar
                        def _(s2):
                            pass
                    with nc.Block() as blk:
                        @blk.sync
                        def _(sy):
                            sy.dma_start(dbg_xfm2.ap(), xraw[:, :]).then_inc(ctx.h["st"], 16)
                            ctx.n["st"] += 16
                            sy.wait_ge(ctx.h["st"], ctx.n["st"])

                ds_h0 = ctx.n["ds"] + 32
                with nc.Block() as blk:
                    @blk.sync
                    def _(sy, ktm=ktm):
                        sy.dma_start(wpool[:, KBUF[0] + 512:KBUF[0] + 512 + (ktm - 4) * 128],
                                     bass.AP(kspill, 0, [[8192, 128], [1, (ktm - 4) * 128]])
                                     ).then_inc(ctx.h["ds"], 16)
                        sy.dma_start(wpool[:, VBUF[0] + 512:VBUF[0] + 512 + (ktm - 4) * 128],
                                     bass.AP(vspill, 0, [[8192, 128], [1, (ktm - 4) * 128]])
                                     ).then_inc(ctx.h["ds"], 16)

                    @blk.gpsimd
                    def _(g, qt=qt):
                        g.indirect_dma_start(
                            wpool[:, KBUF[0]:KBUF[0] + 512], None,
                            bass.AP(kspill, 0, [[128, 131072], [1, 128]]),
                            IndirectOffsetOnAxis(c_kdiag[:, qt:qt + 1], 0),
                        ).then_inc(ctx.h["dk"], 16)
                        g.indirect_dma_start(
                            wpool[:, VBUF[0]:VBUF[0] + 512], None,
                            bass.AP(vspill, 0, [[128, 131072], [1, 128]]),
                            IndirectOffsetOnAxis(c_kdiag[:, qt:qt + 1], 0),
                        ).then_inc(ctx.h["dk"], 16)
                ctx.n["ds"] = ds_h0
                ctx.n["dk"] += 32
                dk_h0 = ctx.n["dk"]

                if DEBUG_B0 and qt == 0:
                    with nc.Block() as blk:
                        @blk.vector
                        def _(v):
                            v.wait_ge(ctx.h["ds"], ds_h0)
                            v.wait_ge(ctx.h["dk"], dk_h0)
                            v.tensor_copy(xraw[:, :], xfm[:, 0:2048])
                        @blk.scalar
                        def _(s2):
                            pass
                    with nc.Block() as blk:
                        @blk.sync
                        def _(sy):
                            sy.dma_start(dbg_xfm3.ap(), xraw[:, :]).then_inc(ctx.h["st"], 16)
                            ctx.n["st"] += 16
                            sy.wait_ge(ctx.h["st"], ctx.n["st"])

                emit_proj(ctx, nc, wpool, WQO, xfm, qctx, 16, ps_a, ps_b,
                          pre_waits=[("dg", dg_wq)])

                if DEBUG_B0 and qt == 0:
                    with nc.Block() as blk:
                        @blk.vector
                        def _(v):
                            v.wait_ge(ctx.h["ds"], ds_h0)
                            v.wait_ge(ctx.h["dk"], dk_h0)
                            v.tensor_copy(xraw[:, :], xfm[:, 0:2048])
                        @blk.sync
                        def _(sy):
                            pass
                    with nc.Block() as blk:
                        @blk.sync
                        def _(sy):
                            sy.dma_start(dbg_xfm.ap()[:, 0:2048], xraw[:, :]).then_inc(ctx.h["st"], 16)
                            ctx.n["st"] += 16
                            sy.wait_ge(ctx.h["st"], ctx.n["st"])
                    for part in range(8):
                        with nc.Block() as blk:
                            @blk.vector
                            def _(v, part=part):
                                v.tensor_copy(xraw[:, 0:1024], qctx[:, part * 1024:(part + 1) * 1024])
                            @blk.scalar
                            def _(s2):
                                pass
                        with nc.Block() as blk:
                            @blk.sync
                            def _(sy, part=part):
                                sy.dma_start(dbg_q.ap()[:, part * 1024:(part + 1) * 1024],
                                             xraw[:, 0:1024]).then_inc(ctx.h["st"], 16)
                                ctx.n["st"] += 16
                                sy.wait_ge(ctx.h["st"], ctx.n["st"])
                    for part in range(4):
                        with nc.Block() as blk:
                            @blk.vector
                            def _(v, part=part):
                                v.tensor_copy(xraw[:, :], wpool[:, KBUF[0] + part * 2048: KBUF[0] + (part + 1) * 2048])
                            @blk.scalar
                            def _(s2):
                                pass
                        with nc.Block() as blk:
                            @blk.sync
                            def _(sy, part=part):
                                sy.dma_start(dbg_kbuf.ap()[:, part * 2048:(part + 1) * 2048],
                                             xraw[:, :]).then_inc(ctx.h["st"], 16)
                                ctx.n["st"] += 16
                                sy.wait_ge(ctx.h["st"], ctx.n["st"])
                    for part in range(4):
                        with nc.Block() as blk:
                            @blk.vector
                            def _(v, part=part):
                                v.tensor_copy(xraw[:, :], wpool[:, VBUF[0] + part * 2048: VBUF[0] + (part + 1) * 2048])
                            @blk.scalar
                            def _(s2):
                                pass
                        with nc.Block() as blk:
                            @blk.sync
                            def _(sy, part=part):
                                sy.dma_start(dbg_vbuf.ap()[:, part * 2048:(part + 1) * 2048],
                                             xraw[:, :]).then_inc(ctx.h["st"], 16)
                                ctx.n["st"] += 16
                                sy.wait_ge(ctx.h["st"], ctx.n["st"])

                dg_wo = ctx.n["dg"] + 16 * 16
                with nc.Block() as blk:
                    @blk.gpsimd
                    def _(g):
                        for t in range(16):
                            g.dma_start(wpool[:, WQO + t * 2048: WQO + (t + 1) * 2048],
                                        wo_i[128 * t:128 * (t + 1), :]).then_inc(ctx.h["dg"], 16)
                ctx.n["dg"] = dg_wo

                tp_stack.close()
                hd_stack = contextlib.ExitStack()
                ps_s0 = hd_stack.enter_context(nc.psum_tensor(pfx + "ps_s0_%d" % qt, [128, 1024], F32))
                ps_s1 = hd_stack.enter_context(nc.psum_tensor(pfx + "ps_s1_%d" % qt, [128, 1024], F32))
                ps_o = hd_stack.enter_context(nc.psum_tensor(pfx + "ps_o_%d" % qt, [128, 512], F32))
                ps_l = hd_stack.enter_context(nc.psum_tensor(pfx + "ps_l_%d" % qt, [1, 512], F32))
                if True:
                    for hh in range(16):
                        kb, vb = KBUF[hh % 2], VBUF[hh % 2]
                        pe0, act0, dve0 = ctx.n["pe"], ctx.n["act"], ctx.n["dve"]
                        ds_need = ds_h0 + 32 * hh
                        dk_need = dk_h0 + 32 * hh
                        with nc.Block() as blk:
                            @blk.sync
                            def _(sy, hh=hh, ktm=ktm, pe0=pe0):
                                if hh < 15:
                                    sy.wait_ge(ctx.h["pe"], pe0 + 1)
                                    nkb, nvb = KBUF[(hh + 1) % 2], VBUF[(hh + 1) % 2]
                                    sy.dma_start(wpool[:, nkb + 512:nkb + 512 + (ktm - 4) * 128],
                                                 bass.AP(kspill, (hh + 1) * 128 * 8192,
                                                         [[8192, 128], [1, (ktm - 4) * 128]])
                                                 ).then_inc(ctx.h["ds"], 16)
                                    sy.dma_start(wpool[:, nvb + 512:nvb + 512 + (ktm - 4) * 128],
                                                 bass.AP(vspill, (hh + 1) * 128 * 8192,
                                                         [[8192, 128], [1, (ktm - 4) * 128]])
                                                 ).then_inc(ctx.h["ds"], 16)

                            @blk.gpsimd
                            def _(g, qt=qt, hh=hh, pe0=pe0):
                                if hh < 15:
                                    nkb, nvb = KBUF[(hh + 1) % 2], VBUF[(hh + 1) % 2]
                                    g.wait_ge(ctx.h["pe"], pe0 + 1)
                                    g.indirect_dma_start(
                                        wpool[:, nkb:nkb + 512], None,
                                        bass.AP(kspill, 0, [[128, 131072], [1, 128]]),
                                        IndirectOffsetOnAxis(c_kdiag[:, qt:qt + 1], 0),
                                        element_offset=(hh + 1) * 1048576,
                                    ).then_inc(ctx.h["dk"], 16)
                                    g.indirect_dma_start(
                                        wpool[:, nvb:nvb + 512], None,
                                        bass.AP(vspill, 0, [[128, 131072], [1, 128]]),
                                        IndirectOffsetOnAxis(c_kdiag[:, qt:qt + 1], 0),
                                        element_offset=(hh + 1) * 1048576,
                                    ).then_inc(ctx.h["dk"], 16)

                            @blk.tensor
                            def _(t, hh=hh, kb=kb, vb=vb, kpn=kpn,
                                  pe0=pe0, act0=act0, dve0=dve0,
                                  ds_need=ds_need, dk_need=dk_need):
                                t.wait_ge(ctx.h["ds"], ds_need)
                                t.wait_ge(ctx.h["dk"], dk_need)
                                for kp in range(kpn):
                                    ps_s = ps_s0 if kp % 2 == 0 else ps_s1
                                    if kp >= 2:
                                        t.wait_ge(ctx.h["act"], act0 + kp - 1)
                                    ins = None
                                    for half in range(2):
                                        co = kp * 256 + half * 128
                                        ins = t.matmul(ps_s[:, half * 512:(half + 1) * 512],
                                                       wpool[:, kb + co: kb + co + 128],
                                                       qctx[:, hh * 512:(hh + 1) * 512],
                                                       start=True, stop=True)
                                    ins.then_inc(ctx.h["pe"], 1)
                                    t.wait_ge(ctx.h["dve"], dve0 + kp + 1)
                                    em = exppm if kp < 2 else expp
                                    for half in range(2):
                                        co = kp * 256 + half * 128
                                        ins = t.matmul(ps_o[:, :],
                                                       wpool[:, vb + co: vb + co + 128],
                                                       em[:, half * 512:(half + 1) * 512],
                                                       start=(kp == 0 and half == 0),
                                                       stop=(kp == kpn - 1 and half == 1))
                                    ins.then_inc(ctx.h["pe"], 1)
                                t.wait_ge(ctx.h["dve"], dve0 + kpn)
                                t.matmul(ps_l[:, :], c_ones[:, :], lpart[:, :],
                                         start=True, stop=True)
                                t.drain().then_inc(ctx.h["pe"], 1)
                                t.wait_ge(ctx.h["dve"], dve0 + kpn + 1)
                                t.matmul(ps_a[:, :], c_onesr[:, :], linv[:, :],
                                         start=True, stop=True)
                                t.drain().then_inc(ctx.h["pe"], 1)

                            @blk.scalar
                            def _(s, qt=qt, kpn=kpn, pe0=pe0, act0=act0, dve0=dve0):
                                col0 = job2_col(qt, 0)
                                for kp in range(kpn):
                                    ps_s = ps_s0 if kp % 2 == 0 else ps_s1
                                    s.wait_ge(ctx.h["pe"], pe0 + 2 * kp + 1)
                                    if kp >= 1:
                                        s.wait_ge(ctx.h["dve"], dve0 + kp)
                                    s.activation(expp[:, :], ps_s[:, :], AF.Exp,
                                                 bias=c_bias[:, col0 + kp:col0 + kp + 1],
                                                 scale=c_scl[:, 0:1])
                                    s.drain().then_inc(ctx.h["act"], 1)
                                s.wait_ge(ctx.h["pe"], pe0 + 2 * kpn)
                                s.copy(ctmp[:, :], ps_o[:, :])
                                s.drain().then_inc(ctx.h["act"], 1)

                            @blk.vector
                            def _(v, qt=qt, hh=hh, kpn=kpn, pe0=pe0, act0=act0, dve0=dve0):
                                col0 = job2_col(qt, 0)
                                v.memset(lpart[:, :], 0.0)
                                v.drain()

                                for kp in range(kpn):
                                    v.wait_ge(ctx.h["act"], act0 + kp + 1)
                                    if kp < 2:
                                        if kp >= 1:
                                            v.wait_ge(ctx.h["pe"], pe0 + 2 * kp)
                                        v.tensor_tensor(exppm[:, :], expp[:, :],
                                                        c_masks[:, kp * 1024:(kp + 1) * 1024],
                                                        ALU.mult)
                                        v.tensor_add(lpart[:, :], lpart[:, :], exppm[:, 0:512])
                                        v.tensor_add(lpart[:, :], lpart[:, :], exppm[:, 512:1024])
                                    else:
                                        v.tensor_add(lpart[:, :], lpart[:, :], expp[:, 0:512])
                                        v.tensor_add(lpart[:, :], lpart[:, :], expp[:, 512:1024])
                                    v.drain().then_inc(ctx.h["dve"], 1)
                                v.wait_ge(ctx.h["pe"], pe0 + 2 * kpn + 1)
                                v.reciprocal(linv[:, :], ps_l[:, :])
                                v.drain().then_inc(ctx.h["dve"], 1)
                                v.wait_ge(ctx.h["pe"], pe0 + 2 * kpn + 2)
                                v.wait_ge(ctx.h["act"], act0 + kpn + 1)
                                v.tensor_mul(qctx[:, hh * 512:(hh + 1) * 512],
                                             ctmp[:, :], ps_a[:, :])
                                v.drain().then_inc(ctx.h["dve"], 1)

                        ctx.n["pe"] = pe0 + 2 * kpn + 2
                        ctx.n["act"] = act0 + kpn + 1
                        ctx.n["dve"] = dve0 + kpn + 2
                        if hh < 15:
                            ctx.n["ds"] += 32
                            ctx.n["dk"] += 32

                hd_stack.close()
                tp_stack = contextlib.ExitStack()
                ps_tp = tp_stack.enter_context(nc.psum_tensor(pfx + "ps_tp_%d" % (qt + 1), [128, 512], F32))
                ps_tp2 = tp_stack.enter_context(nc.psum_tensor(pfx + "ps_tp2_%d" % (qt + 1), [128, 512], F32))
                emit_proj(ctx, nc, wpool, WQO, qctx, yfm, 16, ps_a, ps_b,
                          bias_sb=c_bo, act_func=AF.Identity,
                          pre_waits=[("dg", dg_wo)])

                if DEBUG_B0 and qt == 0:
                    for part in range(8):
                        with nc.Block() as blk:
                            @blk.vector
                            def _(v, part=part):
                                v.tensor_copy(xraw[:, 0:1024], qctx[:, part * 1024:(part + 1) * 1024])
                            @blk.scalar
                            def _(s2):
                                pass
                        with nc.Block() as blk:
                            @blk.sync
                            def _(sy, part=part):
                                sy.dma_start(dbg_ctx.ap()[:, part * 1024:(part + 1) * 1024],
                                             xraw[:, 0:1024]).then_inc(ctx.h["st"], 16)
                                ctx.n["st"] += 16
                                sy.wait_ge(ctx.h["st"], ctx.n["st"])
                    for part in range(8):
                        with nc.Block() as blk:
                            @blk.vector
                            def _(v, part=part):
                                v.tensor_copy(xraw[:, 0:1024], yfm[:, part * 1024:(part + 1) * 1024])
                            @blk.scalar
                            def _(s2):
                                pass
                        with nc.Block() as blk:
                            @blk.sync
                            def _(sy, part=part):
                                sy.dma_start(dbg_y.ap()[:, part * 1024:(part + 1) * 1024],
                                             xraw[:, 0:1024]).then_inc(ctx.h["st"], 16)
                                ctx.n["st"] += 16
                                sy.wait_ge(ctx.h["st"], ctx.n["st"])

                for tt in range(4):
                    pairs, evacs = [], []
                    for g in range(4):
                        ps = ps_tp if g % 2 == 0 else ps_tp2
                        grp = []
                        for k4 in range(4):
                            dt = g * 4 + k4
                            grp.append((ps[:, k4 * 128:(k4 + 1) * 128],
                                        yfm[:, dt * 512 + tt * 128: dt * 512 + (tt + 1) * 128]))
                        pairs.append(grp)
                        evacs.append((ytm[:, g * 512:(g + 1) * 512], ps[:, :]))
                    emit_transpose(ctx, nc, pairs, c_ident[:, :], evacs)
                    ds_xr = ctx.n["ds"] + 16

                    dve0e, act0e = ctx.n["dve"], ctx.n["act"]
                    with nc.Block() as blk:
                        @blk.vector
                        def _(v, qt=qt, tt=tt, dve0e=dve0e, act0e=act0e, ds_xr=ds_xr):
                            ln_vec_emit(ctx, v, ytm[:, 0:2048], 1, stats, aggr, rstd,
                                        dve0e, act0e, out_ap=lny[:, 0:2048],
                                        scratch32=ctmp)
                            v.wait_ge(ctx.h["ds"], ds_xr)
                            v.tensor_add(xraw[:, :], xraw[:, :], lny[:, 0:2048])
                            v.drain().then_inc(ctx.h["dve"], 1)

                        @blk.scalar
                        def _(s, dve0e=dve0e, act0e=act0e):
                            ln_scl_emit(ctx, s, 1, aggr, rstd, eps_col, dve0e, act0e)

                        @blk.sync
                        def _(sy, qt=qt, tt=tt, dve0e=dve0e):
                            sy.dma_start(xraw[:, :],
                                         xown[(qt * 4 + tt) * 128:(qt * 4 + tt + 1) * 128, :]
                                         ).then_inc(ctx.h["ds"], 16)
                            sy.wait_ge(ctx.h["dve"], dve0e + 3)
                            sy.dma_start(resid1.ap()[(qt * 4 + tt) * 128:(qt * 4 + tt + 1) * 128, :],
                                         xraw[:, :]).then_inc(ctx.h["st"], 16)
                            sy.wait_ge(ctx.h["st"], ctx.n["st"] + 16)
                    ctx.n["ds"] = ds_xr
                    ctx.n["dve"] = dve0e + 3
                    ctx.n["act"] = act0e + 1
                    ctx.n["st"] += 16

        tp_stack.close()
        with nc.Block() as blk:
            @blk.sync
            def _(sy):
                sy.wait_ge(ctx.h["st"], ctx.n["st"])

    stack.close()
    if own:
        ctx.close()
    return nc



def _prep_b0_inmaps(inputs):
    x = np.asarray(inputs["L0_attn_x"], dtype=np.float32)
    wq = np.ascontiguousarray(np.asarray(inputs["L0_attn_wq"], dtype=np.float32))
    wk = np.ascontiguousarray(np.asarray(inputs["L0_attn_wk"], dtype=np.float32))
    wv = np.ascontiguousarray(np.asarray(inputs["L0_attn_wv"], dtype=np.float32))
    wo = np.ascontiguousarray(np.asarray(inputs["L0_attn_wo"], dtype=np.float32))
    bo = np.asarray(inputs["L0_attn_bo"], dtype=np.float32)
    bo_t = np.ascontiguousarray(bo.reshape(16, 128).T).astype(np.float32)
    ident = np.eye(128, dtype=BF)
    masks = host_causal_masks_paired()
    scl = host_scl()
    ones32 = np.ones((128, 1), dtype=np.float32)
    onesr32 = np.ones((1, 128), dtype=np.float32)

    in_maps = []
    for c in range(NCORE):
        b = c // 4
        blkA, blkB = core_blocks(c)
        xb = np.ascontiguousarray(x[b])
        xo = np.ascontiguousarray(np.concatenate(
            [xb[1024 * blkA:1024 * blkA + 1024], xb[1024 * blkB:1024 * blkB + 1024]], axis=0))
        bias_tbl, kdiag = host_b0_tables(c)
        in_maps.append({
            "xkv": xb, "xown": xo,
            "wq": wq, "wk": wk, "wv": wv, "wo": wo, "bo_t": bo_t,
            "ident": ident, "masks": masks,
            "bias_tbl": bias_tbl, "kdiag": kdiag,
            "scl": scl, "ones32": ones32, "onesr32": onesr32,
        })
    return in_maps


def run_b0(inputs):
    nc = build_b0()
    in_maps = _prep_b0_inmaps(inputs)
    res = run_bass_kernel_spmd(nc, in_maps, core_ids=list(range(8)))
    if DEBUG_B0:
        return res.results
    return [np.asarray(r["resid1"]) for r in res.results]


def assemble(resids):
    out = np.zeros((B, S, D), dtype=np.float32)
    for c in range(NCORE):
        b = c // 4
        blkA, blkB = core_blocks(c)
        out[b, 1024 * blkA:1024 * blkA + 1024] = resids[c][0:1024]
        out[b, 1024 * blkB:1024 * blkB + 1024] = resids[c][1024:2048]
    return out


def run_b1(inputs, resids1):
    nc = build_b1()
    in_maps = _prep_b1_inmaps(inputs, resids1)
    res = run_bass_kernel_spmd(nc, in_maps, core_ids=list(range(8)))
    return [np.asarray(r["resid_o"]) for r in res.results]


N_LAUNCH = 1


def run_fused(inputs):
    nc = build_fused()
    in_maps = _prep_fused_inmaps(inputs)
    res = run_bass_kernel_spmd(nc, in_maps, core_ids=list(range(8)))
    return [np.asarray(r["resid_o"]) for r in res.results]


def kernel(**inputs):
    if N_LAUNCH == 1:
        return assemble(run_fused(inputs))
    resids = run_b0(inputs)
    if N_LAUNCH >= 2:
        resids = run_b1(inputs, resids)
    if N_LAUNCH >= 3:
        resids = run_b2(inputs, resids)
    if N_LAUNCH >= 4:
        resids = run_b3(inputs, resids)
    return assemble(resids)



B1CUT = 99
B1_ACT = AF.Gelu
B1_HN = 99
B1_EVAC = 1



def _b1_finish(nc, ctx, stack):
    with nc.Block() as blk:
        @blk.sync
        def _(sy):
            sy.wait_ge(ctx.h["st"], ctx.n["st"])
    stack.close()
    if own:
        ctx.close()
    return nc


def _cut(n):
    return B1CUT < n


def build_b1(nc=None, ctx=None, pfx="", io=None):
    own = nc is None
    if own:
        nc = bass.Bass("TRN2", target_bir_lowering=False)
        ctx = Ctx(nc)

    resid_i = _io(io, "resid", lambda: nc.dram_tensor("resid", [T, D], F32, kind="ExternalInput"))
    gw_i = nc.dram_tensor("gate_w", [6, D], F32, kind="ExternalInput")
    w1_i = nc.dram_tensor("w1", [6 * 2048, 1408], F32, kind="ExternalInput")
    b1c_i = nc.dram_tensor("b1c", [128, 66], F32, kind="ExternalInput")
    w2_i = nc.dram_tensor("w2", [6 * 1408, 2048], F32, kind="ExternalInput")
    b2c_i = nc.dram_tensor("b2c", [128, 96], F32, kind="ExternalInput")
    ident_i = _io(io, "ident", lambda: nc.dram_tensor("ident", [128, 128], BF16, kind="ExternalInput"))
    ident32_i = nc.dram_tensor("ident32", [128, 128], F32, kind="ExternalInput")
    scl_i = _io(io, "scl", lambda: nc.dram_tensor("scl", [128, 8], F32, kind="ExternalInput"))
    onesr_i = _io(io, "onesr32", lambda: nc.dram_tensor("onesr32", [1, 128], F32, kind="ExternalInput"))
    iota_i = nc.dram_tensor("iota", [128, 1], F32, kind="ExternalInput")
    bsel_i = nc.dram_tensor("bsel", [128, 2], F32, kind="ExternalInput")

    cc_in = nc.dram_tensor(pfx + "cc_in", [128, 32], F32)
    cc_out = nc.dram_tensor(pfx + "cc_out", [128, 32], F32)
    y0sp = nc.dram_tensor(pfx + "y0sp", [64 * 128, 512], BF16)

    resid_o = _io(io, "resid_out", lambda: nc.dram_tensor("resid_o", [T, D], F32, kind="ExternalOutput"))

    stack = contextlib.ExitStack()

    def sb(name, shape, dt):
        return stack.enter_context(nc.sbuf_tensor(pfx + name, shape, dt))

    c_ident = sb("c_ident", [128, 128], BF16)
    c_id32 = sb("c_id32", [128, 128], F32)
    c_scl = sb("c_scl", [128, 8], F32)
    c_onesr = sb("c_onesr", [1, 128], F32)
    c_iota = sb("c_iota", [128, 1], F32)
    c_bsel = sb("c_bsel", [128, 2], F32)
    xfm = sb("xfm", [128, 16 * 2048], BF16)
    wbuf = sb("wbuf", [128, 16 * 1408], BF16)
    hbuf = sb("hbuf", [128, 11 * 2048], BF16)
    rbuf = sb("rbuf", [128, 2 * 2048], F32)
    tbuf = sb("tbuf", [128, 2 * 2048], BF16)
    gbuf = sb("gbuf", [128, 2048], F32)
    yfm = sb("yfm", [128, 16 * 512], BF16)
    y0ld = xfm[:, 0:16 * 512]
    ytm = tbuf[:, 0:2048]
    lny = tbuf[:, 2048:4096]
    xraw = rbuf[:, 0:2048]
    ctmp = sb("ctmp", [128, 512], F32)
    stats = sb("stats", [128, 96], F32)
    aggr = sb("aggr", [128, 8], F32)
    rstd = sb("rstd", [128, 4], F32)
    xbar = sb("xbar", [128, 48], F32)
    gwT = sb("gwT", [128, 96], F32)
    gsm = sb("gsm", [1, 96], F32)
    b1all = sb("b1all", [128, 66], F32)
    b2all = sb("b2all", [128, 96], F32)
    indc = sb("indc", [128, 16], F32)
    gidx1 = sb("gidx1", [1, 8], mybir.dt.uint32)
    gidx2 = sb("gidx2", [1, 8], mybir.dt.uint32)
    idxt = sb("idxt", [128, 2], I32)
    brow = gbuf[0:1, 0:2048]
    bcol = sb("bcol", [128, 40], F32)
    ps_a = stack.enter_context(nc.psum_tensor(pfx + "ps_a", [128, 512], F32))
    ps_b = stack.enter_context(nc.psum_tensor(pfx + "ps_b", [128, 512], F32))
    ps_tp = stack.enter_context(nc.psum_tensor(pfx + "ps_tp", [128, 512], F32))
    ps_tp2 = stack.enter_context(nc.psum_tensor(pfx + "ps_tp2", [128, 512], F32))
    ps_s = stack.enter_context(nc.psum_tensor(pfx + "ps_s", [128, 16], F32))
    ps_g = stack.enter_context(nc.psum_tensor(pfx + "ps_g", [1, 16], F32))
    eps_col = c_scl[:, 1:2]

    with nc.Block() as blk:
        @blk.sync
        def _(sy):
            for dst, src in [(c_ident, ident_i), (c_id32, ident32_i),
                             (c_scl, scl_i), (c_onesr, onesr_i),
                             (c_iota, iota_i), (c_bsel, bsel_i),
                             (b1all, b1c_i), (b2all, b2c_i)]:
                sy.dma_start(dst[:, :], src.ap()).then_inc(ctx.h["ds"], 16)
            ctx.n["ds"] += 8 * 16
            sy.wait_ge(ctx.h["ds"], ctx.n["ds"])

    if _cut(1):
        return _b1_finish(nc, ctx, stack)
    emit_xln_phase(ctx, nc, resid_i, 0, T,
                   lambda g, dt: xfm[:, dt * 2048 + g * 256: dt * 2048 + (g + 1) * 256],
                   rbuf, tbuf, stats, aggr, rstd, eps_col, c_ident[:, :],
                   ps_tp, ps_tp2, ntt=2)

    if _cut(2):
        return _b1_finish(nc, ctx, stack)
    with nc.Block() as blk:
        @blk.vector
        def _(v):
            for dm in range(16):
                v.tensor_reduce(xbar[:, dm:dm + 1], xfm[:, dm * 2048:(dm + 1) * 2048],
                                mybir.AxisListType.X, ALU.add)
                v.drain()
            v.tensor_scalar(xbar[:, 32:48], xbar[:, 0:16], c_bsel[:, 1:2], None, ALU.mult)
            v.drain()
            v.tensor_scalar(xbar[:, 16:32], xbar[:, 0:16], c_bsel[:, 0:1], None, ALU.mult)
            v.drain().then_inc(ctx.h["dve"], 1)
            ctx.n["dve"] += 1

        @blk.scalar
        def _(s):
            pass

    if _cut(3):
        return _b1_finish(nc, ctx, stack)
    with nc.Block() as blk:
        @blk.gpsimd
        def _(g):
            g.wait_ge(ctx.h["dve"], ctx.n["dve"])
            g.dma_start(cc_in[:, :], xbar[:, 16:48]).then_inc(ctx.h["dg"], 16)
            ctx.n["dg"] += 16
            g.wait_ge(ctx.h["dg"], ctx.n["dg"])
            g.collective_compute(
                "AllReduce", ALU.add, replica_groups=[list(range(8))],
                ins=[cc_in.ap().opt()], outs=[cc_out.ap().opt()],
            ).then_inc(ctx.h["cc"], 1)
            ctx.n["cc"] += 1
            g.wait_ge(ctx.h["cc"], ctx.n["cc"])
            g.dma_start(xbar[:, 16:48], cc_out[:, :]).then_inc(ctx.h["dg"], 16)
            ctx.n["dg"] += 16
            g.wait_ge(ctx.h["dg"], ctx.n["dg"])

    if _cut(4):
        return _b1_finish(nc, ctx, stack)
    with nc.Block() as blk:
        @blk.vector
        def _(v):
            v.tensor_scalar(xbar[:, 0:16], xbar[:, 16:32], c_bsel[:, 0:1], None, ALU.mult)
            v.drain()
            v.scalar_tensor_tensor(xbar[:, 0:16], xbar[:, 32:48], c_bsel[:, 1:2],
                                   xbar[:, 0:16], ALU.mult, ALU.add)
            v.drain()
            v.tensor_scalar(xbar[:, 0:16], xbar[:, 0:16], c_scl[:, 2:3], None, ALU.mult)
            v.drain().then_inc(ctx.h["dve"], 1)
            ctx.n["dve"] += 1

        @blk.scalar
        def _(s):
            pass

    if _cut(5):
        return _b1_finish(nc, ctx, stack)
    with nc.Block() as blk:
        @blk.sync
        def _(sy):
            sy.dma_start(gbuf[0:6, 0:2048], gw_i[:, :]).then_inc(ctx.h["ds"], 16)
            ctx.n["ds"] += 16
            sy.wait_ge(ctx.h["ds"], ctx.n["ds"])

    pairs = []
    evacs = []
    for dm in range(16):
        ps = ps_tp if dm % 2 == 0 else ps_tp2
        pairs.append([(ps[:, 0:6], gbuf[0:6, dm * 128:(dm + 1) * 128])])
        evacs.append((gwT[:, dm * 6:(dm + 1) * 6], ps[:, 0:6]))
    emit_transpose(ctx, nc, pairs, c_id32[0:6, 0:6], evacs)

    pe0 = ctx.n["pe"]
    with nc.Block() as blk:
        @blk.tensor
        def _(t):
            t.wait_ge(ctx.h["dve"], ctx.n["dve"])
            for dm in range(16):
                t.matmul(ps_g[:, 0:6], xbar[:, dm:dm + 1], gwT[:, dm * 6:(dm + 1) * 6],
                         start=(dm == 0), stop=(dm == 15))
            t.drain().then_inc(ctx.h["pe"], 1)
    ctx.n["pe"] = pe0 + 1

    if _cut(7):
        return _b1_finish(nc, ctx, stack)
    dve0, act0 = ctx.n["dve"], ctx.n["act"]
    with nc.Block() as blk:
        @blk.vector
        def _(v, dve0=dve0, act0=act0):
            v.memset(gsm[:, 0:8], NEG_BIG)
            v.drain()
            v.wait_ge(ctx.h["pe"], ctx.n["pe"])
            v.tensor_copy(gsm[:, 0:6], ps_g[:, 0:6])
            v.drain()
            v.reduce_max(gsm[:, 8:9], gsm[:, 0:8], mybir.AxisListType.X)
            v.drain()
            v.tensor_scalar(gsm[:, 16:24], gsm[:, 0:8], gsm[:, 8:9], None, ALU.subtract)
            v.drain().then_inc(ctx.h["dve"], 1)
            v.wait_ge(ctx.h["act"], act0 + 1)
            v.reduce_sum(gsm[:, 9:10], gsm[:, 24:32], mybir.AxisListType.X)
            v.drain()
            v.reciprocal(gsm[:, 10:11], gsm[:, 9:10])
            v.drain()
            v.tensor_scalar(gsm[:, 24:32], gsm[:, 24:32], gsm[:, 10:11], None, ALU.mult)
            v.drain()
            v.reduce_max(gsm[:, 32:33], gsm[:, 24:32], mybir.AxisListType.X)
            v.drain()
            v.tensor_copy(gsm[:, 33:40], gsm[:, 32:33].broadcast_to((1, 7)))
            v.drain()
            v.max_index(gidx1[:, :], gsm[:, 32:40], gsm[:, 24:32])
            v.drain()
            v.match_replace(gsm[:, 40:48], gsm[:, 32:40], gsm[:, 24:32], -1e30)
            v.drain()
            v.reduce_max(gsm[:, 48:49], gsm[:, 40:48], mybir.AxisListType.X)
            v.drain()
            v.tensor_copy(gsm[:, 49:56], gsm[:, 48:49].broadcast_to((1, 7)))
            v.drain()
            v.max_index(gidx2[:, :], gsm[:, 48:56], gsm[:, 40:48])
            v.drain()
            v.tensor_scalar(gsm[:, 56:57], gsm[:, 32:33], gsm[:, 48:49], None, ALU.add)
            v.drain()
            v.reciprocal(gsm[:, 57:58], gsm[:, 56:57])
            v.drain()
            v.tensor_scalar(gsm[:, 58:59], gsm[:, 32:33], gsm[:, 57:58], None, ALU.mult)
            v.drain()
            v.tensor_scalar(gsm[:, 59:60], gsm[:, 48:49], gsm[:, 57:58], None, ALU.mult)
            v.drain()
            v.tensor_scalar(gsm[:, 64:72], gsm[:, 24:32], gsm[:, 32:33], None, ALU.is_ge)
            v.drain()
            v.tensor_scalar(gsm[:, 72:80], gsm[:, 40:48], gsm[:, 48:49], None, ALU.is_ge)
            v.drain().then_inc(ctx.h["dve"], 1)

        @blk.scalar
        def _(s, dve0=dve0, act0=act0):
            s.wait_ge(ctx.h["dve"], dve0 + 1)
            s.activation(gsm[:, 24:32], gsm[:, 16:24], AF.Exp)
            s.drain().then_inc(ctx.h["act"], 1)
    ctx.n["dve"] = dve0 + 2
    ctx.n["act"] = act0 + 1


    if _cut(8):
        return _b1_finish(nc, ctx, stack)
    pe0 = ctx.n["pe"]
    act0 = ctx.n["act"]
    dve0 = ctx.n["dve"]
    with nc.Block() as blk:
        @blk.tensor
        def _(t, pe0=pe0):
            for e in range(6):
                t.matmul(ps_s[:, e:e + 1], c_onesr[:, :], gsm[:, 64 + e:65 + e],
                         start=True, stop=True)
                t.matmul(ps_s[:, 6 + e:7 + e], c_onesr[:, :], gsm[:, 72 + e:73 + e],
                         start=True, stop=True)
            t.matmul(ps_s[:, 12:13], c_onesr[:, :], gsm[:, 58:59], start=True, stop=True)
            t.matmul(ps_s[:, 13:14], c_onesr[:, :], gsm[:, 59:60], start=True, stop=True)
            t.drain().then_inc(ctx.h["pe"], 1)

        @blk.scalar
        def _(s, pe0=pe0):
            s.wait_ge(ctx.h["pe"], pe0 + 1)
            s.copy(indc[:, 0:14], ps_s[:, 0:14])
            s.drain().then_inc(ctx.h["act"], 1)
    ctx.n["pe"] = pe0 + 1
    ctx.n["act"] = act0 + 1

    with nc.Block() as blk:
        @blk.vector
        def _(v, dve0=dve0):
            v.tensor_scalar(bcol[:, 34:40], indc[:, 0:6], indc[:, 12:13], None, ALU.mult)
            v.drain()
            v.scalar_tensor_tensor(bcol[:, 34:40], indc[:, 6:12], indc[:, 13:14],
                                   bcol[:, 34:40], ALU.mult, ALU.add)
            v.drain()
            v.memset(bcol[:, 16:32], 0.0)
            v.drain()
            for e in range(6):
                v.scalar_tensor_tensor(bcol[:, 16:32], b2all[:, e * 16:(e + 1) * 16],
                                       bcol[:, 34 + e:35 + e], bcol[:, 16:32],
                                       ALU.mult, ALU.add)
                v.drain()
            v.drain().then_inc(ctx.h["dve"], 1)

        @blk.scalar
        def _(s):
            pass
    ctx.n["dve"] = dve0 + 1

    if _cut(10):
        return _b1_finish(nc, ctx, stack)
    for sidx, gidx in [(0, gidx1), (1, gidx2)]:
        pe0 = ctx.n["pe"]
        dve0 = ctx.n["dve"]
        with nc.Block() as blk:
            @blk.vector
            def _(v, sidx=sidx, gidx=gidx, pe0=pe0):
                v.tensor_copy(gsm[:, 60:61], gidx[:, 0:1])
                v.drain()
                v.tensor_scalar(gsm[:, 61:62], gsm[:, 60:61], 2048.0, None, ALU.mult)
                v.drain()
                v.tensor_scalar(gsm[:, 62:63], gsm[:, 60:61], 1408.0, None, ALU.mult)
                v.drain()
                v.drain().then_inc(ctx.h["dve"], 1)
                v.wait_ge(ctx.h["pe"], pe0 + 1)
                v.tensor_tensor(idxt[:, 0:1], ps_a[:, 0:1], c_iota[:, :], ALU.add)
                v.drain()
                v.tensor_tensor(idxt[:, 1:2], ps_b[:, 0:1], c_iota[:, :], ALU.add)
                v.drain().then_inc(ctx.h["dve"], 1)

            @blk.tensor
            def _(t, sidx=sidx, dve0=dve0):
                t.wait_ge(ctx.h["dve"], dve0 + 1)
                t.matmul(ps_a[:, 0:1], c_onesr[:, :], gsm[:, 61:62], start=True, stop=True)
                t.matmul(ps_b[:, 0:1], c_onesr[:, :], gsm[:, 62:63], start=True, stop=True)
                t.matmul(ps_s[:, 0:1], c_onesr[:, :], gsm[:, 58 + sidx:59 + sidx],
                         start=True, stop=True)
                t.drain().then_inc(ctx.h["pe"], 1)
        ctx.n["pe"] = pe0 + 1
        ctx.n["dve"] = dve0 + 2

        if _cut(11 + 10 * sidx):
            return _b1_finish(nc, ctx, stack)
        act0 = ctx.n["act"]
        dve0 = ctx.n["dve"]
        with nc.Block() as blk:
            @blk.scalar
            def _(s, sidx=sidx, act0=act0):
                s.copy(bcol[:, 32 + sidx:33 + sidx], ps_s[:, 0:1])
                s.drain().then_inc(ctx.h["act"], 1)

            @blk.vector
            def _(v, sidx=sidx, dve0=dve0):
                v.memset(bcol[:, 0:11], 0.0)
                v.drain()
                for e in range(6):
                    icol = indc[:, e:e + 1] if sidx == 0 else indc[:, 6 + e:7 + e]
                    v.scalar_tensor_tensor(bcol[:, 0:11], b1all[:, e * 11:(e + 1) * 11],
                                           icol, bcol[:, 0:11], ALU.mult, ALU.add)
                    v.drain()
                v.drain().then_inc(ctx.h["dve"], 1)
        ctx.n["act"] = act0 + 1
        ctx.n["dve"] = dve0 + 1

        if _cut(12 + 10 * sidx):
            return _b1_finish(nc, ctx, stack)
        for dm in range(16):
            dk0 = ctx.n["dk"]
            dve0 = ctx.n["dve"]
            with nc.Block() as blk:
                @blk.gpsimd
                def _(g, dm=dm, dk0=dk0):
                    g.indirect_dma_start(
                        gbuf[:, 0:1408], None, w1_i.ap(),
                        IndirectOffsetOnAxis(idxt[:, 0:1], 0),
                        element_offset=dm * 128 * 1408,
                    ).then_inc(ctx.h["dk"], 16)

                @blk.vector
                def _(v, dm=dm, dk0=dk0):
                    v.wait_ge(ctx.h["dk"], dk0 + 16)
                    v.tensor_copy(wbuf[:, dm * 1408:(dm + 1) * 1408], gbuf[:, 0:1408])
                    v.drain().then_inc(ctx.h["dve"], 1)
            ctx.n["dk"] = dk0 + 16
            ctx.n["dve"] = dve0 + 1

        if _cut(13 + 10 * sidx):
            return _b1_finish(nc, ctx, stack)
        pe0 = ctx.n["pe"]
        act0 = ctx.n["act"]
        with nc.Block() as blk:
            @blk.tensor
            def _(t, pe0=pe0, act0=act0):
                i = 0
                for ch in range(4):
                    for ft in range(11):
                        if i >= B1_HN:
                            break
                        ps = ps_a if i % 2 == 0 else ps_b
                        if i >= 2:
                            t.wait_ge(ctx.h["act"], act0 + i - 1)
                        for dm in range(16):
                            t.matmul(ps[:, :],
                                     wbuf[:, dm * 1408 + ft * 128: dm * 1408 + (ft + 1) * 128],
                                     xfm[:, dm * 2048 + ch * 512: dm * 2048 + (ch + 1) * 512],
                                     start=(dm == 0), stop=(dm == 15))
                        t.drain().then_inc(ctx.h["pe"], 1)
                        i += 1

            @blk.scalar
            def _(s, pe0=pe0, act0=act0):
                i = 0
                for ch in range(4):
                    for ft in range(11):
                        if i >= B1_HN:
                            break
                        ps = ps_a if i % 2 == 0 else ps_b
                        s.wait_ge(ctx.h["pe"], pe0 + i + 1)
                        if B1_EVAC == 1:
                            s.activation(hbuf[:, ft * 2048 + ch * 512: ft * 2048 + (ch + 1) * 512],
                                         ps[:, :], B1_ACT, bias=bcol[:, ft:ft + 1], scale=1.0)
                        elif B1_EVAC == 2:
                            s.copy(hbuf[:, ft * 2048 + ch * 512: ft * 2048 + (ch + 1) * 512],
                                   ps[:, :])
                        s.drain().then_inc(ctx.h["act"], 1)
                        i += 1
        nh = min(44, B1_HN)
        ctx.n["pe"] = pe0 + nh
        ctx.n["act"] = act0 + nh

        if _cut(14 + 10 * sidx):
            return _b1_finish(nc, ctx, stack)
        for ft in range(11):
            dk0 = ctx.n["dk"]
            dve0 = ctx.n["dve"]
            with nc.Block() as blk:
                @blk.gpsimd
                def _(g, ft=ft, dk0=dk0):
                    g.indirect_dma_start(
                        gbuf[:, 0:2048], None, w2_i.ap(),
                        IndirectOffsetOnAxis(idxt[:, 1:2], 0),
                        element_offset=ft * 128 * 2048,
                    ).then_inc(ctx.h["dk"], 16)

                @blk.vector
                def _(v, ft=ft, dk0=dk0):
                    v.wait_ge(ctx.h["dk"], dk0 + 16)
                    v.tensor_scalar(wbuf[:, ft * 2048:(ft + 1) * 2048], gbuf[:, 0:2048],
                                    bcol[:, 32 + sidx:33 + sidx], None, ALU.mult)
                    v.drain().then_inc(ctx.h["dve"], 1)
            ctx.n["dk"] = dk0 + 16
            ctx.n["dve"] = dve0 + 1

        if _cut(15 + 10 * sidx):
            return _b1_finish(nc, ctx, stack)
        if sidx == 0:
            for ch in range(4):
                pe0 = ctx.n["pe"]
                act0 = ctx.n["act"]
                st0 = ctx.n["st"]
                with nc.Block() as blk:
                    @blk.tensor
                    def _(t, ch=ch, pe0=pe0, act0=act0):
                        for u in range(16):
                            ps = ps_a if u % 2 == 0 else ps_b
                            if u >= 2:
                                t.wait_ge(ctx.h["act"], act0 + u - 1)
                            for ft in range(11):
                                t.matmul(ps[:, :],
                                         wbuf[:, ft * 2048 + u * 128: ft * 2048 + (u + 1) * 128],
                                         hbuf[:, ft * 2048 + ch * 512: ft * 2048 + (ch + 1) * 512],
                                         start=(ft == 0), stop=(ft == 10))
                            t.drain().then_inc(ctx.h["pe"], 1)

                    @blk.scalar
                    def _(s, ch=ch, pe0=pe0, act0=act0):
                        for u in range(16):
                            ps = ps_a if u % 2 == 0 else ps_b
                            s.wait_ge(ctx.h["pe"], pe0 + u + 1)
                            s.copy(yfm[:, u * 512:(u + 1) * 512], ps[:, :])
                            s.drain().then_inc(ctx.h["act"], 1)

                    @blk.sync
                    def _(sy, ch=ch, act0=act0, st0=st0):
                        sy.wait_ge(ctx.h["act"], act0 + 16)
                        dst = bass.AP(y0sp, ch * 16 * 128 * 512,
                                      [[512, 128], [128 * 512, 16], [1, 512]])
                        sy.dma_start(dst, yfm[:, :]).then_inc(ctx.h["st"], 16)
                        sy.wait_ge(ctx.h["st"], st0 + 16)
                ctx.n["pe"] = pe0 + 16
                ctx.n["act"] = act0 + 16
                ctx.n["st"] = st0 + 16
        else:
            for ch in range(4):
                ds0 = ctx.n["ds"]
                pe0 = ctx.n["pe"]
                dve0 = ctx.n["dve"]
                with nc.Block() as blk:
                    @blk.sync
                    def _(sy, ch=ch, ds0=ds0):
                        src = bass.AP(y0sp, ch * 16 * 128 * 512,
                                      [[512, 128], [128 * 512, 16], [1, 512]])
                        sy.dma_start(y0ld[:, :], src).then_inc(ctx.h["ds"], 16)

                    @blk.tensor
                    def _(t, ch=ch, pe0=pe0, dve0=dve0):
                        for u in range(16):
                            ps = ps_a if u % 2 == 0 else ps_b
                            if u >= 2:
                                t.wait_ge(ctx.h["dve"], dve0 + u - 1)
                            for ft in range(11):
                                t.matmul(ps[:, :],
                                         wbuf[:, ft * 2048 + u * 128: ft * 2048 + (u + 1) * 128],
                                         hbuf[:, ft * 2048 + ch * 512: ft * 2048 + (ch + 1) * 512],
                                         start=(ft == 0), stop=(ft == 10))
                            t.drain().then_inc(ctx.h["pe"], 1)

                    @blk.vector
                    def _(v, ch=ch, pe0=pe0, dve0=dve0, ds0=ds0):
                        v.wait_ge(ctx.h["ds"], ds0 + 16)
                        for u in range(16):
                            ps = ps_a if u % 2 == 0 else ps_b
                            v.wait_ge(ctx.h["pe"], pe0 + u + 1)
                            v.scalar_tensor_tensor(ctmp[:, :], ps[:, :], 1.0,
                                                   y0ld[:, u * 512:(u + 1) * 512],
                                                   ALU.mult, ALU.add)
                            v.drain()
                            v.tensor_scalar(yfm[:, u * 512:(u + 1) * 512], ctmp[:, :],
                                            bcol[:, 16 + u:17 + u], None, ALU.add)
                            v.drain().then_inc(ctx.h["dve"], 1)

                    @blk.scalar
                    def _(s):
                        pass
                ctx.n["ds"] = ds0 + 16
                ctx.n["pe"] = pe0 + 16
                ctx.n["dve"] = dve0 + 16

                emit_epilogue_chunk(ctx, nc, yfm, resid_i, resid_o, ch * 512,
                                    xraw, ytm, lny, stats, aggr, rstd, eps_col,
                                    ctmp, ps_tp, ps_tp2, c_ident[:, :])

    with nc.Block() as blk:
        @blk.sync
        def _(sy):
            sy.wait_ge(ctx.h["st"], ctx.n["st"])

    stack.close()
    if own:
        ctx.close()
    return nc


def _prep_b1_inmaps(inputs, resids1):
    gw = np.ascontiguousarray(np.asarray(inputs["L0_ffn_gate_W"], dtype=np.float32))
    w1 = np.ascontiguousarray(np.asarray(inputs["L0_ffn_W1"], dtype=np.float32).reshape(6 * 2048, 1408))
    b1 = np.asarray(inputs["L0_ffn_b1"], dtype=np.float32)
    b1c = np.ascontiguousarray(b1.reshape(6, 11, 128).transpose(2, 0, 1).reshape(128, 66))
    w2 = np.ascontiguousarray(np.asarray(inputs["L0_ffn_W2"], dtype=np.float32).reshape(6 * 1408, 2048))
    b2 = np.asarray(inputs["L0_ffn_b2"], dtype=np.float32)
    b2c = np.ascontiguousarray(b2.reshape(6, 16, 128).transpose(2, 0, 1).reshape(128, 96))
    ident = np.eye(128, dtype=BF)
    ident32 = np.eye(128, dtype=np.float32)
    scl = host_scl()
    onesr32 = np.ones((1, 128), dtype=np.float32)
    iota = np.arange(128, dtype=np.float32).reshape(128, 1)
    in_maps = []
    for c in range(NCORE):
        b = c // 4
        bsel = np.zeros((128, 2), dtype=np.float32)
        bsel[:, b] = 1.0
        m = {
            "gate_w": gw, "w1": w1, "b1c": b1c, "w2": w2, "b2c": b2c,
            "ident": ident, "ident32": ident32, "scl": scl,
            "onesr32": onesr32, "iota": iota, "bsel": bsel,
        }
        if resids1[c] is not None:
            m["resid"] = np.ascontiguousarray(resids1[c])
        in_maps.append(m)
    return in_maps



def build_b2(nc=None, ctx=None, pfx="", io=None):
    own = nc is None
    if own:
        nc = bass.Bass("TRN2", target_bir_lowering=False)
        ctx = Ctx(nc)

    resid_i = _io(io, "resid", lambda: nc.dram_tensor("resid", [T, D], F32, kind="ExternalInput"))
    gb_ap = io.get("gb") if io is not None else None
    bidx_i = io.get("bidx") if io is not None else None
    if gb_ap is None:
        xbnd_i = nc.dram_tensor("xbnd", [1024, D], F32, kind="ExternalInput")
    else:
        xbnd_i = None
    wqkv_i = nc.dram_tensor("w_qkv", [3 * D, D], F32, kind="ExternalInput")
    wout_i = nc.dram_tensor("w_out", [D, D], F32, kind="ExternalInput")
    ident_i = _io(io, "ident", lambda: nc.dram_tensor("ident", [128, 128], BF16, kind="ExternalInput"))
    masks_i = _io(io, "masks2", lambda: nc.dram_tensor("masks", [128, 8 * 512], BF16, kind="ExternalInput"))
    bias_i = nc.dram_tensor("bias16", [128, 16], F32, kind="ExternalInput")
    scl_i = _io(io, "scl", lambda: nc.dram_tensor("scl", [128, 8], F32, kind="ExternalInput"))
    ones_i = _io(io, "ones32", lambda: nc.dram_tensor("ones32", [128, 1], F32, kind="ExternalInput"))
    onesr_i = _io(io, "onesr32", lambda: nc.dram_tensor("onesr32", [1, 128], F32, kind="ExternalInput"))

    resid_o = _io(io, "resid_out", lambda: nc.dram_tensor("resid_o", [T, D], F32, kind="ExternalOutput"))

    stack = contextlib.ExitStack()

    def sb(name, shape, dt):
        return stack.enter_context(nc.sbuf_tensor(pfx + name, shape, dt))

    c_ident = sb("c_ident", [128, 128], BF16)
    c_masks = sb("c_masks", [128, 8 * 512], BF16)
    c_bias = sb("c_bias", [128, 16], F32)
    c_scl = sb("c_scl", [128, 8], F32)
    c_ones = sb("c_ones", [128, 1], F32)
    c_onesr = sb("c_onesr", [1, 128], F32)
    c_bidx = sb("c_bidx", [128, 8], I32)
    xkv = sb("xkv", [128, 16 * 1536], BF16)
    qb = sb("qb", [128, 16 * 1024], BF16)
    kb = sb("kb", [128, 16 * 1536], BF16)
    vb = sb("vb", [128, 12 * 2048], BF16)
    rbuf = sb("rbuf", [128, 2048], F32)
    tbuf = sb("tbuf", [128, 4096], BF16)
    lpart = sb("lpart", [128, 512], F32)
    linv = lpart[0:1, :]
    ctmp = sb("ctmp", [128, 512], F32)
    stats = sb("stats", [128, 96], F32)
    aggr = sb("aggr", [128, 8], F32)
    rstd = sb("rstd", [128, 4], F32)
    yfm = xkv[:, 0:8192]
    expp = xkv[:, 8192:9216]
    exppm = xkv[:, 9216:10240]
    wqb = tbuf[:, 0:2048]
    wT = tbuf[:, 2048:4096]
    ytm = tbuf[:, 0:2048]
    lny = tbuf[:, 2048:4096]
    xraw = rbuf[:, 0:2048]
    ps_a = stack.enter_context(nc.psum_tensor(pfx + "ps_a", [128, 512], F32))
    ps_b = stack.enter_context(nc.psum_tensor(pfx + "ps_b", [128, 512], F32))
    eps_col = c_scl[:, 1:2]

    with nc.Block() as blk:
        @blk.sync
        def _(sy):
            consts = [(c_ident, ident_i), (c_masks, masks_i),
                      (c_bias, bias_i), (c_scl, scl_i),
                      (c_ones, ones_i), (c_onesr, onesr_i)]
            if bidx_i is not None:
                consts.append((c_bidx, bidx_i))
            for dst, src in consts:
                sy.dma_start(dst[:, :], src.ap()).then_inc(ctx.h["ds"], 16)
            ctx.n["ds"] += len(consts) * 16
            sy.wait_ge(ctx.h["ds"], ctx.n["ds"])

    for run in range(2):
        tp_stack = contextlib.ExitStack()
        ps_tp = tp_stack.enter_context(nc.psum_tensor(pfx + "b2tp_%d" % (run * 10), [128, 512], F32))
        ps_tp2 = tp_stack.enter_context(nc.psum_tensor(pfx + "b2tp2_%d" % (run * 10), [128, 512], F32))

        for g in range(12):
            if gb_ap is not None and g < 4:
                dk0 = ctx.n["dk"] + 16
                with nc.Block() as blk:
                    @blk.gpsimd
                    def _(gp, run=run, g=g):
                        gp.indirect_dma_start(
                            rbuf[:, :], None, gb_ap.ap(),
                            IndirectOffsetOnAxis(c_bidx[:, run * 4 + g: run * 4 + g + 1], 0),
                        ).then_inc(ctx.h["dk"], 16)
                ctx.n["dk"] = dk0
                ld_wait = ("dk", dk0)
            else:
                ds0 = ctx.n["ds"] + 16
                with nc.Block() as blk:
                    @blk.sync
                    def _(sy, run=run, g=g):
                        if g < 4:
                            src = xbnd_i[run * 512 + g * 128: run * 512 + (g + 1) * 128, :]
                        else:
                            r0 = run * 1024 + (g - 4) * 128
                            src = resid_i[r0:r0 + 128, :]
                        sy.dma_start(rbuf[:, :], src).then_inc(ctx.h["ds"], 16)
                ctx.n["ds"] = ds0
                ld_wait = ("ds", ds0)
            emit_ln_block(ctx, nc, rbuf, 1, stats, aggr, rstd, eps_col,
                          out_ap=tbuf[:, 0:2048], dg_wait=ld_wait)
            emit_tm2fm_gen(ctx, nc, tbuf[:, 0:2048],
                           lambda dt, g=g: xkv[:, dt * 1536 + g * 128: dt * 1536 + (g + 1) * 128],
                           c_ident[:, :], ps_tp, ps_tp2, ntt=1)

        for ft in range(48):
            dg0 = ctx.n["dg"] + 16
            with nc.Block() as blk:
                @blk.gpsimd
                def _(g, ft=ft):
                    g.dma_start(wqb[:, :], wqkv_i[ft * 128:(ft + 1) * 128, :]
                                ).then_inc(ctx.h["dg"], 16)
            ctx.n["dg"] = dg0

            pairs, evacs = [], []
            for j in range(4):
                ps = ps_tp if j % 2 == 0 else ps_tp2
                grp = [(ps[:, k4 * 128:(k4 + 1) * 128],
                        wqb[:, (4 * j + k4) * 128:(4 * j + k4 + 1) * 128])
                       for k4 in range(4)]
                pairs.append(grp)
                evacs.append((wT[:, j * 512:(j + 1) * 512], ps[:, :]))
            emit_transpose(ctx, nc, pairs, c_ident[:, :], evacs,
                           pre_waits=[("dg", dg0)])

            pe0, act0 = ctx.n["pe"], ctx.n["act"]
            with nc.Block() as blk:
                @blk.tensor
                def _(t, ft=ft, pe0=pe0, act0=act0):
                    if ft < 32:
                        chunks = [1, 2] if ft < 16 else [0, 1, 2]
                        for i, c in enumerate(chunks):
                            ps = ps_a if i % 2 == 0 else ps_b
                            if i >= 2:
                                t.wait_ge(ctx.h["act"], act0 + i - 1)
                            for dm in range(16):
                                t.matmul(ps[:, :], wT[:, dm * 128:(dm + 1) * 128],
                                         xkv[:, dm * 1536 + c * 512: dm * 1536 + (c + 1) * 512],
                                         start=(dm == 0), stop=(dm == 15))
                            t.drain().then_inc(ctx.h["pe"], 1)
                    else:
                        for kt in range(12):
                            ps = ps_a if kt % 2 == 0 else ps_b
                            if kt >= 2:
                                t.wait_ge(ctx.h["act"], act0 + kt - 1)
                            for dm in range(16):
                                t.matmul(ps[:, 0:128],
                                         xkv[:, dm * 1536 + kt * 128: dm * 1536 + (kt + 1) * 128],
                                         wT[:, dm * 128:(dm + 1) * 128],
                                         start=(dm == 0), stop=(dm == 15))
                            t.drain().then_inc(ctx.h["pe"], 1)

                @blk.scalar
                def _(s, ft=ft, pe0=pe0, act0=act0):
                    if ft < 32:
                        chunks = [1, 2] if ft < 16 else [0, 1, 2]
                        for i, c in enumerate(chunks):
                            ps = ps_a if i % 2 == 0 else ps_b
                            s.wait_ge(ctx.h["pe"], pe0 + i + 1)
                            if ft < 16:
                                dst = qb[:, ft * 1024 + (c - 1) * 512: ft * 1024 + c * 512]
                            else:
                                dst = kb[:, (ft - 16) * 1536 + c * 512: (ft - 16) * 1536 + (c + 1) * 512]
                            s.copy(dst, ps[:, :])
                            s.drain().then_inc(ctx.h["act"], 1)
                    else:
                        for kt in range(12):
                            ps = ps_a if kt % 2 == 0 else ps_b
                            s.wait_ge(ctx.h["pe"], pe0 + kt + 1)
                            s.copy(vb[:, kt * 2048 + (ft - 32) * 128: kt * 2048 + (ft - 31) * 128],
                                   ps[:, 0:128])
                            s.drain().then_inc(ctx.h["act"], 1)
            ngr = (2 if ft < 16 else 3) if ft < 32 else 12
            ctx.n["pe"] = pe0 + ngr
            ctx.n["act"] = act0 + ngr

        tp_stack.close()

        for pos in range(2):
            at_stack = contextlib.ExitStack()
            ps_s0 = at_stack.enter_context(nc.psum_tensor(pfx + "b2s0_%d" % (run * 10 + pos), [128, 1024], F32))
            ps_s1 = at_stack.enter_context(nc.psum_tensor(pfx + "b2s1_%d" % (run * 10 + pos), [128, 1024], F32))
            ps_o = at_stack.enter_context(nc.psum_tensor(pfx + "b2o_%d" % (run * 10 + pos), [128, 512], F32))
            ps_l = at_stack.enter_context(nc.psum_tensor(pfx + "b2l_%d" % (run * 10 + pos), [1, 512], F32))

            for hh in range(16):
                pe0, act0, dve0 = ctx.n["pe"], ctx.n["act"], ctx.n["dve"]
                kpn = 4
                with nc.Block() as blk:
                    @blk.tensor
                    def _(t, run=run, pos=pos, hh=hh, pe0=pe0, act0=act0, dve0=dve0):
                        for kp in range(kpn):
                            ps_s = ps_s0 if kp % 2 == 0 else ps_s1
                            if kp >= 2:
                                t.wait_ge(ctx.h["act"], act0 + kp - 1)
                            ins = None
                            for half in range(2):
                                kt = 2 * kp + half
                                ins = t.matmul(ps_s[:, half * 512:(half + 1) * 512],
                                               kb[:, hh * 1536 + pos * 512 + kt * 128:
                                                  hh * 1536 + pos * 512 + (kt + 1) * 128],
                                               qb[:, hh * 1024 + pos * 512:
                                                  hh * 1024 + (pos + 1) * 512],
                                               start=True, stop=True)
                            ins.then_inc(ctx.h["pe"], 1)
                            t.wait_ge(ctx.h["dve"], dve0 + kp + 1)
                            for half in range(2):
                                kt = 2 * kp + half
                                ins = t.matmul(ps_o[:, :],
                                               vb[:, (pos * 4 + kt) * 2048 + hh * 128:
                                                  (pos * 4 + kt) * 2048 + (hh + 1) * 128],
                                               exppm[:, half * 512:(half + 1) * 512],
                                               start=(kp == 0 and half == 0),
                                               stop=(kp == kpn - 1 and half == 1))
                            ins.then_inc(ctx.h["pe"], 1)
                        t.wait_ge(ctx.h["dve"], dve0 + kpn)
                        t.matmul(ps_l[:, :], c_ones[:, :], lpart[:, :],
                                 start=True, stop=True)
                        t.drain().then_inc(ctx.h["pe"], 1)
                        t.wait_ge(ctx.h["dve"], dve0 + kpn + 1)
                        t.matmul(ps_a[:, :], c_onesr[:, :], linv[:, :],
                                 start=True, stop=True)
                        t.drain().then_inc(ctx.h["pe"], 1)

                    @blk.scalar
                    def _(s, run=run, pos=pos, pe0=pe0, act0=act0, dve0=dve0):
                        col0 = (run * 2 + pos) * 4
                        for kp in range(kpn):
                            ps_s = ps_s0 if kp % 2 == 0 else ps_s1
                            s.wait_ge(ctx.h["pe"], pe0 + 2 * kp + 1)
                            if kp >= 1:
                                s.wait_ge(ctx.h["dve"], dve0 + kp)
                            s.activation(expp[:, :], ps_s[:, :], AF.Exp,
                                         bias=c_bias[:, col0 + kp:col0 + kp + 1],
                                         scale=c_scl[:, 0:1])
                            s.drain().then_inc(ctx.h["act"], 1)
                        s.wait_ge(ctx.h["pe"], pe0 + 2 * kpn)
                        s.copy(ctmp[:, :], ps_o[:, :])
                        s.drain().then_inc(ctx.h["act"], 1)

                    @blk.vector
                    def _(v, pos=pos, hh=hh, pe0=pe0, act0=act0, dve0=dve0):
                        v.memset(lpart[:, :], 0.0)
                        v.drain()

                        for kp in range(kpn):
                            v.wait_ge(ctx.h["act"], act0 + kp + 1)
                            if kp >= 1:
                                v.wait_ge(ctx.h["pe"], pe0 + 2 * kp)
                            v.tensor_tensor(exppm[:, :], expp[:, :],
                                            c_masks[:, kp * 1024:(kp + 1) * 1024],
                                            ALU.mult)
                            v.drain()
                            v.tensor_add(lpart[:, :], lpart[:, :], exppm[:, 0:512])
                            v.drain()
                            v.tensor_add(lpart[:, :], lpart[:, :], exppm[:, 512:1024])
                            v.drain().then_inc(ctx.h["dve"], 1)
                        v.wait_ge(ctx.h["pe"], pe0 + 2 * kpn + 1)
                        v.reciprocal(linv[:, :], ps_l[:, :])
                        v.drain().then_inc(ctx.h["dve"], 1)
                        v.wait_ge(ctx.h["pe"], pe0 + 2 * kpn + 2)
                        v.wait_ge(ctx.h["act"], act0 + kpn + 1)
                        v.tensor_mul(qb[:, hh * 1024 + pos * 512: hh * 1024 + (pos + 1) * 512],
                                     ctmp[:, :], ps_a[:, :])
                        v.drain().then_inc(ctx.h["dve"], 1)

                ctx.n["pe"] = pe0 + 2 * kpn + 2
                ctx.n["act"] = act0 + kpn + 1
                ctx.n["dve"] = dve0 + kpn + 2

            at_stack.close()
            tp_stack = contextlib.ExitStack()
            ps_tp = tp_stack.enter_context(nc.psum_tensor(pfx + "b2tpo_%d" % (run * 10 + pos), [128, 512], F32))
            ps_tp2 = tp_stack.enter_context(nc.psum_tensor(pfx + "b2tpo2_%d" % (run * 10 + pos), [128, 512], F32))

            for u in range(16):
                dg0 = ctx.n["dg"] + 16
                with nc.Block() as blk:
                    @blk.gpsimd
                    def _(g, u=u):
                        g.dma_start(wqb[:, :], wout_i[u * 128:(u + 1) * 128, :]
                                    ).then_inc(ctx.h["dg"], 16)
                ctx.n["dg"] = dg0
                pairs, evacs = [], []
                for j in range(4):
                    ps = ps_tp if j % 2 == 0 else ps_tp2
                    grp = [(ps[:, k4 * 128:(k4 + 1) * 128],
                            wqb[:, (4 * j + k4) * 128:(4 * j + k4 + 1) * 128])
                           for k4 in range(4)]
                    pairs.append(grp)
                    evacs.append((wT[:, j * 512:(j + 1) * 512], ps[:, :]))
                emit_transpose(ctx, nc, pairs, c_ident[:, :], evacs,
                               pre_waits=[("dg", dg0)])

                pe0, act0 = ctx.n["pe"], ctx.n["act"]
                with nc.Block() as blk:
                    @blk.tensor
                    def _(t, pos=pos, u=u, pe0=pe0):
                        ps = ps_a if u % 2 == 0 else ps_b
                        for dm in range(16):
                            t.matmul(ps[:, :], wT[:, dm * 128:(dm + 1) * 128],
                                     qb[:, dm * 1024 + pos * 512: dm * 1024 + (pos + 1) * 512],
                                     start=(dm == 0), stop=(dm == 15))
                        t.drain().then_inc(ctx.h["pe"], 1)

                    @blk.scalar
                    def _(s, u=u, pe0=pe0):
                        s.wait_ge(ctx.h["pe"], pe0 + 1)
                        s.copy(yfm[:, u * 512:(u + 1) * 512], ps_a[:, :] if u % 2 == 0 else ps_b[:, :])
                        s.drain().then_inc(ctx.h["act"], 1)
                ctx.n["pe"] = pe0 + 1
                ctx.n["act"] = act0 + 1

            emit_epilogue_chunk(ctx, nc, yfm, resid_i, resid_o,
                                run * 1024 + pos * 512,
                                xraw, ytm, lny, stats, aggr, rstd, eps_col,
                                ctmp, ps_tp, ps_tp2, c_ident[:, :])
            tp_stack.close()

    with nc.Block() as blk:
        @blk.sync
        def _(sy):
            sy.wait_ge(ctx.h["st"], ctx.n["st"])

    stack.close()
    if own:
        ctx.close()
    return nc


def _prep_b2_inmaps(inputs, resids2):
    wqkv = np.ascontiguousarray(np.asarray(inputs["L1_attn_w_qkv"], dtype=np.float32))
    wout = np.ascontiguousarray(np.asarray(inputs["L1_attn_w_out"], dtype=np.float32))
    ident = np.eye(128, dtype=BF)
    masks = host_swa_masks()
    scl = host_scl()
    ones32 = np.ones((128, 1), dtype=np.float32)
    onesr32 = np.ones((1, 128), dtype=np.float32)
    have_r = resids2[0] is not None
    if have_r:
        full = np.zeros((B, S, D), dtype=np.float32)
        for c in range(NCORE):
            b = c // 4
            blkA, blkB = core_blocks(c)
            full[b, 1024 * blkA:1024 * blkA + 1024] = resids2[c][0:1024]
            full[b, 1024 * blkB:1024 * blkB + 1024] = resids2[c][1024:2048]
    in_maps = []
    for c in range(NCORE):
        b = c // 4
        blkA, blkB = core_blocks(c)
        m = {
            "w_qkv": wqkv, "w_out": wout,
            "ident": ident, "masks": masks, "bias16": host_b2_bias(c),
            "scl": scl, "ones32": ones32, "onesr32": onesr32,
        }
        if have_r:
            xbnd = np.zeros((1024, D), dtype=np.float32)
            for r, blk in enumerate([blkA, blkB]):
                if blk > 0:
                    xbnd[r * 512:(r + 1) * 512] = full[b, 1024 * blk - 512:1024 * blk]
            m["resid"] = np.ascontiguousarray(resids2[c])
            m["xbnd"] = xbnd
        in_maps.append(m)
    return in_maps


def run_b2(inputs, resids2):
    nc = build_b2()
    in_maps = _prep_b2_inmaps(inputs, resids2)
    res = run_bass_kernel_spmd(nc, in_maps, core_ids=list(range(8)))
    return [np.asarray(r["resid_o"]) for r in res.results]



def build_b3(nc=None, ctx=None, pfx="", io=None):
    own = nc is None
    if own:
        nc = bass.Bass("TRN2", target_bir_lowering=False)
        ctx = Ctx(nc)

    resid_i = _io(io, "resid", lambda: nc.dram_tensor("resid", [T, D], F32, kind="ExternalInput"))
    gw_i = _io(io, "qgate_w", lambda: nc.dram_tensor("gate_w", [8192, 2048], I32, kind="ExternalInput"))
    uw_i = nc.dram_tensor("up_w", [8192, 2048], I32, kind="ExternalInput")
    dw_i = nc.dram_tensor("down_w", [2048, 8192], I32, kind="ExternalInput")
    gsc_i = nc.dram_tensor("gsc", [128, 64], F32, kind="ExternalInput")
    usc_i = nc.dram_tensor("usc", [128, 64], F32, kind="ExternalInput")
    dsc_i = nc.dram_tensor("dsc", [128, 16], F32, kind="ExternalInput")
    ident_i = _io(io, "ident", lambda: nc.dram_tensor("ident", [128, 128], BF16, kind="ExternalInput"))
    scl_i = _io(io, "scl", lambda: nc.dram_tensor("scl", [128, 8], F32, kind="ExternalInput"))

    hspill = nc.dram_tensor(pfx + "hspill", [64 * 128, 2048], BF16)
    wdtsp = nc.dram_tensor(pfx + "wdtsp", [16 * 128, 8192], BF16)

    resid_o = _io(io, "resid_out", lambda: nc.dram_tensor("resid_o", [T, D], F32, kind="ExternalOutput"))

    stack = contextlib.ExitStack()

    def sb(name, shape, dt):
        return stack.enter_context(nc.sbuf_tensor(pfx + name, shape, dt))

    c_ident = sb("c_ident", [128, 128], BF16)
    c_gsc = sb("c_gsc", [128, 64], F32)
    c_usc = sb("c_usc", [128, 64], F32)
    c_dsc = sb("c_dsc", [128, 16], F32)
    c_scl = sb("c_scl", [128, 8], F32)
    xfm = sb("xfm", [128, 16 * 2048], BF16)
    gi32 = sb("gi32", [128, 2048], I32)
    ui32 = sb("ui32", [128, 2048], I32)
    wgb = sb("wgb", [128, 2048], BF16)
    wub = sb("wub", [128, 2048], BF16)
    wgT = sb("wgT", [128, 2048], BF16)
    wuT = sb("wuT", [128, 2048], BF16)
    hb = sb("hb", [128, 2048], BF16)
    wdld = sb("wdld", [128, 8192], BF16)
    rbuf = sb("rbuf", [128, 2 * 2048], F32)
    tbuf = sb("tbuf", [128, 2 * 2048], BF16)
    yfm = sb("yfm", [128, 16 * 512], BF16)
    ctmp = sb("ctmp", [128, 512], F32)
    stats = sb("stats", [128, 96], F32)
    aggr = sb("aggr", [128, 8], F32)
    rstd = sb("rstd", [128, 4], F32)
    ytm = tbuf[:, 0:2048]
    lny = tbuf[:, 2048:4096]
    xraw = rbuf[:, 0:2048]
    sg = tbuf[:, 0:512]
    su = tbuf[:, 512:1024]
    di32 = xfm.bitcast(I32)
    wdb = xfm[:, 16384:24576]
    wdT = xfm[:, 24576:32768]
    h64 = xfm
    ps_a = stack.enter_context(nc.psum_tensor(pfx + "ps_a", [128, 512], F32))
    ps_b = stack.enter_context(nc.psum_tensor(pfx + "ps_b", [128, 512], F32))
    ps_tp = stack.enter_context(nc.psum_tensor(pfx + "ps_tp", [128, 512], F32))
    ps_tp2 = stack.enter_context(nc.psum_tensor(pfx + "ps_tp2", [128, 512], F32))
    eps_col = c_scl[:, 1:2]

    with nc.Block() as blk:
        @blk.sync
        def _(sy):
            for dst, src in [(c_ident, ident_i), (c_gsc, gsc_i), (c_usc, usc_i),
                             (c_dsc, dsc_i), (c_scl, scl_i)]:
                sy.dma_start(dst[:, :], src.ap()).then_inc(ctx.h["ds"], 16)
            ctx.n["ds"] += 5 * 16
            sy.wait_ge(ctx.h["ds"], ctx.n["ds"])

    emit_xln_phase(ctx, nc, resid_i, 0, T,
                   lambda g, dt: xfm[:, dt * 2048 + g * 256: dt * 2048 + (g + 1) * 256],
                   rbuf, tbuf, stats, aggr, rstd, eps_col, c_ident[:, :],
                   ps_tp, ps_tp2, ntt=2)

    for it in range(64):
        ds0 = ctx.n["ds"] + 32
        with nc.Block() as blk:
            @blk.sync
            def _(sy, it=it):
                sy.dma_start(gi32[:, :], gw_i[it * 128:(it + 1) * 128, :]
                             ).then_inc(ctx.h["ds"], 16)
                sy.dma_start(ui32[:, :], uw_i[it * 128:(it + 1) * 128, :]
                             ).then_inc(ctx.h["ds"], 16)
        ctx.n["ds"] = ds0

        dve0 = ctx.n["dve"]
        with nc.Block() as blk:
            @blk.vector
            def _(v, it=it, ds0=ds0, dve0=dve0):
                v.wait_ge(ctx.h["ds"], ds0)
                v.tensor_scalar(wgb[:, :], gi32[:, :], -128.0, c_gsc[:, it:it + 1],
                                ALU.add, ALU.mult)
                v.drain()
                v.tensor_scalar(wub[:, :], ui32[:, :], -128.0, c_usc[:, it:it + 1],
                                ALU.add, ALU.mult)
                v.drain().then_inc(ctx.h["dve"], 1)

            @blk.scalar
            def _(s):
                pass
        ctx.n["dve"] = dve0 + 1

        pairs, evacs = [], []
        for m, (srcb, dstb) in enumerate([(wgb, wgT), (wub, wuT)]):
            for j in range(4):
                ps = ps_tp if (m * 4 + j) % 2 == 0 else ps_tp2
                grp = [(ps[:, k4 * 128:(k4 + 1) * 128],
                        srcb[:, (4 * j + k4) * 128:(4 * j + k4 + 1) * 128])
                       for k4 in range(4)]
                pairs.append(grp)
                evacs.append((dstb[:, j * 512:(j + 1) * 512], ps[:, :]))
        emit_transpose(ctx, nc, pairs, c_ident[:, :], evacs,
                       pre_waits=[("dve", ctx.n["dve"])])

        st0 = ctx.n["st"]
        pe0, act0, dve0 = ctx.n["pe"], ctx.n["act"], ctx.n["dve"]
        with nc.Block() as blk:
            @blk.tensor
            def _(t, it=it, pe0=pe0, act0=act0):
                for ch in range(4):
                    if ch >= 1:
                        t.wait_ge(ctx.h["act"], act0 + 2 * ch - 1)
                    for dm in range(16):
                        t.matmul(ps_a[:, :], wgT[:, dm * 128:(dm + 1) * 128],
                                 xfm[:, dm * 2048 + ch * 512: dm * 2048 + (ch + 1) * 512],
                                 start=(dm == 0), stop=(dm == 15))
                    t.drain().then_inc(ctx.h["pe"], 1)
                    if ch >= 1:
                        t.wait_ge(ctx.h["act"], act0 + 2 * ch)
                    for dm in range(16):
                        t.matmul(ps_b[:, :], wuT[:, dm * 128:(dm + 1) * 128],
                                 xfm[:, dm * 2048 + ch * 512: dm * 2048 + (ch + 1) * 512],
                                 start=(dm == 0), stop=(dm == 15))
                    t.drain().then_inc(ctx.h["pe"], 1)

            @blk.scalar
            def _(s, it=it, pe0=pe0, act0=act0, dve0=dve0):
                for ch in range(4):
                    s.wait_ge(ctx.h["pe"], pe0 + 2 * ch + 1)
                    if ch >= 1:
                        s.wait_ge(ctx.h["dve"], dve0 + ch)
                    s.activation(sg[:, :], ps_a[:, :], AF.Silu)
                    s.drain().then_inc(ctx.h["act"], 1)
                    s.wait_ge(ctx.h["pe"], pe0 + 2 * ch + 2)
                    s.copy(su[:, :], ps_b[:, :])
                    s.drain().then_inc(ctx.h["act"], 1)

            @blk.vector
            def _(v, it=it, act0=act0, dve0=dve0):
                for ch in range(4):
                    v.wait_ge(ctx.h["act"], act0 + 2 * ch + 2)
                    v.tensor_tensor(hb[:, ch * 512:(ch + 1) * 512], sg[:, :], su[:, :],
                                    ALU.mult)
                    v.drain().then_inc(ctx.h["dve"], 1)

            @blk.sync
            def _(sy, it=it, dve0=dve0, st0=st0):
                sy.wait_ge(ctx.h["dve"], dve0 + 4)
                sy.dma_start(hspill[it * 128:(it + 1) * 128, :], hb[:, :]
                             ).then_inc(ctx.h["st"], 16)
                sy.wait_ge(ctx.h["st"], st0 + 16)
        ctx.n["pe"] = pe0 + 8
        ctx.n["act"] = act0 + 8
        ctx.n["dve"] = dve0 + 4
        ctx.n["st"] = st0 + 16

    for dmt in range(16):
        ds0 = ctx.n["ds"] + 16
        with nc.Block() as blk:
            @blk.sync
            def _(sy, dmt=dmt):
                sy.dma_start(di32[:, 0:8192], dw_i[dmt * 128:(dmt + 1) * 128, :]
                             ).then_inc(ctx.h["ds"], 16)
        ctx.n["ds"] = ds0

        dve0 = ctx.n["dve"]
        with nc.Block() as blk:
            @blk.vector
            def _(v, dmt=dmt, ds0=ds0, dve0=dve0):
                v.wait_ge(ctx.h["ds"], ds0)
                v.tensor_scalar(wdb[:, :], di32[:, 0:8192], -128.0,
                                c_dsc[:, dmt:dmt + 1], ALU.add, ALU.mult)
                v.drain().then_inc(ctx.h["dve"], 1)

            @blk.scalar
            def _(s):
                pass
        ctx.n["dve"] = dve0 + 1

        pairs, evacs = [], []
        for j in range(16):
            ps = ps_tp if j % 2 == 0 else ps_tp2
            grp = [(ps[:, k4 * 128:(k4 + 1) * 128],
                    wdb[:, (4 * j + k4) * 128:(4 * j + k4 + 1) * 128])
                   for k4 in range(4)]
            pairs.append(grp)
            evacs.append((wdT[:, j * 512:(j + 1) * 512], ps[:, :]))
        emit_transpose(ctx, nc, pairs, c_ident[:, :], evacs,
                       pre_waits=[("dve", ctx.n["dve"])])

        st0 = ctx.n["st"]
        with nc.Block() as blk:
            @blk.sync
            def _(sy, dmt=dmt, st0=st0):
                sy.dma_start(wdtsp[dmt * 128:(dmt + 1) * 128, :], wdT[:, :]
                             ).then_inc(ctx.h["st"], 16)
                sy.wait_ge(ctx.h["st"], st0 + 16)
        ctx.n["st"] = st0 + 16

    for ch in range(4):
        ds0 = ctx.n["ds"] + 64 * 16
        with nc.Block() as blk:
            @blk.sync
            def _(sy, ch=ch):
                for it in range(64):
                    src = bass.AP(hspill, (it * 128) * 2048 + ch * 512,
                                  [[2048, 128], [1, 512]])
                    sy.dma_start(h64[:, it * 512:(it + 1) * 512], src
                                 ).then_inc(ctx.h["ds"], 16)
        ctx.n["ds"] = ds0

        for dmt in range(16):
            dg0 = ctx.n["dg"] + 16
            pe0, act0 = ctx.n["pe"], ctx.n["act"]
            with nc.Block() as blk:
                @blk.gpsimd
                def _(g, dmt=dmt):
                    g.dma_start(wdld[:, :], wdtsp[dmt * 128:(dmt + 1) * 128, :]
                                ).then_inc(ctx.h["dg"], 16)

                @blk.tensor
                def _(t, ch=ch, dmt=dmt, pe0=pe0, dg0=dg0, ds0=ds0):
                    t.wait_ge(ctx.h["ds"], ds0)
                    t.wait_ge(ctx.h["dg"], dg0)
                    ps = ps_a if dmt % 2 == 0 else ps_b
                    for it in range(64):
                        t.matmul(ps[:, :], wdld[:, it * 128:(it + 1) * 128],
                                 h64[:, it * 512:(it + 1) * 512],
                                 start=(it == 0), stop=(it == 63))
                    t.drain().then_inc(ctx.h["pe"], 1)

                @blk.scalar
                def _(s, dmt=dmt, pe0=pe0):
                    ps = ps_a if dmt % 2 == 0 else ps_b
                    s.wait_ge(ctx.h["pe"], pe0 + 1)
                    s.copy(yfm[:, dmt * 512:(dmt + 1) * 512], ps[:, :])
                    s.drain().then_inc(ctx.h["act"], 1)
            ctx.n["pe"] = pe0 + 1
            ctx.n["act"] = act0 + 1
            ctx.n["dg"] = dg0

        emit_epilogue_chunk(ctx, nc, yfm, resid_i, resid_o, ch * 512,
                            xraw, ytm, lny, stats, aggr, rstd, eps_col,
                            ctmp, ps_tp, ps_tp2, c_ident[:, :])

    with nc.Block() as blk:
        @blk.sync
        def _(sy):
            sy.wait_ge(ctx.h["st"], ctx.n["st"])

    stack.close()
    if own:
        ctx.close()
    return nc


def _prep_b3_inmaps(inputs, resids3):
    gw = np.ascontiguousarray(np.asarray(inputs["L1_ffn_gate_w"], dtype=np.int32))
    uw = np.ascontiguousarray(np.asarray(inputs["L1_ffn_up_w"], dtype=np.int32))
    dw = np.ascontiguousarray(np.asarray(inputs["L1_ffn_down_w"], dtype=np.int32))
    gsc = np.ascontiguousarray(np.asarray(inputs["L1_ffn_gate_scale"], dtype=np.float32
                                          ).reshape(64, 128).T)
    usc = np.ascontiguousarray(np.asarray(inputs["L1_ffn_up_scale"], dtype=np.float32
                                          ).reshape(64, 128).T)
    dsc = np.ascontiguousarray(np.asarray(inputs["L1_ffn_down_scale"], dtype=np.float32
                                          ).reshape(16, 128).T)
    ident = np.eye(128, dtype=BF)
    scl = host_scl()
    in_maps = []
    for c in range(NCORE):
        m = {
            "gate_w": gw, "up_w": uw, "down_w": dw,
            "gsc": gsc, "usc": usc, "dsc": dsc,
            "ident": ident, "scl": scl,
        }
        if resids3[c] is not None:
            m["resid"] = np.ascontiguousarray(resids3[c])
        in_maps.append(m)
    return in_maps


def run_b3(inputs, resids3):
    nc = build_b3()
    in_maps = _prep_b3_inmaps(inputs, resids3)
    res = run_bass_kernel_spmd(nc, in_maps, core_ids=list(range(8)))
    return [np.asarray(r["resid_o"]) for r in res.results]



def host_bidx(c):
    idx = np.zeros((128, 8), dtype=np.int32)
    blkA, blkB = core_blocks(c)
    for r, blk in enumerate([blkA, blkB]):
        if blk == 0:
            base = 8192
        else:
            prev = blk - 1
            jp = prev if prev <= 3 else 7 - prev
            o = (c // 4) * 4 + jp
            slab = 0 if prev <= 3 else 1
            base = o * 1024 + slab * 512
        for t in range(4):
            idx[:, r * 4 + t] = base + t * 128 + np.arange(128)
    return idx


def build_fused():
    nc = bass.Bass("TRN2", target_bir_lowering=False)
    ctx = Ctx(nc)

    x_full = nc.dram_tensor("xkv", [S, D], F32, kind="ExternalInput")
    x_own = nc.dram_tensor("xown", [T, D], F32, kind="ExternalInput")
    ident = nc.dram_tensor("ident", [128, 128], BF16, kind="ExternalInput")
    scl = nc.dram_tensor("scl", [128, 8], F32, kind="ExternalInput")
    ones32 = nc.dram_tensor("ones32", [128, 1], F32, kind="ExternalInput")
    onesr32 = nc.dram_tensor("onesr32", [1, 128], F32, kind="ExternalInput")
    masks2 = nc.dram_tensor("masks2", [128, 8 * 512], BF16, kind="ExternalInput")
    qgate_w = nc.dram_tensor("qgate_w", [8192, 2048], I32, kind="ExternalInput")
    bidx = nc.dram_tensor("bidx", [128, 8], I32, kind="ExternalInput")

    r1 = nc.dram_tensor("f_r1", [T, D], F32)
    r2 = nc.dram_tensor("f_r2", [T, D], F32)
    r3 = nc.dram_tensor("f_r3", [T, D], F32)
    ccb = nc.dram_tensor("f_ccb", [1024, D], F32)
    gb = nc.dram_tensor("f_gb", [9216, D], F32)

    resid_o = nc.dram_tensor("resid_o", [T, D], F32, kind="ExternalOutput")

    shared = {"ident": ident, "scl": scl, "ones32": ones32, "onesr32": onesr32}

    build_b0(nc, ctx, "a_", dict(shared, x_full=x_full, x_own=x_own, resid_out=r1))
    build_b1(nc, ctx, "b_", dict(shared, resid=r1, resid_out=r2))

    zstack = contextlib.ExitStack()
    zbuf = zstack.enter_context(nc.sbuf_tensor("f_zbuf", [128, 2048], F32))
    with nc.Block() as blk:
        @blk.vector
        def _(v):
            v.memset(zbuf[:, :], 0.0)
            v.drain().then_inc(ctx.h["dve"], 1)
            ctx.n["dve"] += 1

        @blk.scalar
        def _(s):
            pass

    with nc.Block() as blk:
        @blk.gpsimd
        def _(g):
            g.dma_start(ccb.ap()[0:512, :], r2.ap()[512:1024, :]).then_inc(ctx.h["dg"], 16)
            g.dma_start(ccb.ap()[512:1024, :], r2.ap()[1536:2048, :]).then_inc(ctx.h["dg"], 16)
            ctx.n["dg"] += 32
            g.wait_ge(ctx.h["dg"], ctx.n["dg"])
            g.collective_compute(
                "AllGather", ALU.bypass, replica_groups=[list(range(8))],
                ins=[ccb.ap().opt()], outs=[gb.ap()[0:8192, :].opt()],
            ).then_inc(ctx.h["cc"], 1)
            ctx.n["cc"] += 1
            g.wait_ge(ctx.h["cc"], ctx.n["cc"])

        @blk.sync
        def _(sy):
            sy.wait_ge(ctx.h["dve"], ctx.n["dve"])
            for t in range(4):
                sy.dma_start(gb.ap()[8192 + t * 128: 8192 + (t + 1) * 128, :],
                             zbuf[:, :]).then_inc(ctx.h["st"], 16)
            ctx.n["st"] += 64
            sy.wait_ge(ctx.h["st"], ctx.n["st"])
    zstack.close()

    build_b2(nc, ctx, "c_", dict(shared, resid=r2, resid_out=r3, gb=gb, bidx=bidx,
                                 masks2=masks2))
    build_b3(nc, ctx, "d_", dict(shared, resid=r3, resid_out=resid_o, qgate_w=qgate_w))

    ctx.close()
    return nc


def _prep_fused_inmaps(inputs):
    m0 = _prep_b0_inmaps(inputs)
    m1 = _prep_b1_inmaps(inputs, [None] * NCORE)
    m2 = _prep_b2_inmaps(inputs, [None] * NCORE)
    m3 = _prep_b3_inmaps(inputs, [None] * NCORE)
    in_maps = []
    for c in range(NCORE):
        m = {}
        m.update({k: v for k, v in m0[c].items()})
        m.update({k: v for k, v in m1[c].items() if k != "resid"})
        m.update({k: v for k, v in m2[c].items() if k not in ("resid", "xbnd", "masks")})
        m["masks2"] = m2[c]["masks"]
        m.update({k: v for k, v in m3[c].items() if k not in ("resid", "gate_w")})
        m["qgate_w"] = m3[c]["gate_w"]
        m["bidx"] = host_bidx(c)
        in_maps.append(m)
    return in_maps


# version 03
# speedup vs baseline: 1.0146x; 1.0146x over previous
import contextlib
import numpy as np
import ml_dtypes
import concourse.bass as bass
import concourse.mybir as mybir
from concourse.bass_utils import run_bass_kernel_spmd
from concourse.bass_core_rust import IndirectOffsetOnAxis

F32 = mybir.dt.float32
BF16 = mybir.dt.bfloat16
I32 = mybir.dt.int32
AF = mybir.ActivationFunctionType
ALU = mybir.AluOpType
BF = ml_dtypes.bfloat16

B, S, D = 2, 8192, 2048
DEBUG_B0 = False
T = 2048
NCORE = 8
KTMAX = [28, 32, 60, 64]
KT2 = [k // 2 for k in KTMAX]
NJOB2 = sum(KT2)
EXP_SHIFT = -30.0
SCALE_QK = float(1.0 / np.sqrt(128.0))
NEG_BIG = -1e30
EPS = 1e-5


def core_blocks(c):
    j = c % 4
    return [j, 7 - j]


def qt_q0(c, qt):
    blkA, blkB = core_blocks(c)
    blk = blkA if qt < 2 else blkB
    return 1024 * blk + 512 * (qt % 2)


def job2_col(qt, kp):
    return sum(KT2[:qt]) + kp



def host_causal_masks_paired():
    out = np.zeros((128, 2 * 1024), dtype=np.float32)
    k = np.arange(128)[:, None]
    q = np.arange(512)[None, :]
    for slot, (d0, d1) in enumerate([(0, 128), (256, 384)]):
        out[:, slot * 1024:slot * 1024 + 512] = (k + d0 <= q)
        out[:, slot * 1024 + 512:(slot + 1) * 1024] = (k + d1 <= q)
    return out.astype(BF)


def host_b0_tables(c):
    bias = np.zeros((128, NJOB2), dtype=np.float32)
    kdiag = np.zeros((128, 4), dtype=np.int32)
    for qt in range(4):
        p0 = (qt_q0(c, qt) + 512) // 128
        kdiag[:, qt] = np.arange(128) * 64 + (p0 - 4)
        for pi in range(KT2[qt]):
            col = job2_col(qt, pi)
            if pi < 2:
                bias[:, col] = EXP_SHIFT
            else:
                bias[:, col] = EXP_SHIFT if (2 * (pi - 2) + 1) < (p0 - 4) else NEG_BIG
    return bias, kdiag


def host_scl():
    scl = np.zeros((128, 8), dtype=np.float32)
    scl[:, 0] = SCALE_QK
    scl[:, 1] = EPS
    scl[:, 2] = 1.0 / 8192.0
    scl[:, 3] = -128.0
    return scl



class Ctx:
    def __init__(self, nc):
        self.nc = nc
        self._cms = []
        self.h = {}
        self.n = {}
        for nm in ["ds", "dg", "dk", "cc", "pe", "act", "dve", "st"]:
            cm = nc.semaphore(nm)
            self.h[nm] = cm.__enter__()
            self._cms.append(cm)
            self.n[nm] = 0

    def close(self):
        for cm in reversed(self._cms):
            cm.__exit__(None, None, None)


def ln_vec_emit(ctx, v, x_ap, ntt, stats, aggr, rstd, dve0, act0, out_ap=None,
                dg_wait=None, scratch32=None):
    if out_ap is None:
        out_ap = x_ap
    if dg_wait is not None:
        v.wait_ge(ctx.h[dg_wait[0]], dg_wait[1])
    for t in range(ntt):
        for ch in range(4):
            src = x_ap[:, t * 2048 + 512 * ch: t * 2048 + 512 * (ch + 1)]
            if scratch32 is not None:
                v.tensor_copy(scratch32[:, 0:512], src)
                v.drain()
                src = scratch32[:, 0:512]
            v.bn_stats(stats[:, t * 24 + 6 * ch: t * 24 + 6 * (ch + 1)], src)
            v.drain()
        v.bn_aggr(aggr[:, 2 * t:2 * t + 2], stats[:, t * 24:(t + 1) * 24])
        v.drain()
    v.drain().then_inc(ctx.h["dve"], 1)
    v.wait_ge(ctx.h["act"], act0 + 1)
    for t in range(ntt):
        v.reciprocal(rstd[:, t:t + 1], rstd[:, t:t + 1])
    v.drain()
    for t in range(ntt):
        v.tensor_scalar(out_ap[:, t * 2048:(t + 1) * 2048],
                        x_ap[:, t * 2048:(t + 1) * 2048],
                        aggr[:, 2 * t:2 * t + 1], rstd[:, t:t + 1],
                        ALU.subtract, ALU.mult)
    v.drain().then_inc(ctx.h["dve"], 1)


def ln_scl_emit(ctx, s, ntt, aggr, rstd, eps_col, dve0, act0):
    s.wait_ge(ctx.h["dve"], dve0 + 1)
    for t in range(ntt):
        s.activation(rstd[:, t:t + 1], aggr[:, 2 * t + 1:2 * t + 2], AF.Sqrt,
                     bias=eps_col, scale=1.0)
    s.drain().then_inc(ctx.h["act"], 1)


def emit_ln_block(ctx, nc, x_ap, ntt, stats, aggr, rstd, eps_col, out_ap=None,
                  dg_wait=None, extra_gp=None, scratch32=None):
    dve0, act0 = ctx.n["dve"], ctx.n["act"]
    with nc.Block() as blk:
        @blk.vector
        def _(v):
            ln_vec_emit(ctx, v, x_ap, ntt, stats, aggr, rstd, dve0, act0,
                        out_ap=out_ap, dg_wait=dg_wait, scratch32=scratch32)

        @blk.scalar
        def _(s):
            ln_scl_emit(ctx, s, ntt, aggr, rstd, eps_col, dve0, act0)

        if extra_gp is not None:
            @blk.gpsimd
            def _(g):
                extra_gp(g)
    ctx.n["dve"] = dve0 + 2
    ctx.n["act"] = act0 + 1


def emit_transpose(ctx, nc, pairs, ident, evac_list, pre_waits=()):
    pe0, act0 = ctx.n["pe"], ctx.n["act"]
    ngrp = len(evac_list)
    with nc.Block() as blk:
        @blk.tensor
        def _(t):
            for sem, val in pre_waits:
                t.wait_ge(ctx.h[sem], val)
            for gi, (grp, _dst) in enumerate(zip(pairs, evac_list)):
                if gi >= 2:
                    t.wait_ge(ctx.h["act"], act0 + gi - 1)
                for (dst_ps, src) in grp:
                    t.matmul(dst_ps, src, ident, start=True, stop=True)
                t.drain().then_inc(ctx.h["pe"], 1)

        @blk.scalar
        def _(s):
            for gi, (dst, ps_ap) in enumerate(evac_list):
                s.wait_ge(ctx.h["pe"], pe0 + gi + 1)
                s.copy(dst, ps_ap)
                s.drain().then_inc(ctx.h["act"], 1)
    ctx.n["pe"] = pe0 + ngrp
    ctx.n["act"] = act0 + ngrp


def emit_tm2fm(ctx, nc, src_tm, dst_fm, ident, ps_tp, ps_tp2, ntt=4):
    pairs, evacs = [], []
    for dt in range(16):
        ps = ps_tp if dt % 2 == 0 else ps_tp2
        grp = []
        for tt in range(ntt):
            grp.append((ps[:, tt * 128:(tt + 1) * 128],
                        src_tm[:, tt * 2048 + dt * 128: tt * 2048 + (dt + 1) * 128]))
        pairs.append(grp)
        evacs.append((dst_fm[:, dt * ntt * 128:(dt + 1) * ntt * 128],
                      ps[:, 0:ntt * 128]))
    emit_transpose(ctx, nc, pairs, ident, evacs)


def emit_proj(ctx, nc, wpool, w_off, x_fm, out_sb, nout, psA, psB,
              bias_sb=None, act_func=None, pre_waits=(), ndm=16, xstride=512):
    pe0, act0 = ctx.n["pe"], ctx.n["act"]
    with nc.Block() as blk:
        @blk.tensor
        def _(t):
            for sem, val in pre_waits:
                t.wait_ge(ctx.h[sem], val)
            for u in range(nout):
                ps = psA if u % 2 == 0 else psB
                if u >= 2:
                    t.wait_ge(ctx.h["act"], act0 + u - 1)
                ins = None
                for dm in range(ndm):
                    ins = t.matmul(ps[:, :],
                                   wpool[:, w_off + dm * 2048 + u * 128: w_off + dm * 2048 + (u + 1) * 128],
                                   x_fm[:, dm * xstride:dm * xstride + 512],
                                   start=(dm == 0), stop=(dm == ndm - 1))
                ins.then_inc(ctx.h["pe"], 1)

        @blk.scalar
        def _(s):
            for u in range(nout):
                ps = psA if u % 2 == 0 else psB
                s.wait_ge(ctx.h["pe"], pe0 + u + 1)
                if act_func is not None:
                    s.activation(out_sb[:, u * 512:(u + 1) * 512], ps[:, :],
                                 act_func,
                                 bias=(bias_sb[:, u:u + 1] if bias_sb is not None else 0.0),
                                 scale=1.0)
                else:
                    s.copy(out_sb[:, u * 512:(u + 1) * 512], ps[:, :])
                s.drain().then_inc(ctx.h["act"], 1)
    ctx.n["pe"] = pe0 + nout
    ctx.n["act"] = act0 + nout


def st_grp(st0, g):
    tot = st0
    for i in range(4 * (g + 1)):
        tot += 16 if i < 16 else 64
    return tot




def emit_tm2fm_gen(ctx, nc, src_tm, dst_fn, ident, ps_tp, ps_tp2, ntt=4):
    pairs, evacs = [], []
    for dt in range(16):
        ps = ps_tp if dt % 2 == 0 else ps_tp2
        grp = [(ps[:, tt * 128:(tt + 1) * 128],
                src_tm[:, tt * 2048 + dt * 128: tt * 2048 + (dt + 1) * 128])
               for tt in range(ntt)]
        pairs.append(grp)
        evacs.append((dst_fn(dt), ps[:, 0:ntt * 128]))
    emit_transpose(ctx, nc, pairs, ident, evacs)


def emit_xln_phase(ctx, nc, resid_dram, row0, nrows, xfm_dst_fn, rbuf, tbuf,
                   stats, aggr, rstd, eps_col, ident, ps_tp, ps_tp2, ntt=4):
    for g in range(nrows // (128 * ntt)):
        ds0 = ctx.n["ds"] + ntt * 16
        with nc.Block() as blk:
            @blk.sync
            def _(sy, g=g):
                for tt in range(ntt):
                    r = row0 + g * 128 * ntt + tt * 128
                    sy.dma_start(rbuf[:, tt * 2048:(tt + 1) * 2048],
                                 resid_dram[r:r + 128, :]).then_inc(ctx.h["ds"], 16)
        ctx.n["ds"] = ds0
        emit_ln_block(ctx, nc, rbuf, ntt, stats, aggr, rstd, eps_col,
                      out_ap=tbuf, dg_wait=("ds", ds0))
        emit_tm2fm_gen(ctx, nc, tbuf, lambda dt, g=g: xfm_dst_fn(g, dt),
                       ident, ps_tp, ps_tp2, ntt=ntt)


def emit_epilogue_chunk(ctx, nc, yfm, resid_in, resid_out, row0,
                        xraw, ytm, lny, stats, aggr, rstd, eps_col, ctmp,
                        ps_tp, ps_tp2, ident):
    for tt in range(4):
        pairs, evacs = [], []
        for g in range(4):
            ps = ps_tp if g % 2 == 0 else ps_tp2
            grp = []
            for k4 in range(4):
                dt = g * 4 + k4
                grp.append((ps[:, k4 * 128:(k4 + 1) * 128],
                            yfm[:, dt * 512 + tt * 128: dt * 512 + (tt + 1) * 128]))
            pairs.append(grp)
            evacs.append((ytm[:, g * 512:(g + 1) * 512], ps[:, :]))
        emit_transpose(ctx, nc, pairs, ident, evacs)
        ds_xr = ctx.n["ds"] + 16
        dve0e, act0e = ctx.n["dve"], ctx.n["act"]
        with nc.Block() as blk:
            @blk.vector
            def _(v, tt=tt, dve0e=dve0e, act0e=act0e, ds_xr=ds_xr):
                ln_vec_emit(ctx, v, ytm[:, 0:2048], 1, stats, aggr, rstd,
                            dve0e, act0e, out_ap=lny[:, 0:2048], scratch32=ctmp)
                v.wait_ge(ctx.h["ds"], ds_xr)
                v.tensor_add(xraw[:, :], xraw[:, :], lny[:, 0:2048])
                v.drain().then_inc(ctx.h["dve"], 1)

            @blk.scalar
            def _(s, dve0e=dve0e, act0e=act0e):
                ln_scl_emit(ctx, s, 1, aggr, rstd, eps_col, dve0e, act0e)

            @blk.sync
            def _(sy, tt=tt, dve0e=dve0e):
                r = row0 + tt * 128
                sy.dma_start(xraw[:, :], resid_in[r:r + 128, :]).then_inc(ctx.h["ds"], 16)
                sy.wait_ge(ctx.h["dve"], dve0e + 3)
                sy.dma_start(resid_out[r:r + 128, :], xraw[:, :]).then_inc(ctx.h["st"], 16)
                sy.wait_ge(ctx.h["st"], ctx.n["st"] + 16)
        ctx.n["ds"] = ds_xr
        ctx.n["dve"] = dve0e + 3
        ctx.n["act"] = act0e + 1
        ctx.n["st"] += 16


def host_swa_masks():
    out = np.zeros((128, 8 * 512), dtype=np.float32)
    k = np.arange(128)[:, None]
    q = np.arange(512)[None, :]
    for kt in range(8):
        dlt = -512 + 128 * kt
        m = ((k + dlt <= q) & (k + dlt >= q - 512))
        out[:, kt * 512:(kt + 1) * 512] = m
    return out.astype(BF)


def host_b2_bias(c):
    bias = np.full((128, 16), EXP_SHIFT, dtype=np.float32)
    blkA, blkB = core_blocks(c)
    for run, blk in enumerate([blkA, blkB]):
        for pos in range(2):
            gchunk = 2 * blk + pos
            if gchunk == 0:
                for kp in range(2):
                    bias[:, (run * 2 + pos) * 4 + kp] = NEG_BIG
    return bias



def _io(io, key, mk):
    return io[key] if io is not None and key in io else mk()


def build_b0(nc=None, ctx=None, pfx="", io=None):
    own = nc is None
    if own:
        nc = bass.Bass("TRN2", target_bir_lowering=False)
        ctx = Ctx(nc)

    xkv = _io(io, "x_full", lambda: nc.dram_tensor("xkv", [S, D], F32, kind="ExternalInput"))
    xown = _io(io, "x_own", lambda: nc.dram_tensor("xown", [T, D], F32, kind="ExternalInput"))
    wq_i = nc.dram_tensor("wq", [D, D], F32, kind="ExternalInput")
    wk_i = nc.dram_tensor("wk", [D, D], F32, kind="ExternalInput")
    wv_i = nc.dram_tensor("wv", [D, D], F32, kind="ExternalInput")
    wo_i = nc.dram_tensor("wo", [D, D], F32, kind="ExternalInput")
    bo_i = nc.dram_tensor("bo_t", [128, 16], F32, kind="ExternalInput")
    ident_i = _io(io, "ident", lambda: nc.dram_tensor("ident", [128, 128], BF16, kind="ExternalInput"))
    masks_i = nc.dram_tensor("masks", [128, 2 * 1024], BF16, kind="ExternalInput")
    bias_i = nc.dram_tensor("bias_tbl", [128, NJOB2], F32, kind="ExternalInput")
    kdiag_i = nc.dram_tensor("kdiag", [128, 4], I32, kind="ExternalInput")
    scl_i = _io(io, "scl", lambda: nc.dram_tensor("scl", [128, 8], F32, kind="ExternalInput"))
    ones_i = _io(io, "ones32", lambda: nc.dram_tensor("ones32", [128, 1], F32, kind="ExternalInput"))
    onesr_i = _io(io, "onesr32", lambda: nc.dram_tensor("onesr32", [1, 128], F32, kind="ExternalInput"))

    kspill = nc.dram_tensor(pfx + "kspill", [16 * 128, 64 * 128], BF16)
    vspill = nc.dram_tensor(pfx + "vspill", [16 * 128, 64 * 128], BF16)
    resid1 = _io(io, "resid_out", lambda: nc.dram_tensor("resid1", [T, D], F32, kind="ExternalOutput"))
    if DEBUG_B0:
        dbg_xfm = nc.dram_tensor("dbg_xfm", [128, 16 * 512], F32, kind="ExternalOutput")
        dbg_q = nc.dram_tensor("dbg_q", [128, 16 * 512], F32, kind="ExternalOutput")
        dbg_ctx = nc.dram_tensor("dbg_ctx", [128, 16 * 512], F32, kind="ExternalOutput")
        dbg_y = nc.dram_tensor("dbg_y", [128, 16 * 512], F32, kind="ExternalOutput")
        dbg_kbuf = nc.dram_tensor("dbg_kbuf", [128, 8192], F32, kind="ExternalOutput")
        dbg_xtm = nc.dram_tensor("dbg_xtm", [128, 8192], F32, kind="ExternalOutput")
        dbg_st = nc.dram_tensor("dbg_st", [128, 112], F32, kind="ExternalOutput")
        dbg_xfm2 = nc.dram_tensor("dbg_xfm2", [128, 2048], F32, kind="ExternalOutput")
        dbg_xfm3 = nc.dram_tensor("dbg_xfm3", [128, 2048], F32, kind="ExternalOutput")
        dbg_vbuf = nc.dram_tensor("dbg_vbuf", [128, 8192], F32, kind="ExternalOutput")

    stack = contextlib.ExitStack()

    def sb(name, shape, dt):
        return stack.enter_context(nc.sbuf_tensor(pfx + name, shape, dt))

    c_ident = sb("c_ident", [128, 128], BF16)
    c_masks = sb("c_masks", [128, 2 * 1024], BF16)
    c_bias = sb("c_bias", [128, NJOB2], F32)
    c_kdiag = sb("c_kdiag", [128, 4], I32)
    c_scl = sb("c_scl", [128, 8], F32)
    c_ones = sb("c_ones", [128, 1], F32)
    c_onesr = sb("c_onesr", [1, 128], F32)
    c_bo = sb("c_bo", [128, 16], F32)
    wpool = sb("wpool", [128, 32 * 2048], BF16)
    xtm = sb("xtm", [128, 8192], BF16)
    xfm = sb("xfm", [128, 16 * 512], BF16)
    qctx = sb("qctx", [128, 16 * 512], BF16)
    yfm = xfm
    ytm = xtm[:, 0:2048]
    lny = xtm[:, 2048:4096]
    xraw = sb("xraw", [128, 2048], F32)
    kvev = sb("kvev", [128, 2048], BF16)
    expp = kvev[:, 0:1024]
    exppm = kvev[:, 1024:2048]
    lpart = sb("lpart", [128, 512], F32)
    linv = lpart[0:1, :]
    ctmp = sb("ctmp", [128, 512], F32)
    stats = sb("stats", [128, 96], F32)
    aggr = sb("aggr", [128, 8], F32)
    rstd = sb("rstd", [128, 4], F32)
    ps_a = stack.enter_context(nc.psum_tensor(pfx + "ps_a", [128, 512], F32))
    ps_b = stack.enter_context(nc.psum_tensor(pfx + "ps_b", [128, 512], F32))
    if True:
        WQO = 0
        KBUF = [16 * 2048, 16 * 2048 + 8192]
        VBUF = [16 * 2048 + 2 * 8192, 16 * 2048 + 3 * 8192]
        eps_col = c_scl[:, 1:2]

        _sid_p0, _ = nc.enter_named_scope("wkv", False)
        with nc.Block() as blk:
            @blk.sync
            def _(sy):
                for dst, src in [(c_ident, ident_i), (c_masks, masks_i),
                                 (c_bias, bias_i), (c_kdiag, kdiag_i),
                                 (c_scl, scl_i), (c_ones, ones_i),
                                 (c_onesr, onesr_i), (c_bo, bo_i)]:
                    sy.dma_start(dst[:, :], src.ap()).then_inc(ctx.h["ds"], 16)
                ctx.n["ds"] += 8 * 16
                sy.wait_ge(ctx.h["ds"], ctx.n["ds"])

            @blk.gpsimd
            def _(g):
                for t in range(16):
                    g.dma_start(wpool[:, t * 2048:(t + 1) * 2048],
                                wk_i[128 * t:128 * (t + 1), :]).then_inc(ctx.h["dg"], 16)
                    g.dma_start(wpool[:, (16 + t) * 2048:(17 + t) * 2048],
                                wv_i[128 * t:128 * (t + 1), :]).then_inc(ctx.h["dg"], 16)
                for tt in range(4):
                    g.dma_start(xtm[:, tt * 2048:(tt + 1) * 2048],
                                xkv[tt * 128:(tt + 1) * 128, :]).then_inc(ctx.h["dg"], 16)
                ctx.n["dg"] += 36 * 16
                g.wait_ge(ctx.h["dg"], ctx.n["dg"])


        tp_stack = contextlib.ExitStack()
        ps_tp = tp_stack.enter_context(nc.psum_tensor(pfx + "ps_tp_0", [128, 512], F32))
        ps_tp2 = tp_stack.enter_context(nc.psum_tensor(pfx + "ps_tp2_0", [128, 512], F32))
        nc.leave_named_scope("wkv", _sid_p0, False)
        if True:
            _sid_kv, _ = nc.enter_named_scope("kv", False)
            for sc in range(16):
                xb = xtm[:, 0:8192]

                def loadx(g, sc=sc):
                    if sc > 0:
                        for tt in range(4):
                            g.dma_start(xtm[:, tt * 2048:(tt + 1) * 2048],
                                        xkv[sc * 512 + tt * 128: sc * 512 + (tt + 1) * 128, :]
                                        ).then_inc(ctx.h["dg"], 16)

                dgw = ctx.n["dg"] + (4 * 16 if sc > 0 else 0)
                ctx.n["dg"] = dgw
                emit_ln_block(ctx, nc, xb, 4, stats, aggr, rstd, eps_col,
                              dg_wait=("dg", dgw),
                              extra_gp=loadx, scratch32=ctmp)
                emit_tm2fm(ctx, nc, xb, xfm, c_ident[:, :], ps_tp, ps_tp2)

                pe0, act0, st0 = ctx.n["pe"], ctx.n["act"], ctx.n["st"]


                with nc.Block() as blk:
                    @blk.tensor
                    def _(t, sc=sc, pe0=pe0, act0=act0):
                        for i in range(32):
                            ps = ps_a if i % 2 == 0 else ps_b
                            if i >= 2:
                                t.wait_ge(ctx.h["act"], act0 + i - 1)
                            ins = None
                            if i < 16:
                                dk = i
                                for dm in range(16):
                                    ins = t.matmul(ps[:, :],
                                                   wpool[:, dm * 2048 + dk * 128: dm * 2048 + (dk + 1) * 128],
                                                   xfm[:, dm * 512:(dm + 1) * 512],
                                                   start=(dm == 0), stop=(dm == 15))
                            else:
                                tt, dvc = (i - 16) // 4, (i - 16) % 4
                                for dm in range(16):
                                    ins = t.matmul(ps[:, :],
                                                   xfm[:, dm * 512 + tt * 128: dm * 512 + (tt + 1) * 128],
                                                   wpool[:, (16 + dm) * 2048 + dvc * 512: (16 + dm) * 2048 + (dvc + 1) * 512],
                                                   start=(dm == 0), stop=(dm == 15))
                            ins.then_inc(ctx.h["pe"], 1)

                    @blk.scalar
                    def _(s, sc=sc, pe0=pe0, act0=act0, st0=st0):
                        for i in range(32):
                            ps = ps_a if i % 2 == 0 else ps_b
                            s.wait_ge(ctx.h["pe"], pe0 + i + 1)
                            if i >= 4:
                                s.wait_ge(ctx.h["st"], st_grp(st0, (i - 4) // 4))
                            slot = kvev[:, (i % 4) * 512:(i % 4 + 1) * 512]
                            s.copy(slot, ps[:, :])
                            s.drain().then_inc(ctx.h["act"], 1)

                    @blk.sync
                    def _(sy, sc=sc, act0=act0, st0=st0):
                        for g in range(8):
                            sy.wait_ge(ctx.h["act"], act0 + 4 * (g + 1))
                            for i in range(4 * g, 4 * g + 4):
                                slot = kvev[:, (i % 4) * 512:(i % 4 + 1) * 512]
                                if i < 16:
                                    dkk = i
                                    dst = bass.AP(kspill, dkk * 128 * 8192 + 4 * sc * 128,
                                                  [[8192, 128], [1, 512]])
                                    sy.dma_start(dst, slot[:, :]).then_inc(ctx.h["st"], 16)
                                else:
                                    tt, dvc = (i - 16) // 4, (i - 16) % 4
                                    for u in range(4):
                                        hh = 4 * dvc + u
                                        dst = bass.AP(vspill,
                                                      hh * 128 * 8192 + (4 * sc + tt) * 128,
                                                      [[8192, 128], [1, 128]])
                                        sy.dma_start(dst, slot[:, u * 128:(u + 1) * 128]
                                                     ).then_inc(ctx.h["st"], 16)
                        sy.wait_ge(ctx.h["st"], st_grp(st0, 7))
                ctx.n["pe"] = pe0 + 32
                ctx.n["act"] = act0 + 32
                ctx.n["st"] = st_grp(st0, 7)

            nc.leave_named_scope("kv", _sid_kv, False)
            _sid_at, _ = nc.enter_named_scope("attn", False)
            for qt in range(4):
                ktm, kpn = KTMAX[qt], KT2[qt]

                dg_wq = ctx.n["dg"] + 20 * 16
                with nc.Block() as blk:
                    @blk.gpsimd
                    def _(g, qt=qt):
                        for t in range(16):
                            g.dma_start(wpool[:, WQO + t * 2048: WQO + (t + 1) * 2048],
                                        wq_i[128 * t:128 * (t + 1), :]).then_inc(ctx.h["dg"], 16)
                        for tt in range(4):
                            g.dma_start(xtm[:, tt * 2048:(tt + 1) * 2048],
                                        xown[qt * 512 + tt * 128: qt * 512 + (tt + 1) * 128, :]
                                        ).then_inc(ctx.h["dg"], 16)
                ctx.n["dg"] = dg_wq

                emit_ln_block(ctx, nc, xtm[:, 0:8192], 4, stats, aggr, rstd,
                              eps_col, dg_wait=("dg", dg_wq), scratch32=ctmp)
                if DEBUG_B0 and qt == 0:
                    with nc.Block() as blk:
                        @blk.vector
                        def _(v):
                            v.tensor_copy(xraw[:, 0:8], aggr[:, :])
                            v.tensor_copy(xraw[:, 8:12], rstd[:, :])
                            v.tensor_copy(xraw[:, 16:112], stats[:, :])
                        @blk.scalar
                        def _(s2):
                            pass
                    with nc.Block() as blk:
                        @blk.sync
                        def _(sy):
                            sy.dma_start(dbg_st.ap(), xraw[:, 0:112]).then_inc(ctx.h["st"], 16)
                            ctx.n["st"] += 16
                            sy.wait_ge(ctx.h["st"], ctx.n["st"])
                    for pp in range(4):
                        with nc.Block() as blk:
                            @blk.vector
                            def _(v, pp=pp):
                                v.tensor_copy(xraw[:, :], xtm[:, pp * 2048:(pp + 1) * 2048])
                            @blk.scalar
                            def _(s2):
                                pass
                        with nc.Block() as blk:
                            @blk.sync
                            def _(sy, pp=pp):
                                sy.dma_start(dbg_xtm.ap()[:, pp * 2048:(pp + 1) * 2048],
                                             xraw[:, :]).then_inc(ctx.h["st"], 16)
                                ctx.n["st"] += 16
                                sy.wait_ge(ctx.h["st"], ctx.n["st"])
                emit_tm2fm(ctx, nc, xtm[:, 0:8192], xfm, c_ident[:, :], ps_tp, ps_tp2)

                if DEBUG_B0 and qt == 0:
                    with nc.Block() as blk:
                        @blk.vector
                        def _(v):
                            v.tensor_copy(xraw[:, :], xfm[:, 0:2048])
                        @blk.scalar
                        def _(s2):
                            pass
                    with nc.Block() as blk:
                        @blk.sync
                        def _(sy):
                            sy.dma_start(dbg_xfm2.ap(), xraw[:, :]).then_inc(ctx.h["st"], 16)
                            ctx.n["st"] += 16
                            sy.wait_ge(ctx.h["st"], ctx.n["st"])

                ds_h0 = ctx.n["ds"] + 32
                with nc.Block() as blk:
                    @blk.sync
                    def _(sy, ktm=ktm):
                        sy.dma_start(wpool[:, KBUF[0] + 512:KBUF[0] + 512 + (ktm - 4) * 128],
                                     bass.AP(kspill, 0, [[8192, 128], [1, (ktm - 4) * 128]])
                                     ).then_inc(ctx.h["ds"], 16)
                        sy.dma_start(wpool[:, VBUF[0] + 512:VBUF[0] + 512 + (ktm - 4) * 128],
                                     bass.AP(vspill, 0, [[8192, 128], [1, (ktm - 4) * 128]])
                                     ).then_inc(ctx.h["ds"], 16)

                    @blk.gpsimd
                    def _(g, qt=qt):
                        g.indirect_dma_start(
                            wpool[:, KBUF[0]:KBUF[0] + 512], None,
                            bass.AP(kspill, 0, [[128, 131072], [1, 128]]),
                            IndirectOffsetOnAxis(c_kdiag[:, qt:qt + 1], 0),
                        ).then_inc(ctx.h["dk"], 16)
                        g.indirect_dma_start(
                            wpool[:, VBUF[0]:VBUF[0] + 512], None,
                            bass.AP(vspill, 0, [[128, 131072], [1, 128]]),
                            IndirectOffsetOnAxis(c_kdiag[:, qt:qt + 1], 0),
                        ).then_inc(ctx.h["dk"], 16)
                ctx.n["ds"] = ds_h0
                ctx.n["dk"] += 32
                dk_h0 = ctx.n["dk"]

                if DEBUG_B0 and qt == 0:
                    with nc.Block() as blk:
                        @blk.vector
                        def _(v):
                            v.wait_ge(ctx.h["ds"], ds_h0)
                            v.wait_ge(ctx.h["dk"], dk_h0)
                            v.tensor_copy(xraw[:, :], xfm[:, 0:2048])
                        @blk.scalar
                        def _(s2):
                            pass
                    with nc.Block() as blk:
                        @blk.sync
                        def _(sy):
                            sy.dma_start(dbg_xfm3.ap(), xraw[:, :]).then_inc(ctx.h["st"], 16)
                            ctx.n["st"] += 16
                            sy.wait_ge(ctx.h["st"], ctx.n["st"])

                emit_proj(ctx, nc, wpool, WQO, xfm, qctx, 16, ps_a, ps_b,
                          pre_waits=[("dg", dg_wq)])

                if DEBUG_B0 and qt == 0:
                    with nc.Block() as blk:
                        @blk.vector
                        def _(v):
                            v.wait_ge(ctx.h["ds"], ds_h0)
                            v.wait_ge(ctx.h["dk"], dk_h0)
                            v.tensor_copy(xraw[:, :], xfm[:, 0:2048])
                        @blk.sync
                        def _(sy):
                            pass
                    with nc.Block() as blk:
                        @blk.sync
                        def _(sy):
                            sy.dma_start(dbg_xfm.ap()[:, 0:2048], xraw[:, :]).then_inc(ctx.h["st"], 16)
                            ctx.n["st"] += 16
                            sy.wait_ge(ctx.h["st"], ctx.n["st"])
                    for part in range(8):
                        with nc.Block() as blk:
                            @blk.vector
                            def _(v, part=part):
                                v.tensor_copy(xraw[:, 0:1024], qctx[:, part * 1024:(part + 1) * 1024])
                            @blk.scalar
                            def _(s2):
                                pass
                        with nc.Block() as blk:
                            @blk.sync
                            def _(sy, part=part):
                                sy.dma_start(dbg_q.ap()[:, part * 1024:(part + 1) * 1024],
                                             xraw[:, 0:1024]).then_inc(ctx.h["st"], 16)
                                ctx.n["st"] += 16
                                sy.wait_ge(ctx.h["st"], ctx.n["st"])
                    for part in range(4):
                        with nc.Block() as blk:
                            @blk.vector
                            def _(v, part=part):
                                v.tensor_copy(xraw[:, :], wpool[:, KBUF[0] + part * 2048: KBUF[0] + (part + 1) * 2048])
                            @blk.scalar
                            def _(s2):
                                pass
                        with nc.Block() as blk:
                            @blk.sync
                            def _(sy, part=part):
                                sy.dma_start(dbg_kbuf.ap()[:, part * 2048:(part + 1) * 2048],
                                             xraw[:, :]).then_inc(ctx.h["st"], 16)
                                ctx.n["st"] += 16
                                sy.wait_ge(ctx.h["st"], ctx.n["st"])
                    for part in range(4):
                        with nc.Block() as blk:
                            @blk.vector
                            def _(v, part=part):
                                v.tensor_copy(xraw[:, :], wpool[:, VBUF[0] + part * 2048: VBUF[0] + (part + 1) * 2048])
                            @blk.scalar
                            def _(s2):
                                pass
                        with nc.Block() as blk:
                            @blk.sync
                            def _(sy, part=part):
                                sy.dma_start(dbg_vbuf.ap()[:, part * 2048:(part + 1) * 2048],
                                             xraw[:, :]).then_inc(ctx.h["st"], 16)
                                ctx.n["st"] += 16
                                sy.wait_ge(ctx.h["st"], ctx.n["st"])

                dg_wo = ctx.n["dg"] + 16 * 16
                with nc.Block() as blk:
                    @blk.gpsimd
                    def _(g):
                        for t in range(16):
                            g.dma_start(wpool[:, WQO + t * 2048: WQO + (t + 1) * 2048],
                                        wo_i[128 * t:128 * (t + 1), :]).then_inc(ctx.h["dg"], 16)
                ctx.n["dg"] = dg_wo

                tp_stack.close()
                hd_stack = contextlib.ExitStack()
                ps_s0 = hd_stack.enter_context(nc.psum_tensor(pfx + "ps_s0_%d" % qt, [128, 1024], F32))
                ps_s1 = hd_stack.enter_context(nc.psum_tensor(pfx + "ps_s1_%d" % qt, [128, 1024], F32))
                ps_o = hd_stack.enter_context(nc.psum_tensor(pfx + "ps_o_%d" % qt, [128, 512], F32))
                ps_l = hd_stack.enter_context(nc.psum_tensor(pfx + "ps_l_%d" % qt, [1, 512], F32))
                if True:
                    for hh in range(16):
                        kb, vb = KBUF[hh % 2], VBUF[hh % 2]
                        pe0, act0, dve0 = ctx.n["pe"], ctx.n["act"], ctx.n["dve"]
                        ds_need = ds_h0 + 32 * hh
                        dk_need = dk_h0 + 32 * hh
                        with nc.Block() as blk:
                            @blk.sync
                            def _(sy, hh=hh, ktm=ktm, pe0=pe0):
                                if hh < 15:
                                    sy.wait_ge(ctx.h["pe"], pe0 + 1)
                                    nkb, nvb = KBUF[(hh + 1) % 2], VBUF[(hh + 1) % 2]
                                    sy.dma_start(wpool[:, nkb + 512:nkb + 512 + (ktm - 4) * 128],
                                                 bass.AP(kspill, (hh + 1) * 128 * 8192,
                                                         [[8192, 128], [1, (ktm - 4) * 128]])
                                                 ).then_inc(ctx.h["ds"], 16)
                                    sy.dma_start(wpool[:, nvb + 512:nvb + 512 + (ktm - 4) * 128],
                                                 bass.AP(vspill, (hh + 1) * 128 * 8192,
                                                         [[8192, 128], [1, (ktm - 4) * 128]])
                                                 ).then_inc(ctx.h["ds"], 16)

                            @blk.gpsimd
                            def _(g, qt=qt, hh=hh, pe0=pe0):
                                if hh < 15:
                                    nkb, nvb = KBUF[(hh + 1) % 2], VBUF[(hh + 1) % 2]
                                    g.wait_ge(ctx.h["pe"], pe0 + 1)
                                    g.indirect_dma_start(
                                        wpool[:, nkb:nkb + 512], None,
                                        bass.AP(kspill, 0, [[128, 131072], [1, 128]]),
                                        IndirectOffsetOnAxis(c_kdiag[:, qt:qt + 1], 0),
                                        element_offset=(hh + 1) * 1048576,
                                    ).then_inc(ctx.h["dk"], 16)
                                    g.indirect_dma_start(
                                        wpool[:, nvb:nvb + 512], None,
                                        bass.AP(vspill, 0, [[128, 131072], [1, 128]]),
                                        IndirectOffsetOnAxis(c_kdiag[:, qt:qt + 1], 0),
                                        element_offset=(hh + 1) * 1048576,
                                    ).then_inc(ctx.h["dk"], 16)

                            @blk.tensor
                            def _(t, hh=hh, kb=kb, vb=vb, kpn=kpn,
                                  pe0=pe0, act0=act0, dve0=dve0,
                                  ds_need=ds_need, dk_need=dk_need):
                                t.wait_ge(ctx.h["ds"], ds_need)
                                t.wait_ge(ctx.h["dk"], dk_need)
                                for kp in range(kpn):
                                    ps_s = ps_s0 if kp % 2 == 0 else ps_s1
                                    if kp >= 2:
                                        t.wait_ge(ctx.h["act"], act0 + kp - 1)
                                    ins = None
                                    for half in range(2):
                                        co = kp * 256 + half * 128
                                        ins = t.matmul(ps_s[:, half * 512:(half + 1) * 512],
                                                       wpool[:, kb + co: kb + co + 128],
                                                       qctx[:, hh * 512:(hh + 1) * 512],
                                                       start=True, stop=True)
                                    ins.then_inc(ctx.h["pe"], 1)
                                    t.wait_ge(ctx.h["dve"], dve0 + kp + 1)
                                    em = exppm if kp < 2 else expp
                                    for half in range(2):
                                        co = kp * 256 + half * 128
                                        ins = t.matmul(ps_o[:, :],
                                                       wpool[:, vb + co: vb + co + 128],
                                                       em[:, half * 512:(half + 1) * 512],
                                                       start=(kp == 0 and half == 0),
                                                       stop=(kp == kpn - 1 and half == 1))
                                    ins.then_inc(ctx.h["pe"], 1)
                                t.wait_ge(ctx.h["dve"], dve0 + kpn)
                                t.matmul(ps_l[:, :], c_ones[:, :], lpart[:, :],
                                         start=True, stop=True)
                                t.drain().then_inc(ctx.h["pe"], 1)
                                t.wait_ge(ctx.h["dve"], dve0 + kpn + 1)
                                t.matmul(ps_a[:, :], c_onesr[:, :], linv[:, :],
                                         start=True, stop=True)
                                t.drain().then_inc(ctx.h["pe"], 1)

                            @blk.scalar
                            def _(s, qt=qt, kpn=kpn, pe0=pe0, act0=act0, dve0=dve0):
                                col0 = job2_col(qt, 0)
                                for kp in range(kpn):
                                    ps_s = ps_s0 if kp % 2 == 0 else ps_s1
                                    s.wait_ge(ctx.h["pe"], pe0 + 2 * kp + 1)
                                    if kp >= 1:
                                        s.wait_ge(ctx.h["dve"], dve0 + kp)
                                    s.activation(expp[:, :], ps_s[:, :], AF.Exp,
                                                 bias=c_bias[:, col0 + kp:col0 + kp + 1],
                                                 scale=c_scl[:, 0:1])
                                    s.drain().then_inc(ctx.h["act"], 1)
                                s.wait_ge(ctx.h["pe"], pe0 + 2 * kpn)
                                s.copy(ctmp[:, :], ps_o[:, :])
                                s.drain().then_inc(ctx.h["act"], 1)

                            @blk.vector
                            def _(v, qt=qt, hh=hh, kpn=kpn, pe0=pe0, act0=act0, dve0=dve0):
                                col0 = job2_col(qt, 0)
                                v.memset(lpart[:, :], 0.0)
                                v.drain()

                                for kp in range(kpn):
                                    v.wait_ge(ctx.h["act"], act0 + kp + 1)
                                    if kp < 2:
                                        if kp >= 1:
                                            v.wait_ge(ctx.h["pe"], pe0 + 2 * kp)
                                        v.tensor_tensor(exppm[:, :], expp[:, :],
                                                        c_masks[:, kp * 1024:(kp + 1) * 1024],
                                                        ALU.mult)
                                        v.tensor_add(lpart[:, :], lpart[:, :], exppm[:, 0:512])
                                        v.tensor_add(lpart[:, :], lpart[:, :], exppm[:, 512:1024])
                                    else:
                                        v.tensor_add(lpart[:, :], lpart[:, :], expp[:, 0:512])
                                        v.tensor_add(lpart[:, :], lpart[:, :], expp[:, 512:1024])
                                    v.drain().then_inc(ctx.h["dve"], 1)
                                v.wait_ge(ctx.h["pe"], pe0 + 2 * kpn + 1)
                                v.reciprocal(linv[:, :], ps_l[:, :])
                                v.drain().then_inc(ctx.h["dve"], 1)
                                v.wait_ge(ctx.h["pe"], pe0 + 2 * kpn + 2)
                                v.wait_ge(ctx.h["act"], act0 + kpn + 1)
                                v.tensor_mul(qctx[:, hh * 512:(hh + 1) * 512],
                                             ctmp[:, :], ps_a[:, :])
                                v.drain().then_inc(ctx.h["dve"], 1)

                        ctx.n["pe"] = pe0 + 2 * kpn + 2
                        ctx.n["act"] = act0 + kpn + 1
                        ctx.n["dve"] = dve0 + kpn + 2
                        if hh < 15:
                            ctx.n["ds"] += 32
                            ctx.n["dk"] += 32

                hd_stack.close()
                tp_stack = contextlib.ExitStack()
                ps_tp = tp_stack.enter_context(nc.psum_tensor(pfx + "ps_tp_%d" % (qt + 1), [128, 512], F32))
                ps_tp2 = tp_stack.enter_context(nc.psum_tensor(pfx + "ps_tp2_%d" % (qt + 1), [128, 512], F32))
                emit_proj(ctx, nc, wpool, WQO, qctx, yfm, 16, ps_a, ps_b,
                          bias_sb=c_bo, act_func=AF.Identity,
                          pre_waits=[("dg", dg_wo)])

                if DEBUG_B0 and qt == 0:
                    for part in range(8):
                        with nc.Block() as blk:
                            @blk.vector
                            def _(v, part=part):
                                v.tensor_copy(xraw[:, 0:1024], qctx[:, part * 1024:(part + 1) * 1024])
                            @blk.scalar
                            def _(s2):
                                pass
                        with nc.Block() as blk:
                            @blk.sync
                            def _(sy, part=part):
                                sy.dma_start(dbg_ctx.ap()[:, part * 1024:(part + 1) * 1024],
                                             xraw[:, 0:1024]).then_inc(ctx.h["st"], 16)
                                ctx.n["st"] += 16
                                sy.wait_ge(ctx.h["st"], ctx.n["st"])
                    for part in range(8):
                        with nc.Block() as blk:
                            @blk.vector
                            def _(v, part=part):
                                v.tensor_copy(xraw[:, 0:1024], yfm[:, part * 1024:(part + 1) * 1024])
                            @blk.scalar
                            def _(s2):
                                pass
                        with nc.Block() as blk:
                            @blk.sync
                            def _(sy, part=part):
                                sy.dma_start(dbg_y.ap()[:, part * 1024:(part + 1) * 1024],
                                             xraw[:, 0:1024]).then_inc(ctx.h["st"], 16)
                                ctx.n["st"] += 16
                                sy.wait_ge(ctx.h["st"], ctx.n["st"])

                for tt in range(4):
                    pairs, evacs = [], []
                    for g in range(4):
                        ps = ps_tp if g % 2 == 0 else ps_tp2
                        grp = []
                        for k4 in range(4):
                            dt = g * 4 + k4
                            grp.append((ps[:, k4 * 128:(k4 + 1) * 128],
                                        yfm[:, dt * 512 + tt * 128: dt * 512 + (tt + 1) * 128]))
                        pairs.append(grp)
                        evacs.append((ytm[:, g * 512:(g + 1) * 512], ps[:, :]))
                    emit_transpose(ctx, nc, pairs, c_ident[:, :], evacs)
                    ds_xr = ctx.n["ds"] + 16

                    dve0e, act0e = ctx.n["dve"], ctx.n["act"]
                    with nc.Block() as blk:
                        @blk.vector
                        def _(v, qt=qt, tt=tt, dve0e=dve0e, act0e=act0e, ds_xr=ds_xr):
                            ln_vec_emit(ctx, v, ytm[:, 0:2048], 1, stats, aggr, rstd,
                                        dve0e, act0e, out_ap=lny[:, 0:2048],
                                        scratch32=ctmp)
                            v.wait_ge(ctx.h["ds"], ds_xr)
                            v.tensor_add(xraw[:, :], xraw[:, :], lny[:, 0:2048])
                            v.drain().then_inc(ctx.h["dve"], 1)

                        @blk.scalar
                        def _(s, dve0e=dve0e, act0e=act0e):
                            ln_scl_emit(ctx, s, 1, aggr, rstd, eps_col, dve0e, act0e)

                        @blk.sync
                        def _(sy, qt=qt, tt=tt, dve0e=dve0e):
                            sy.dma_start(xraw[:, :],
                                         xown[(qt * 4 + tt) * 128:(qt * 4 + tt + 1) * 128, :]
                                         ).then_inc(ctx.h["ds"], 16)
                            sy.wait_ge(ctx.h["dve"], dve0e + 3)
                            sy.dma_start(resid1.ap()[(qt * 4 + tt) * 128:(qt * 4 + tt + 1) * 128, :],
                                         xraw[:, :]).then_inc(ctx.h["st"], 16)
                            sy.wait_ge(ctx.h["st"], ctx.n["st"] + 16)
                    ctx.n["ds"] = ds_xr
                    ctx.n["dve"] = dve0e + 3
                    ctx.n["act"] = act0e + 1
                    ctx.n["st"] += 16

        tp_stack.close()
        with nc.Block() as blk:
            @blk.sync
            def _(sy):
                sy.wait_ge(ctx.h["st"], ctx.n["st"])
        nc.leave_named_scope("attn", _sid_at, False)

    stack.close()
    if own:
        ctx.close()
    return nc



def _prep_b0_inmaps(inputs):
    x = np.asarray(inputs["L0_attn_x"], dtype=np.float32)
    wq = np.ascontiguousarray(np.asarray(inputs["L0_attn_wq"], dtype=np.float32))
    wk = np.ascontiguousarray(np.asarray(inputs["L0_attn_wk"], dtype=np.float32))
    wv = np.ascontiguousarray(np.asarray(inputs["L0_attn_wv"], dtype=np.float32))
    wo = np.ascontiguousarray(np.asarray(inputs["L0_attn_wo"], dtype=np.float32))
    bo = np.asarray(inputs["L0_attn_bo"], dtype=np.float32)
    bo_t = np.ascontiguousarray(bo.reshape(16, 128).T).astype(np.float32)
    ident = np.eye(128, dtype=BF)
    masks = host_causal_masks_paired()
    scl = host_scl()
    ones32 = np.ones((128, 1), dtype=np.float32)
    onesr32 = np.ones((1, 128), dtype=np.float32)

    in_maps = []
    for c in range(NCORE):
        b = c // 4
        blkA, blkB = core_blocks(c)
        xb = np.ascontiguousarray(x[b])
        xo = np.ascontiguousarray(np.concatenate(
            [xb[1024 * blkA:1024 * blkA + 1024], xb[1024 * blkB:1024 * blkB + 1024]], axis=0))
        bias_tbl, kdiag = host_b0_tables(c)
        in_maps.append({
            "xkv": xb, "xown": xo,
            "wq": wq, "wk": wk, "wv": wv, "wo": wo, "bo_t": bo_t,
            "ident": ident, "masks": masks,
            "bias_tbl": bias_tbl, "kdiag": kdiag,
            "scl": scl, "ones32": ones32, "onesr32": onesr32,
        })
    return in_maps


def run_b0(inputs):
    nc = build_b0()
    in_maps = _prep_b0_inmaps(inputs)
    res = run_bass_kernel_spmd(nc, in_maps, core_ids=list(range(8)))
    if DEBUG_B0:
        return res.results
    return [np.asarray(r["resid1"]) for r in res.results]


def assemble(resids):
    out = np.zeros((B, S, D), dtype=np.float32)
    for c in range(NCORE):
        b = c // 4
        blkA, blkB = core_blocks(c)
        out[b, 1024 * blkA:1024 * blkA + 1024] = resids[c][0:1024]
        out[b, 1024 * blkB:1024 * blkB + 1024] = resids[c][1024:2048]
    return out


def run_b1(inputs, resids1):
    nc = build_b1()
    in_maps = _prep_b1_inmaps(inputs, resids1)
    res = run_bass_kernel_spmd(nc, in_maps, core_ids=list(range(8)))
    return [np.asarray(r["resid_o"]) for r in res.results]


N_LAUNCH = 1


def run_fused(inputs):
    nc = build_fused()
    in_maps = _prep_fused_inmaps(inputs)
    res = run_bass_kernel_spmd(nc, in_maps, core_ids=list(range(8)))
    return [np.asarray(r["resid_o"]) for r in res.results]


def kernel(**inputs):
    if N_LAUNCH == 1:
        return assemble(run_fused(inputs))
    resids = run_b0(inputs)
    if N_LAUNCH >= 2:
        resids = run_b1(inputs, resids)
    if N_LAUNCH >= 3:
        resids = run_b2(inputs, resids)
    if N_LAUNCH >= 4:
        resids = run_b3(inputs, resids)
    return assemble(resids)



B1CUT = 99
B1_ACT = AF.Gelu
B1_HN = 99
B1_EVAC = 1



def _b1_finish(nc, ctx, stack):
    with nc.Block() as blk:
        @blk.sync
        def _(sy):
            sy.wait_ge(ctx.h["st"], ctx.n["st"])
    stack.close()
    if own:
        ctx.close()
    return nc


def _cut(n):
    return B1CUT < n


def build_b1(nc=None, ctx=None, pfx="", io=None):
    own = nc is None
    if own:
        nc = bass.Bass("TRN2", target_bir_lowering=False)
        ctx = Ctx(nc)

    resid_i = _io(io, "resid", lambda: nc.dram_tensor("resid", [T, D], F32, kind="ExternalInput"))
    gw_i = nc.dram_tensor("gate_w", [6, D], F32, kind="ExternalInput")
    w1_i = nc.dram_tensor("w1", [6 * 2048, 1408], F32, kind="ExternalInput")
    b1c_i = nc.dram_tensor("b1c", [128, 66], F32, kind="ExternalInput")
    w2_i = nc.dram_tensor("w2", [6 * 1408, 2048], F32, kind="ExternalInput")
    b2c_i = nc.dram_tensor("b2c", [128, 96], F32, kind="ExternalInput")
    ident_i = _io(io, "ident", lambda: nc.dram_tensor("ident", [128, 128], BF16, kind="ExternalInput"))
    ident32_i = nc.dram_tensor("ident32", [128, 128], F32, kind="ExternalInput")
    scl_i = _io(io, "scl", lambda: nc.dram_tensor("scl", [128, 8], F32, kind="ExternalInput"))
    onesr_i = _io(io, "onesr32", lambda: nc.dram_tensor("onesr32", [1, 128], F32, kind="ExternalInput"))
    iota_i = nc.dram_tensor("iota", [128, 1], F32, kind="ExternalInput")
    bsel_i = nc.dram_tensor("bsel", [128, 2], F32, kind="ExternalInput")

    cc_in = nc.dram_tensor(pfx + "cc_in", [128, 32], F32)
    cc_out = nc.dram_tensor(pfx + "cc_out", [128, 32], F32)
    y0sp = nc.dram_tensor(pfx + "y0sp", [64 * 128, 512], BF16)

    resid_o = _io(io, "resid_out", lambda: nc.dram_tensor("resid_o", [T, D], F32, kind="ExternalOutput"))

    stack = contextlib.ExitStack()

    def sb(name, shape, dt):
        return stack.enter_context(nc.sbuf_tensor(pfx + name, shape, dt))

    c_ident = sb("c_ident", [128, 128], BF16)
    c_id32 = sb("c_id32", [128, 128], F32)
    c_scl = sb("c_scl", [128, 8], F32)
    c_onesr = sb("c_onesr", [1, 128], F32)
    c_iota = sb("c_iota", [128, 1], F32)
    c_bsel = sb("c_bsel", [128, 2], F32)
    xfm = sb("xfm", [128, 16 * 2048], BF16)
    wbuf = sb("wbuf", [128, 16 * 1408], BF16)
    hbuf = sb("hbuf", [128, 11 * 2048], BF16)
    rbuf = sb("rbuf", [128, 2 * 2048], F32)
    tbuf = sb("tbuf", [128, 2 * 2048], BF16)
    gbuf = sb("gbuf", [128, 2048], F32)
    yfm = sb("yfm", [128, 16 * 512], BF16)
    y0ld = xfm[:, 0:16 * 512]
    ytm = tbuf[:, 0:2048]
    lny = tbuf[:, 2048:4096]
    xraw = rbuf[:, 0:2048]
    ctmp = sb("ctmp", [128, 512], F32)
    stats = sb("stats", [128, 96], F32)
    aggr = sb("aggr", [128, 8], F32)
    rstd = sb("rstd", [128, 4], F32)
    xbar = sb("xbar", [128, 48], F32)
    gwT = sb("gwT", [128, 96], F32)
    gsm = sb("gsm", [1, 96], F32)
    b1all = sb("b1all", [128, 66], F32)
    b2all = sb("b2all", [128, 96], F32)
    indc = sb("indc", [128, 16], F32)
    gidx1 = sb("gidx1", [1, 8], mybir.dt.uint32)
    gidx2 = sb("gidx2", [1, 8], mybir.dt.uint32)
    idxt = sb("idxt", [128, 2], I32)
    brow = gbuf[0:1, 0:2048]
    bcol = sb("bcol", [128, 40], F32)
    ps_a = stack.enter_context(nc.psum_tensor(pfx + "ps_a", [128, 512], F32))
    ps_b = stack.enter_context(nc.psum_tensor(pfx + "ps_b", [128, 512], F32))
    ps_tp = stack.enter_context(nc.psum_tensor(pfx + "ps_tp", [128, 512], F32))
    ps_tp2 = stack.enter_context(nc.psum_tensor(pfx + "ps_tp2", [128, 512], F32))
    ps_s = stack.enter_context(nc.psum_tensor(pfx + "ps_s", [128, 16], F32))
    ps_g = stack.enter_context(nc.psum_tensor(pfx + "ps_g", [1, 16], F32))
    eps_col = c_scl[:, 1:2]

    with nc.Block() as blk:
        @blk.sync
        def _(sy):
            for dst, src in [(c_ident, ident_i), (c_id32, ident32_i),
                             (c_scl, scl_i), (c_onesr, onesr_i),
                             (c_iota, iota_i), (c_bsel, bsel_i),
                             (b1all, b1c_i), (b2all, b2c_i)]:
                sy.dma_start(dst[:, :], src.ap()).then_inc(ctx.h["ds"], 16)
            ctx.n["ds"] += 8 * 16
            sy.wait_ge(ctx.h["ds"], ctx.n["ds"])

    if _cut(1):
        return _b1_finish(nc, ctx, stack)
    emit_xln_phase(ctx, nc, resid_i, 0, T,
                   lambda g, dt: xfm[:, dt * 2048 + g * 256: dt * 2048 + (g + 1) * 256],
                   rbuf, tbuf, stats, aggr, rstd, eps_col, c_ident[:, :],
                   ps_tp, ps_tp2, ntt=2)

    if _cut(2):
        return _b1_finish(nc, ctx, stack)
    with nc.Block() as blk:
        @blk.vector
        def _(v):
            for dm in range(16):
                v.tensor_reduce(xbar[:, dm:dm + 1], xfm[:, dm * 2048:(dm + 1) * 2048],
                                mybir.AxisListType.X, ALU.add)
                v.drain()
            v.tensor_scalar(xbar[:, 32:48], xbar[:, 0:16], c_bsel[:, 1:2], None, ALU.mult)
            v.drain()
            v.tensor_scalar(xbar[:, 16:32], xbar[:, 0:16], c_bsel[:, 0:1], None, ALU.mult)
            v.drain().then_inc(ctx.h["dve"], 1)
            ctx.n["dve"] += 1

        @blk.scalar
        def _(s):
            pass

    if _cut(3):
        return _b1_finish(nc, ctx, stack)
    with nc.Block() as blk:
        @blk.gpsimd
        def _(g):
            g.wait_ge(ctx.h["dve"], ctx.n["dve"])
            g.dma_start(cc_in[:, :], xbar[:, 16:48]).then_inc(ctx.h["dg"], 16)
            ctx.n["dg"] += 16
            g.wait_ge(ctx.h["dg"], ctx.n["dg"])
            g.collective_compute(
                "AllReduce", ALU.add, replica_groups=[list(range(8))],
                ins=[cc_in.ap().opt()], outs=[cc_out.ap().opt()],
            ).then_inc(ctx.h["cc"], 1)
            ctx.n["cc"] += 1
            g.wait_ge(ctx.h["cc"], ctx.n["cc"])
            g.dma_start(xbar[:, 16:48], cc_out[:, :]).then_inc(ctx.h["dg"], 16)
            ctx.n["dg"] += 16
            g.wait_ge(ctx.h["dg"], ctx.n["dg"])

    if _cut(4):
        return _b1_finish(nc, ctx, stack)
    with nc.Block() as blk:
        @blk.vector
        def _(v):
            v.tensor_scalar(xbar[:, 0:16], xbar[:, 16:32], c_bsel[:, 0:1], None, ALU.mult)
            v.drain()
            v.scalar_tensor_tensor(xbar[:, 0:16], xbar[:, 32:48], c_bsel[:, 1:2],
                                   xbar[:, 0:16], ALU.mult, ALU.add)
            v.drain()
            v.tensor_scalar(xbar[:, 0:16], xbar[:, 0:16], c_scl[:, 2:3], None, ALU.mult)
            v.drain().then_inc(ctx.h["dve"], 1)
            ctx.n["dve"] += 1

        @blk.scalar
        def _(s):
            pass

    if _cut(5):
        return _b1_finish(nc, ctx, stack)
    with nc.Block() as blk:
        @blk.sync
        def _(sy):
            sy.dma_start(gbuf[0:6, 0:2048], gw_i[:, :]).then_inc(ctx.h["ds"], 16)
            ctx.n["ds"] += 16
            sy.wait_ge(ctx.h["ds"], ctx.n["ds"])

    pairs = []
    evacs = []
    for dm in range(16):
        ps = ps_tp if dm % 2 == 0 else ps_tp2
        pairs.append([(ps[:, 0:6], gbuf[0:6, dm * 128:(dm + 1) * 128])])
        evacs.append((gwT[:, dm * 6:(dm + 1) * 6], ps[:, 0:6]))
    emit_transpose(ctx, nc, pairs, c_id32[0:6, 0:6], evacs)

    pe0 = ctx.n["pe"]
    with nc.Block() as blk:
        @blk.tensor
        def _(t):
            t.wait_ge(ctx.h["dve"], ctx.n["dve"])
            for dm in range(16):
                t.matmul(ps_g[:, 0:6], xbar[:, dm:dm + 1], gwT[:, dm * 6:(dm + 1) * 6],
                         start=(dm == 0), stop=(dm == 15))
            t.drain().then_inc(ctx.h["pe"], 1)
    ctx.n["pe"] = pe0 + 1

    if _cut(7):
        return _b1_finish(nc, ctx, stack)
    dve0, act0 = ctx.n["dve"], ctx.n["act"]
    with nc.Block() as blk:
        @blk.vector
        def _(v, dve0=dve0, act0=act0):
            v.memset(gsm[:, 0:8], NEG_BIG)
            v.drain()
            v.wait_ge(ctx.h["pe"], ctx.n["pe"])
            v.tensor_copy(gsm[:, 0:6], ps_g[:, 0:6])
            v.drain()
            v.reduce_max(gsm[:, 8:9], gsm[:, 0:8], mybir.AxisListType.X)
            v.drain()
            v.tensor_scalar(gsm[:, 16:24], gsm[:, 0:8], gsm[:, 8:9], None, ALU.subtract)
            v.drain().then_inc(ctx.h["dve"], 1)
            v.wait_ge(ctx.h["act"], act0 + 1)
            v.reduce_sum(gsm[:, 9:10], gsm[:, 24:32], mybir.AxisListType.X)
            v.drain()
            v.reciprocal(gsm[:, 10:11], gsm[:, 9:10])
            v.drain()
            v.tensor_scalar(gsm[:, 24:32], gsm[:, 24:32], gsm[:, 10:11], None, ALU.mult)
            v.drain()
            v.reduce_max(gsm[:, 32:33], gsm[:, 24:32], mybir.AxisListType.X)
            v.drain()
            v.tensor_copy(gsm[:, 33:40], gsm[:, 32:33].broadcast_to((1, 7)))
            v.drain()
            v.max_index(gidx1[:, :], gsm[:, 32:40], gsm[:, 24:32])
            v.drain()
            v.match_replace(gsm[:, 40:48], gsm[:, 32:40], gsm[:, 24:32], -1e30)
            v.drain()
            v.reduce_max(gsm[:, 48:49], gsm[:, 40:48], mybir.AxisListType.X)
            v.drain()
            v.tensor_copy(gsm[:, 49:56], gsm[:, 48:49].broadcast_to((1, 7)))
            v.drain()
            v.max_index(gidx2[:, :], gsm[:, 48:56], gsm[:, 40:48])
            v.drain()
            v.tensor_scalar(gsm[:, 56:57], gsm[:, 32:33], gsm[:, 48:49], None, ALU.add)
            v.drain()
            v.reciprocal(gsm[:, 57:58], gsm[:, 56:57])
            v.drain()
            v.tensor_scalar(gsm[:, 58:59], gsm[:, 32:33], gsm[:, 57:58], None, ALU.mult)
            v.drain()
            v.tensor_scalar(gsm[:, 59:60], gsm[:, 48:49], gsm[:, 57:58], None, ALU.mult)
            v.drain()
            v.tensor_scalar(gsm[:, 64:72], gsm[:, 24:32], gsm[:, 32:33], None, ALU.is_ge)
            v.drain()
            v.tensor_scalar(gsm[:, 72:80], gsm[:, 40:48], gsm[:, 48:49], None, ALU.is_ge)
            v.drain().then_inc(ctx.h["dve"], 1)

        @blk.scalar
        def _(s, dve0=dve0, act0=act0):
            s.wait_ge(ctx.h["dve"], dve0 + 1)
            s.activation(gsm[:, 24:32], gsm[:, 16:24], AF.Exp)
            s.drain().then_inc(ctx.h["act"], 1)
    ctx.n["dve"] = dve0 + 2
    ctx.n["act"] = act0 + 1


    if _cut(8):
        return _b1_finish(nc, ctx, stack)
    pe0 = ctx.n["pe"]
    act0 = ctx.n["act"]
    dve0 = ctx.n["dve"]
    with nc.Block() as blk:
        @blk.tensor
        def _(t, pe0=pe0):
            for e in range(6):
                t.matmul(ps_s[:, e:e + 1], c_onesr[:, :], gsm[:, 64 + e:65 + e],
                         start=True, stop=True)
                t.matmul(ps_s[:, 6 + e:7 + e], c_onesr[:, :], gsm[:, 72 + e:73 + e],
                         start=True, stop=True)
            t.matmul(ps_s[:, 12:13], c_onesr[:, :], gsm[:, 58:59], start=True, stop=True)
            t.matmul(ps_s[:, 13:14], c_onesr[:, :], gsm[:, 59:60], start=True, stop=True)
            t.drain().then_inc(ctx.h["pe"], 1)

        @blk.scalar
        def _(s, pe0=pe0):
            s.wait_ge(ctx.h["pe"], pe0 + 1)
            s.copy(indc[:, 0:14], ps_s[:, 0:14])
            s.drain().then_inc(ctx.h["act"], 1)
    ctx.n["pe"] = pe0 + 1
    ctx.n["act"] = act0 + 1

    with nc.Block() as blk:
        @blk.vector
        def _(v, dve0=dve0):
            v.tensor_scalar(bcol[:, 34:40], indc[:, 0:6], indc[:, 12:13], None, ALU.mult)
            v.drain()
            v.scalar_tensor_tensor(bcol[:, 34:40], indc[:, 6:12], indc[:, 13:14],
                                   bcol[:, 34:40], ALU.mult, ALU.add)
            v.drain()
            v.memset(bcol[:, 16:32], 0.0)
            v.drain()
            for e in range(6):
                v.scalar_tensor_tensor(bcol[:, 16:32], b2all[:, e * 16:(e + 1) * 16],
                                       bcol[:, 34 + e:35 + e], bcol[:, 16:32],
                                       ALU.mult, ALU.add)
                v.drain()
            v.drain().then_inc(ctx.h["dve"], 1)

        @blk.scalar
        def _(s):
            pass
    ctx.n["dve"] = dve0 + 1

    if _cut(10):
        return _b1_finish(nc, ctx, stack)
    for sidx, gidx in [(0, gidx1), (1, gidx2)]:
        pe0 = ctx.n["pe"]
        dve0 = ctx.n["dve"]
        with nc.Block() as blk:
            @blk.vector
            def _(v, sidx=sidx, gidx=gidx, pe0=pe0):
                v.tensor_copy(gsm[:, 60:61], gidx[:, 0:1])
                v.drain()
                v.tensor_scalar(gsm[:, 61:62], gsm[:, 60:61], 2048.0, None, ALU.mult)
                v.drain()
                v.tensor_scalar(gsm[:, 62:63], gsm[:, 60:61], 1408.0, None, ALU.mult)
                v.drain()
                v.drain().then_inc(ctx.h["dve"], 1)
                v.wait_ge(ctx.h["pe"], pe0 + 1)
                v.tensor_tensor(idxt[:, 0:1], ps_a[:, 0:1], c_iota[:, :], ALU.add)
                v.drain()
                v.tensor_tensor(idxt[:, 1:2], ps_b[:, 0:1], c_iota[:, :], ALU.add)
                v.drain().then_inc(ctx.h["dve"], 1)

            @blk.tensor
            def _(t, sidx=sidx, dve0=dve0):
                t.wait_ge(ctx.h["dve"], dve0 + 1)
                t.matmul(ps_a[:, 0:1], c_onesr[:, :], gsm[:, 61:62], start=True, stop=True)
                t.matmul(ps_b[:, 0:1], c_onesr[:, :], gsm[:, 62:63], start=True, stop=True)
                t.matmul(ps_s[:, 0:1], c_onesr[:, :], gsm[:, 58 + sidx:59 + sidx],
                         start=True, stop=True)
                t.drain().then_inc(ctx.h["pe"], 1)
        ctx.n["pe"] = pe0 + 1
        ctx.n["dve"] = dve0 + 2

        if _cut(11 + 10 * sidx):
            return _b1_finish(nc, ctx, stack)
        act0 = ctx.n["act"]
        dve0 = ctx.n["dve"]
        with nc.Block() as blk:
            @blk.scalar
            def _(s, sidx=sidx, act0=act0):
                s.copy(bcol[:, 32 + sidx:33 + sidx], ps_s[:, 0:1])
                s.drain().then_inc(ctx.h["act"], 1)

            @blk.vector
            def _(v, sidx=sidx, dve0=dve0):
                v.memset(bcol[:, 0:11], 0.0)
                v.drain()
                for e in range(6):
                    icol = indc[:, e:e + 1] if sidx == 0 else indc[:, 6 + e:7 + e]
                    v.scalar_tensor_tensor(bcol[:, 0:11], b1all[:, e * 11:(e + 1) * 11],
                                           icol, bcol[:, 0:11], ALU.mult, ALU.add)
                    v.drain()
                v.drain().then_inc(ctx.h["dve"], 1)
        ctx.n["act"] = act0 + 1
        ctx.n["dve"] = dve0 + 1

        if _cut(12 + 10 * sidx):
            return _b1_finish(nc, ctx, stack)
        for dm in range(16):
            dk0 = ctx.n["dk"]
            dve0 = ctx.n["dve"]
            with nc.Block() as blk:
                @blk.gpsimd
                def _(g, dm=dm, dk0=dk0):
                    g.indirect_dma_start(
                        gbuf[:, 0:1408], None, w1_i.ap(),
                        IndirectOffsetOnAxis(idxt[:, 0:1], 0),
                        element_offset=dm * 128 * 1408,
                    ).then_inc(ctx.h["dk"], 16)

                @blk.vector
                def _(v, dm=dm, dk0=dk0):
                    v.wait_ge(ctx.h["dk"], dk0 + 16)
                    v.tensor_copy(wbuf[:, dm * 1408:(dm + 1) * 1408], gbuf[:, 0:1408])
                    v.drain().then_inc(ctx.h["dve"], 1)
            ctx.n["dk"] = dk0 + 16
            ctx.n["dve"] = dve0 + 1

        if _cut(13 + 10 * sidx):
            return _b1_finish(nc, ctx, stack)
        pe0 = ctx.n["pe"]
        act0 = ctx.n["act"]
        with nc.Block() as blk:
            @blk.tensor
            def _(t, pe0=pe0, act0=act0):
                i = 0
                for ch in range(4):
                    for ft in range(11):
                        if i >= B1_HN:
                            break
                        ps = ps_a if i % 2 == 0 else ps_b
                        if i >= 2:
                            t.wait_ge(ctx.h["act"], act0 + i - 1)
                        for dm in range(16):
                            t.matmul(ps[:, :],
                                     wbuf[:, dm * 1408 + ft * 128: dm * 1408 + (ft + 1) * 128],
                                     xfm[:, dm * 2048 + ch * 512: dm * 2048 + (ch + 1) * 512],
                                     start=(dm == 0), stop=(dm == 15))
                        t.drain().then_inc(ctx.h["pe"], 1)
                        i += 1

            @blk.scalar
            def _(s, pe0=pe0, act0=act0):
                i = 0
                for ch in range(4):
                    for ft in range(11):
                        if i >= B1_HN:
                            break
                        ps = ps_a if i % 2 == 0 else ps_b
                        s.wait_ge(ctx.h["pe"], pe0 + i + 1)
                        if B1_EVAC == 1:
                            s.activation(hbuf[:, ft * 2048 + ch * 512: ft * 2048 + (ch + 1) * 512],
                                         ps[:, :], B1_ACT, bias=bcol[:, ft:ft + 1], scale=1.0)
                        elif B1_EVAC == 2:
                            s.copy(hbuf[:, ft * 2048 + ch * 512: ft * 2048 + (ch + 1) * 512],
                                   ps[:, :])
                        s.drain().then_inc(ctx.h["act"], 1)
                        i += 1
        nh = min(44, B1_HN)
        ctx.n["pe"] = pe0 + nh
        ctx.n["act"] = act0 + nh

        if _cut(14 + 10 * sidx):
            return _b1_finish(nc, ctx, stack)
        for ft in range(11):
            dk0 = ctx.n["dk"]
            dve0 = ctx.n["dve"]
            with nc.Block() as blk:
                @blk.gpsimd
                def _(g, ft=ft, dk0=dk0):
                    g.indirect_dma_start(
                        gbuf[:, 0:2048], None, w2_i.ap(),
                        IndirectOffsetOnAxis(idxt[:, 1:2], 0),
                        element_offset=ft * 128 * 2048,
                    ).then_inc(ctx.h["dk"], 16)

                @blk.vector
                def _(v, ft=ft, dk0=dk0):
                    v.wait_ge(ctx.h["dk"], dk0 + 16)
                    v.tensor_scalar(wbuf[:, ft * 2048:(ft + 1) * 2048], gbuf[:, 0:2048],
                                    bcol[:, 32 + sidx:33 + sidx], None, ALU.mult)
                    v.drain().then_inc(ctx.h["dve"], 1)
            ctx.n["dk"] = dk0 + 16
            ctx.n["dve"] = dve0 + 1

        if _cut(15 + 10 * sidx):
            return _b1_finish(nc, ctx, stack)
        if sidx == 0:
            for ch in range(4):
                pe0 = ctx.n["pe"]
                act0 = ctx.n["act"]
                st0 = ctx.n["st"]
                with nc.Block() as blk:
                    @blk.tensor
                    def _(t, ch=ch, pe0=pe0, act0=act0):
                        for u in range(16):
                            ps = ps_a if u % 2 == 0 else ps_b
                            if u >= 2:
                                t.wait_ge(ctx.h["act"], act0 + u - 1)
                            for ft in range(11):
                                t.matmul(ps[:, :],
                                         wbuf[:, ft * 2048 + u * 128: ft * 2048 + (u + 1) * 128],
                                         hbuf[:, ft * 2048 + ch * 512: ft * 2048 + (ch + 1) * 512],
                                         start=(ft == 0), stop=(ft == 10))
                            t.drain().then_inc(ctx.h["pe"], 1)

                    @blk.scalar
                    def _(s, ch=ch, pe0=pe0, act0=act0):
                        for u in range(16):
                            ps = ps_a if u % 2 == 0 else ps_b
                            s.wait_ge(ctx.h["pe"], pe0 + u + 1)
                            s.copy(yfm[:, u * 512:(u + 1) * 512], ps[:, :])
                            s.drain().then_inc(ctx.h["act"], 1)

                    @blk.sync
                    def _(sy, ch=ch, act0=act0, st0=st0):
                        sy.wait_ge(ctx.h["act"], act0 + 16)
                        dst = bass.AP(y0sp, ch * 16 * 128 * 512,
                                      [[512, 128], [128 * 512, 16], [1, 512]])
                        sy.dma_start(dst, yfm[:, :]).then_inc(ctx.h["st"], 16)
                        sy.wait_ge(ctx.h["st"], st0 + 16)
                ctx.n["pe"] = pe0 + 16
                ctx.n["act"] = act0 + 16
                ctx.n["st"] = st0 + 16
        else:
            for ch in range(4):
                ds0 = ctx.n["ds"]
                pe0 = ctx.n["pe"]
                dve0 = ctx.n["dve"]
                with nc.Block() as blk:
                    @blk.sync
                    def _(sy, ch=ch, ds0=ds0):
                        src = bass.AP(y0sp, ch * 16 * 128 * 512,
                                      [[512, 128], [128 * 512, 16], [1, 512]])
                        sy.dma_start(y0ld[:, :], src).then_inc(ctx.h["ds"], 16)

                    @blk.tensor
                    def _(t, ch=ch, pe0=pe0, dve0=dve0):
                        for u in range(16):
                            ps = ps_a if u % 2 == 0 else ps_b
                            if u >= 2:
                                t.wait_ge(ctx.h["dve"], dve0 + u - 1)
                            for ft in range(11):
                                t.matmul(ps[:, :],
                                         wbuf[:, ft * 2048 + u * 128: ft * 2048 + (u + 1) * 128],
                                         hbuf[:, ft * 2048 + ch * 512: ft * 2048 + (ch + 1) * 512],
                                         start=(ft == 0), stop=(ft == 10))
                            t.drain().then_inc(ctx.h["pe"], 1)

                    @blk.vector
                    def _(v, ch=ch, pe0=pe0, dve0=dve0, ds0=ds0):
                        v.wait_ge(ctx.h["ds"], ds0 + 16)
                        for u in range(16):
                            ps = ps_a if u % 2 == 0 else ps_b
                            v.wait_ge(ctx.h["pe"], pe0 + u + 1)
                            v.scalar_tensor_tensor(ctmp[:, :], ps[:, :], 1.0,
                                                   y0ld[:, u * 512:(u + 1) * 512],
                                                   ALU.mult, ALU.add)
                            v.drain()
                            v.tensor_scalar(yfm[:, u * 512:(u + 1) * 512], ctmp[:, :],
                                            bcol[:, 16 + u:17 + u], None, ALU.add)
                            v.drain().then_inc(ctx.h["dve"], 1)

                    @blk.scalar
                    def _(s):
                        pass
                ctx.n["ds"] = ds0 + 16
                ctx.n["pe"] = pe0 + 16
                ctx.n["dve"] = dve0 + 16

                emit_epilogue_chunk(ctx, nc, yfm, resid_i, resid_o, ch * 512,
                                    xraw, ytm, lny, stats, aggr, rstd, eps_col,
                                    ctmp, ps_tp, ps_tp2, c_ident[:, :])

    with nc.Block() as blk:
        @blk.sync
        def _(sy):
            sy.wait_ge(ctx.h["st"], ctx.n["st"])

    stack.close()
    if own:
        ctx.close()
    return nc


def _prep_b1_inmaps(inputs, resids1):
    gw = np.ascontiguousarray(np.asarray(inputs["L0_ffn_gate_W"], dtype=np.float32))
    w1 = np.ascontiguousarray(np.asarray(inputs["L0_ffn_W1"], dtype=np.float32).reshape(6 * 2048, 1408))
    b1 = np.asarray(inputs["L0_ffn_b1"], dtype=np.float32)
    b1c = np.ascontiguousarray(b1.reshape(6, 11, 128).transpose(2, 0, 1).reshape(128, 66))
    w2 = np.ascontiguousarray(np.asarray(inputs["L0_ffn_W2"], dtype=np.float32).reshape(6 * 1408, 2048))
    b2 = np.asarray(inputs["L0_ffn_b2"], dtype=np.float32)
    b2c = np.ascontiguousarray(b2.reshape(6, 16, 128).transpose(2, 0, 1).reshape(128, 96))
    ident = np.eye(128, dtype=BF)
    ident32 = np.eye(128, dtype=np.float32)
    scl = host_scl()
    onesr32 = np.ones((1, 128), dtype=np.float32)
    iota = np.arange(128, dtype=np.float32).reshape(128, 1)
    in_maps = []
    for c in range(NCORE):
        b = c // 4
        bsel = np.zeros((128, 2), dtype=np.float32)
        bsel[:, b] = 1.0
        m = {
            "gate_w": gw, "w1": w1, "b1c": b1c, "w2": w2, "b2c": b2c,
            "ident": ident, "ident32": ident32, "scl": scl,
            "onesr32": onesr32, "iota": iota, "bsel": bsel,
        }
        if resids1[c] is not None:
            m["resid"] = np.ascontiguousarray(resids1[c])
        in_maps.append(m)
    return in_maps



def build_b2(nc=None, ctx=None, pfx="", io=None):
    own = nc is None
    if own:
        nc = bass.Bass("TRN2", target_bir_lowering=False)
        ctx = Ctx(nc)

    resid_i = _io(io, "resid", lambda: nc.dram_tensor("resid", [T, D], F32, kind="ExternalInput"))
    gb_ap = io.get("gb") if io is not None else None
    bidx_i = io.get("bidx") if io is not None else None
    if gb_ap is None:
        xbnd_i = nc.dram_tensor("xbnd", [1024, D], F32, kind="ExternalInput")
    else:
        xbnd_i = None
    wqkv_i = nc.dram_tensor("w_qkv", [3 * D, D], F32, kind="ExternalInput")
    wout_i = nc.dram_tensor("w_out", [D, D], F32, kind="ExternalInput")
    ident_i = _io(io, "ident", lambda: nc.dram_tensor("ident", [128, 128], BF16, kind="ExternalInput"))
    masks_i = _io(io, "masks2", lambda: nc.dram_tensor("masks", [128, 8 * 512], BF16, kind="ExternalInput"))
    bias_i = nc.dram_tensor("bias16", [128, 16], F32, kind="ExternalInput")
    scl_i = _io(io, "scl", lambda: nc.dram_tensor("scl", [128, 8], F32, kind="ExternalInput"))
    ones_i = _io(io, "ones32", lambda: nc.dram_tensor("ones32", [128, 1], F32, kind="ExternalInput"))
    onesr_i = _io(io, "onesr32", lambda: nc.dram_tensor("onesr32", [1, 128], F32, kind="ExternalInput"))

    resid_o = _io(io, "resid_out", lambda: nc.dram_tensor("resid_o", [T, D], F32, kind="ExternalOutput"))

    stack = contextlib.ExitStack()

    def sb(name, shape, dt):
        return stack.enter_context(nc.sbuf_tensor(pfx + name, shape, dt))

    c_ident = sb("c_ident", [128, 128], BF16)
    c_masks = sb("c_masks", [128, 8 * 512], BF16)
    c_bias = sb("c_bias", [128, 16], F32)
    c_scl = sb("c_scl", [128, 8], F32)
    c_ones = sb("c_ones", [128, 1], F32)
    c_onesr = sb("c_onesr", [1, 128], F32)
    c_bidx = sb("c_bidx", [128, 8], I32)
    xkv = sb("xkv", [128, 16 * 1536], BF16)
    qb = sb("qb", [128, 16 * 1024], BF16)
    kb = sb("kb", [128, 16 * 1536], BF16)
    vb = sb("vb", [128, 12 * 2048], BF16)
    rbuf = sb("rbuf", [128, 2048], F32)
    tbuf = sb("tbuf", [128, 4096], BF16)
    lpart = sb("lpart", [128, 512], F32)
    linv = lpart[0:1, :]
    ctmp = sb("ctmp", [128, 512], F32)
    stats = sb("stats", [128, 96], F32)
    aggr = sb("aggr", [128, 8], F32)
    rstd = sb("rstd", [128, 4], F32)
    yfm = xkv[:, 0:8192]
    expp = xkv[:, 8192:9216]
    exppm = xkv[:, 9216:10240]
    wqb = tbuf[:, 0:2048]
    wT = tbuf[:, 2048:4096]
    ytm = tbuf[:, 0:2048]
    lny = tbuf[:, 2048:4096]
    xraw = rbuf[:, 0:2048]
    ps_a = stack.enter_context(nc.psum_tensor(pfx + "ps_a", [128, 512], F32))
    ps_b = stack.enter_context(nc.psum_tensor(pfx + "ps_b", [128, 512], F32))
    eps_col = c_scl[:, 1:2]

    with nc.Block() as blk:
        @blk.sync
        def _(sy):
            consts = [(c_ident, ident_i), (c_masks, masks_i),
                      (c_bias, bias_i), (c_scl, scl_i),
                      (c_ones, ones_i), (c_onesr, onesr_i)]
            if bidx_i is not None:
                consts.append((c_bidx, bidx_i))
            for dst, src in consts:
                sy.dma_start(dst[:, :], src.ap()).then_inc(ctx.h["ds"], 16)
            ctx.n["ds"] += len(consts) * 16
            sy.wait_ge(ctx.h["ds"], ctx.n["ds"])

    for run in range(2):
        _sid_pr, _ = nc.enter_named_scope("prj%d" % run, False)
        tp_stack = contextlib.ExitStack()
        ps_tp = tp_stack.enter_context(nc.psum_tensor(pfx + "b2tp_%d" % (run * 10), [128, 512], F32))
        ps_tp2 = tp_stack.enter_context(nc.psum_tensor(pfx + "b2tp2_%d" % (run * 10), [128, 512], F32))

        for g in range(12):
            if gb_ap is not None and g < 4:
                dk0 = ctx.n["dk"] + 16
                with nc.Block() as blk:
                    @blk.gpsimd
                    def _(gp, run=run, g=g):
                        gp.indirect_dma_start(
                            rbuf[:, :], None, gb_ap.ap(),
                            IndirectOffsetOnAxis(c_bidx[:, run * 4 + g: run * 4 + g + 1], 0),
                        ).then_inc(ctx.h["dk"], 16)
                ctx.n["dk"] = dk0
                ld_wait = ("dk", dk0)
            else:
                ds0 = ctx.n["ds"] + 16
                with nc.Block() as blk:
                    @blk.sync
                    def _(sy, run=run, g=g):
                        if g < 4:
                            src = xbnd_i[run * 512 + g * 128: run * 512 + (g + 1) * 128, :]
                        else:
                            r0 = run * 1024 + (g - 4) * 128
                            src = resid_i[r0:r0 + 128, :]
                        sy.dma_start(rbuf[:, :], src).then_inc(ctx.h["ds"], 16)
                ctx.n["ds"] = ds0
                ld_wait = ("ds", ds0)
            emit_ln_block(ctx, nc, rbuf, 1, stats, aggr, rstd, eps_col,
                          out_ap=tbuf[:, 0:2048], dg_wait=ld_wait)
            emit_tm2fm_gen(ctx, nc, tbuf[:, 0:2048],
                           lambda dt, g=g: xkv[:, dt * 1536 + g * 128: dt * 1536 + (g + 1) * 128],
                           c_ident[:, :], ps_tp, ps_tp2, ntt=1)

        for ft in range(48):
            dg0 = ctx.n["dg"] + 16
            with nc.Block() as blk:
                @blk.gpsimd
                def _(g, ft=ft):
                    g.dma_start(wqb[:, :], wqkv_i[ft * 128:(ft + 1) * 128, :]
                                ).then_inc(ctx.h["dg"], 16)
            ctx.n["dg"] = dg0

            pairs, evacs = [], []
            for j in range(4):
                ps = ps_tp if j % 2 == 0 else ps_tp2
                grp = [(ps[:, k4 * 128:(k4 + 1) * 128],
                        wqb[:, (4 * j + k4) * 128:(4 * j + k4 + 1) * 128])
                       for k4 in range(4)]
                pairs.append(grp)
                evacs.append((wT[:, j * 512:(j + 1) * 512], ps[:, :]))
            emit_transpose(ctx, nc, pairs, c_ident[:, :], evacs,
                           pre_waits=[("dg", dg0)])

            pe0, act0 = ctx.n["pe"], ctx.n["act"]
            with nc.Block() as blk:
                @blk.tensor
                def _(t, ft=ft, pe0=pe0, act0=act0):
                    if ft < 32:
                        chunks = [1, 2] if ft < 16 else [0, 1, 2]
                        for i, c in enumerate(chunks):
                            ps = ps_a if i % 2 == 0 else ps_b
                            if i >= 2:
                                t.wait_ge(ctx.h["act"], act0 + i - 1)
                            for dm in range(16):
                                t.matmul(ps[:, :], wT[:, dm * 128:(dm + 1) * 128],
                                         xkv[:, dm * 1536 + c * 512: dm * 1536 + (c + 1) * 512],
                                         start=(dm == 0), stop=(dm == 15))
                            t.drain().then_inc(ctx.h["pe"], 1)
                    else:
                        for kt in range(12):
                            ps = ps_a if kt % 2 == 0 else ps_b
                            if kt >= 2:
                                t.wait_ge(ctx.h["act"], act0 + kt - 1)
                            for dm in range(16):
                                t.matmul(ps[:, 0:128],
                                         xkv[:, dm * 1536 + kt * 128: dm * 1536 + (kt + 1) * 128],
                                         wT[:, dm * 128:(dm + 1) * 128],
                                         start=(dm == 0), stop=(dm == 15))
                            t.drain().then_inc(ctx.h["pe"], 1)

                @blk.scalar
                def _(s, ft=ft, pe0=pe0, act0=act0):
                    if ft < 32:
                        chunks = [1, 2] if ft < 16 else [0, 1, 2]
                        for i, c in enumerate(chunks):
                            ps = ps_a if i % 2 == 0 else ps_b
                            s.wait_ge(ctx.h["pe"], pe0 + i + 1)
                            if ft < 16:
                                dst = qb[:, ft * 1024 + (c - 1) * 512: ft * 1024 + c * 512]
                            else:
                                dst = kb[:, (ft - 16) * 1536 + c * 512: (ft - 16) * 1536 + (c + 1) * 512]
                            s.copy(dst, ps[:, :])
                            s.drain().then_inc(ctx.h["act"], 1)
                    else:
                        for kt in range(12):
                            ps = ps_a if kt % 2 == 0 else ps_b
                            s.wait_ge(ctx.h["pe"], pe0 + kt + 1)
                            s.copy(vb[:, kt * 2048 + (ft - 32) * 128: kt * 2048 + (ft - 31) * 128],
                                   ps[:, 0:128])
                            s.drain().then_inc(ctx.h["act"], 1)
            ngr = (2 if ft < 16 else 3) if ft < 32 else 12
            ctx.n["pe"] = pe0 + ngr
            ctx.n["act"] = act0 + ngr

        tp_stack.close()

        nc.leave_named_scope("prj%d" % run, _sid_pr, False)
        _sid_at2, _ = nc.enter_named_scope("att%d" % run, False)
        for pos in range(2):
            at_stack = contextlib.ExitStack()
            ps_s0 = at_stack.enter_context(nc.psum_tensor(pfx + "b2s0_%d" % (run * 10 + pos), [128, 1024], F32))
            ps_s1 = at_stack.enter_context(nc.psum_tensor(pfx + "b2s1_%d" % (run * 10 + pos), [128, 1024], F32))
            ps_o = at_stack.enter_context(nc.psum_tensor(pfx + "b2o_%d" % (run * 10 + pos), [128, 512], F32))
            ps_l = at_stack.enter_context(nc.psum_tensor(pfx + "b2l_%d" % (run * 10 + pos), [1, 512], F32))

            for hh in range(16):
                pe0, act0, dve0 = ctx.n["pe"], ctx.n["act"], ctx.n["dve"]
                kpn = 4
                with nc.Block() as blk:
                    @blk.tensor
                    def _(t, run=run, pos=pos, hh=hh, pe0=pe0, act0=act0, dve0=dve0):
                        for kp in range(kpn):
                            ps_s = ps_s0 if kp % 2 == 0 else ps_s1
                            if kp >= 2:
                                t.wait_ge(ctx.h["act"], act0 + kp - 1)
                            ins = None
                            for half in range(2):
                                kt = 2 * kp + half
                                ins = t.matmul(ps_s[:, half * 512:(half + 1) * 512],
                                               kb[:, hh * 1536 + pos * 512 + kt * 128:
                                                  hh * 1536 + pos * 512 + (kt + 1) * 128],
                                               qb[:, hh * 1024 + pos * 512:
                                                  hh * 1024 + (pos + 1) * 512],
                                               start=True, stop=True)
                            ins.then_inc(ctx.h["pe"], 1)
                            t.wait_ge(ctx.h["dve"], dve0 + kp + 1)
                            for half in range(2):
                                kt = 2 * kp + half
                                ins = t.matmul(ps_o[:, :],
                                               vb[:, (pos * 4 + kt) * 2048 + hh * 128:
                                                  (pos * 4 + kt) * 2048 + (hh + 1) * 128],
                                               exppm[:, half * 512:(half + 1) * 512],
                                               start=(kp == 0 and half == 0),
                                               stop=(kp == kpn - 1 and half == 1))
                            ins.then_inc(ctx.h["pe"], 1)
                        t.wait_ge(ctx.h["dve"], dve0 + kpn)
                        t.matmul(ps_l[:, :], c_ones[:, :], lpart[:, :],
                                 start=True, stop=True)
                        t.drain().then_inc(ctx.h["pe"], 1)
                        t.wait_ge(ctx.h["dve"], dve0 + kpn + 1)
                        t.matmul(ps_a[:, :], c_onesr[:, :], linv[:, :],
                                 start=True, stop=True)
                        t.drain().then_inc(ctx.h["pe"], 1)

                    @blk.scalar
                    def _(s, run=run, pos=pos, pe0=pe0, act0=act0, dve0=dve0):
                        col0 = (run * 2 + pos) * 4
                        for kp in range(kpn):
                            ps_s = ps_s0 if kp % 2 == 0 else ps_s1
                            s.wait_ge(ctx.h["pe"], pe0 + 2 * kp + 1)
                            if kp >= 1:
                                s.wait_ge(ctx.h["dve"], dve0 + kp)
                            s.activation(expp[:, :], ps_s[:, :], AF.Exp,
                                         bias=c_bias[:, col0 + kp:col0 + kp + 1],
                                         scale=c_scl[:, 0:1])
                            s.drain().then_inc(ctx.h["act"], 1)
                        s.wait_ge(ctx.h["pe"], pe0 + 2 * kpn)
                        s.copy(ctmp[:, :], ps_o[:, :])
                        s.drain().then_inc(ctx.h["act"], 1)

                    @blk.vector
                    def _(v, pos=pos, hh=hh, pe0=pe0, act0=act0, dve0=dve0):
                        v.memset(lpart[:, :], 0.0)
                        v.drain()

                        for kp in range(kpn):
                            v.wait_ge(ctx.h["act"], act0 + kp + 1)
                            if kp >= 1:
                                v.wait_ge(ctx.h["pe"], pe0 + 2 * kp)
                            v.tensor_tensor(exppm[:, :], expp[:, :],
                                            c_masks[:, kp * 1024:(kp + 1) * 1024],
                                            ALU.mult)
                            v.drain()
                            v.tensor_add(lpart[:, :], lpart[:, :], exppm[:, 0:512])
                            v.drain()
                            v.tensor_add(lpart[:, :], lpart[:, :], exppm[:, 512:1024])
                            v.drain().then_inc(ctx.h["dve"], 1)
                        v.wait_ge(ctx.h["pe"], pe0 + 2 * kpn + 1)
                        v.reciprocal(linv[:, :], ps_l[:, :])
                        v.drain().then_inc(ctx.h["dve"], 1)
                        v.wait_ge(ctx.h["pe"], pe0 + 2 * kpn + 2)
                        v.wait_ge(ctx.h["act"], act0 + kpn + 1)
                        v.tensor_mul(qb[:, hh * 1024 + pos * 512: hh * 1024 + (pos + 1) * 512],
                                     ctmp[:, :], ps_a[:, :])
                        v.drain().then_inc(ctx.h["dve"], 1)

                ctx.n["pe"] = pe0 + 2 * kpn + 2
                ctx.n["act"] = act0 + kpn + 1
                ctx.n["dve"] = dve0 + kpn + 2

            at_stack.close()
            tp_stack = contextlib.ExitStack()
            ps_tp = tp_stack.enter_context(nc.psum_tensor(pfx + "b2tpo_%d" % (run * 10 + pos), [128, 512], F32))
            ps_tp2 = tp_stack.enter_context(nc.psum_tensor(pfx + "b2tpo2_%d" % (run * 10 + pos), [128, 512], F32))

            for u in range(16):
                dg0 = ctx.n["dg"] + 16
                with nc.Block() as blk:
                    @blk.gpsimd
                    def _(g, u=u):
                        g.dma_start(wqb[:, :], wout_i[u * 128:(u + 1) * 128, :]
                                    ).then_inc(ctx.h["dg"], 16)
                ctx.n["dg"] = dg0
                pairs, evacs = [], []
                for j in range(4):
                    ps = ps_tp if j % 2 == 0 else ps_tp2
                    grp = [(ps[:, k4 * 128:(k4 + 1) * 128],
                            wqb[:, (4 * j + k4) * 128:(4 * j + k4 + 1) * 128])
                           for k4 in range(4)]
                    pairs.append(grp)
                    evacs.append((wT[:, j * 512:(j + 1) * 512], ps[:, :]))
                emit_transpose(ctx, nc, pairs, c_ident[:, :], evacs,
                               pre_waits=[("dg", dg0)])

                pe0, act0 = ctx.n["pe"], ctx.n["act"]
                with nc.Block() as blk:
                    @blk.tensor
                    def _(t, pos=pos, u=u, pe0=pe0):
                        ps = ps_a if u % 2 == 0 else ps_b
                        for dm in range(16):
                            t.matmul(ps[:, :], wT[:, dm * 128:(dm + 1) * 128],
                                     qb[:, dm * 1024 + pos * 512: dm * 1024 + (pos + 1) * 512],
                                     start=(dm == 0), stop=(dm == 15))
                        t.drain().then_inc(ctx.h["pe"], 1)

                    @blk.scalar
                    def _(s, u=u, pe0=pe0):
                        s.wait_ge(ctx.h["pe"], pe0 + 1)
                        s.copy(yfm[:, u * 512:(u + 1) * 512], ps_a[:, :] if u % 2 == 0 else ps_b[:, :])
                        s.drain().then_inc(ctx.h["act"], 1)
                ctx.n["pe"] = pe0 + 1
                ctx.n["act"] = act0 + 1

            emit_epilogue_chunk(ctx, nc, yfm, resid_i, resid_o,
                                run * 1024 + pos * 512,
                                xraw, ytm, lny, stats, aggr, rstd, eps_col,
                                ctmp, ps_tp, ps_tp2, c_ident[:, :])
            tp_stack.close()
        nc.leave_named_scope("att%d" % run, _sid_at2, False)

    with nc.Block() as blk:
        @blk.sync
        def _(sy):
            sy.wait_ge(ctx.h["st"], ctx.n["st"])

    stack.close()
    if own:
        ctx.close()
    return nc


def _prep_b2_inmaps(inputs, resids2):
    wqkv = np.ascontiguousarray(np.asarray(inputs["L1_attn_w_qkv"], dtype=np.float32))
    wout = np.ascontiguousarray(np.asarray(inputs["L1_attn_w_out"], dtype=np.float32))
    ident = np.eye(128, dtype=BF)
    masks = host_swa_masks()
    scl = host_scl()
    ones32 = np.ones((128, 1), dtype=np.float32)
    onesr32 = np.ones((1, 128), dtype=np.float32)
    have_r = resids2[0] is not None
    if have_r:
        full = np.zeros((B, S, D), dtype=np.float32)
        for c in range(NCORE):
            b = c // 4
            blkA, blkB = core_blocks(c)
            full[b, 1024 * blkA:1024 * blkA + 1024] = resids2[c][0:1024]
            full[b, 1024 * blkB:1024 * blkB + 1024] = resids2[c][1024:2048]
    in_maps = []
    for c in range(NCORE):
        b = c // 4
        blkA, blkB = core_blocks(c)
        m = {
            "w_qkv": wqkv, "w_out": wout,
            "ident": ident, "masks": masks, "bias16": host_b2_bias(c),
            "scl": scl, "ones32": ones32, "onesr32": onesr32,
        }
        if have_r:
            xbnd = np.zeros((1024, D), dtype=np.float32)
            for r, blk in enumerate([blkA, blkB]):
                if blk > 0:
                    xbnd[r * 512:(r + 1) * 512] = full[b, 1024 * blk - 512:1024 * blk]
            m["resid"] = np.ascontiguousarray(resids2[c])
            m["xbnd"] = xbnd
        in_maps.append(m)
    return in_maps


def run_b2(inputs, resids2):
    nc = build_b2()
    in_maps = _prep_b2_inmaps(inputs, resids2)
    res = run_bass_kernel_spmd(nc, in_maps, core_ids=list(range(8)))
    return [np.asarray(r["resid_o"]) for r in res.results]



def build_b3(nc=None, ctx=None, pfx="", io=None):
    own = nc is None
    if own:
        nc = bass.Bass("TRN2", target_bir_lowering=False)
        ctx = Ctx(nc)

    resid_i = _io(io, "resid", lambda: nc.dram_tensor("resid", [T, D], F32, kind="ExternalInput"))
    gw_i = _io(io, "qgate_w", lambda: nc.dram_tensor("gate_w", [8192, 2048], I32, kind="ExternalInput"))
    uw_i = nc.dram_tensor("up_w", [8192, 2048], I32, kind="ExternalInput")
    dw_i = nc.dram_tensor("down_w", [2048, 8192], I32, kind="ExternalInput")
    gsc_i = nc.dram_tensor("gsc", [128, 64], F32, kind="ExternalInput")
    usc_i = nc.dram_tensor("usc", [128, 64], F32, kind="ExternalInput")
    dsc_i = nc.dram_tensor("dsc", [128, 16], F32, kind="ExternalInput")
    ident_i = _io(io, "ident", lambda: nc.dram_tensor("ident", [128, 128], BF16, kind="ExternalInput"))
    scl_i = _io(io, "scl", lambda: nc.dram_tensor("scl", [128, 8], F32, kind="ExternalInput"))

    hspill = nc.dram_tensor(pfx + "hspill", [64 * 128, 2048], BF16)
    wdtsp = nc.dram_tensor(pfx + "wdtsp", [16 * 128, 8192], BF16)

    resid_o = _io(io, "resid_out", lambda: nc.dram_tensor("resid_o", [T, D], F32, kind="ExternalOutput"))

    stack = contextlib.ExitStack()

    def sb(name, shape, dt):
        return stack.enter_context(nc.sbuf_tensor(pfx + name, shape, dt))

    c_ident = sb("c_ident", [128, 128], BF16)
    c_gsc = sb("c_gsc", [128, 64], F32)
    c_usc = sb("c_usc", [128, 64], F32)
    c_dsc = sb("c_dsc", [128, 16], F32)
    c_scl = sb("c_scl", [128, 8], F32)
    xfm = sb("xfm", [128, 16 * 2048], BF16)
    gi32 = sb("gi32", [128, 4096], I32)
    ui32 = sb("ui32", [128, 4096], I32)
    wgb = sb("wgb", [128, 2048], BF16)
    wub = sb("wub", [128, 2048], BF16)
    wgT = sb("wgT", [128, 2048], BF16)
    wuT = sb("wuT", [128, 2048], BF16)
    hb = sb("hb", [128, 2048], BF16)
    wdld = sb("wdld", [128, 2 * 8192], BF16)
    rbuf = sb("rbuf", [128, 2 * 2048], F32)
    tbuf = sb("tbuf", [128, 2 * 2048], BF16)
    yfm = sb("yfm", [128, 16 * 512], BF16)
    ctmp = sb("ctmp", [128, 512], F32)
    stats = sb("stats", [128, 96], F32)
    aggr = sb("aggr", [128, 8], F32)
    rstd = sb("rstd", [128, 4], F32)
    ytm = tbuf[:, 0:2048]
    lny = tbuf[:, 2048:4096]
    xraw = rbuf[:, 0:2048]
    sg = tbuf[:, 0:512]
    su = tbuf[:, 512:1024]
    di32 = xfm.bitcast(I32)
    wdb = xfm[:, 16384:24576]
    wdT = xfm[:, 24576:32768]
    h64 = xfm
    ps_a = stack.enter_context(nc.psum_tensor(pfx + "ps_a", [128, 512], F32))
    ps_b = stack.enter_context(nc.psum_tensor(pfx + "ps_b", [128, 512], F32))
    ps_tp = stack.enter_context(nc.psum_tensor(pfx + "ps_tp", [128, 512], F32))
    ps_tp2 = stack.enter_context(nc.psum_tensor(pfx + "ps_tp2", [128, 512], F32))
    eps_col = c_scl[:, 1:2]

    with nc.Block() as blk:
        @blk.sync
        def _(sy):
            for dst, src in [(c_ident, ident_i), (c_gsc, gsc_i), (c_usc, usc_i),
                             (c_dsc, dsc_i), (c_scl, scl_i)]:
                sy.dma_start(dst[:, :], src.ap()).then_inc(ctx.h["ds"], 16)
            ctx.n["ds"] += 5 * 16
            sy.wait_ge(ctx.h["ds"], ctx.n["ds"])

    _sid1, _ = nc.enter_named_scope("q_xln", False)
    emit_xln_phase(ctx, nc, resid_i, 0, T,
                   lambda g, dt: xfm[:, dt * 2048 + g * 256: dt * 2048 + (g + 1) * 256],
                   rbuf, tbuf, stats, aggr, rstd, eps_col, c_ident[:, :],
                   ps_tp, ps_tp2, ntt=2)

    nc.leave_named_scope("q_xln", _sid1, False)
    _sid2, _ = nc.enter_named_scope("q_gu", False)
    p2sem = ["ds", "dg"]
    with nc.Block() as blk:
        @blk.sync
        def _(sy):
            sy.dma_start(gi32[:, 0:2048], gw_i[0:128, :]).then_inc(ctx.h["ds"], 16)
            sy.dma_start(ui32[:, 0:2048], uw_i[0:128, :]).then_inc(ctx.h["ds"], 16)
            ctx.n["ds"] += 32
    for it in range(64):
        sl = it % 2
        ld_tot = ctx.n[p2sem[sl]]
        dve0 = ctx.n["dve"]
        with nc.Block() as blk:
            @blk.sync
            def _(sy, it=it):
                if it + 1 < 64:
                    s2 = (it + 1) % 2
                    sy.dma_start(gi32[:, s2 * 2048:(s2 + 1) * 2048],
                                 gw_i[(it + 1) * 128:(it + 2) * 128, :]
                                 ).then_inc(ctx.h[p2sem[s2]], 16)
                    sy.dma_start(ui32[:, s2 * 2048:(s2 + 1) * 2048],
                                 uw_i[(it + 1) * 128:(it + 2) * 128, :]
                                 ).then_inc(ctx.h[p2sem[s2]], 16)
                    ctx.n[p2sem[s2]] += 32

            @blk.vector
            def _(v, it=it, sl=sl, ld_tot=ld_tot, dve0=dve0):
                v.wait_ge(ctx.h[p2sem[sl]], ld_tot)
                v.tensor_scalar(wgb[:, :], gi32[:, sl * 2048:(sl + 1) * 2048],
                                -128.0, c_gsc[:, it:it + 1], ALU.add, ALU.mult)
                v.drain()
                v.tensor_scalar(wub[:, :], ui32[:, sl * 2048:(sl + 1) * 2048],
                                -128.0, c_usc[:, it:it + 1], ALU.add, ALU.mult)
                v.drain().then_inc(ctx.h["dve"], 1)

            @blk.scalar
            def _(s):
                pass
        ctx.n["dve"] = dve0 + 1

        pairs, evacs = [], []
        for m, (srcb, dstb) in enumerate([(wgb, wgT), (wub, wuT)]):
            for j in range(4):
                ps = ps_tp if (m * 4 + j) % 2 == 0 else ps_tp2
                grp = [(ps[:, k4 * 128:(k4 + 1) * 128],
                        srcb[:, (4 * j + k4) * 128:(4 * j + k4 + 1) * 128])
                       for k4 in range(4)]
                pairs.append(grp)
                evacs.append((dstb[:, j * 512:(j + 1) * 512], ps[:, :]))
        emit_transpose(ctx, nc, pairs, c_ident[:, :], evacs,
                       pre_waits=[("dve", ctx.n["dve"])])

        st0 = ctx.n["st"]
        pe0, act0, dve0 = ctx.n["pe"], ctx.n["act"], ctx.n["dve"]
        with nc.Block() as blk:
            @blk.tensor
            def _(t, it=it, pe0=pe0, act0=act0):
                for ch in range(4):
                    if ch >= 1:
                        t.wait_ge(ctx.h["act"], act0 + 2 * ch - 1)
                    for dm in range(16):
                        t.matmul(ps_a[:, :], wgT[:, dm * 128:(dm + 1) * 128],
                                 xfm[:, dm * 2048 + ch * 512: dm * 2048 + (ch + 1) * 512],
                                 start=(dm == 0), stop=(dm == 15))
                    t.drain().then_inc(ctx.h["pe"], 1)
                    if ch >= 1:
                        t.wait_ge(ctx.h["act"], act0 + 2 * ch)
                    for dm in range(16):
                        t.matmul(ps_b[:, :], wuT[:, dm * 128:(dm + 1) * 128],
                                 xfm[:, dm * 2048 + ch * 512: dm * 2048 + (ch + 1) * 512],
                                 start=(dm == 0), stop=(dm == 15))
                    t.drain().then_inc(ctx.h["pe"], 1)

            @blk.scalar
            def _(s, it=it, pe0=pe0, act0=act0, dve0=dve0):
                for ch in range(4):
                    s.wait_ge(ctx.h["pe"], pe0 + 2 * ch + 1)
                    if ch >= 1:
                        s.wait_ge(ctx.h["dve"], dve0 + ch)
                    s.activation(sg[:, :], ps_a[:, :], AF.Silu)
                    s.drain().then_inc(ctx.h["act"], 1)
                    s.wait_ge(ctx.h["pe"], pe0 + 2 * ch + 2)
                    s.copy(su[:, :], ps_b[:, :])
                    s.drain().then_inc(ctx.h["act"], 1)

            @blk.vector
            def _(v, it=it, act0=act0, dve0=dve0):
                for ch in range(4):
                    v.wait_ge(ctx.h["act"], act0 + 2 * ch + 2)
                    v.tensor_tensor(hb[:, ch * 512:(ch + 1) * 512], sg[:, :], su[:, :],
                                    ALU.mult)
                    v.drain().then_inc(ctx.h["dve"], 1)

            @blk.sync
            def _(sy, it=it, dve0=dve0, st0=st0):
                sy.wait_ge(ctx.h["dve"], dve0 + 4)
                sy.dma_start(hspill[it * 128:(it + 1) * 128, :], hb[:, :]
                             ).then_inc(ctx.h["st"], 16)
                sy.wait_ge(ctx.h["st"], st0 + 16)
        ctx.n["pe"] = pe0 + 8
        ctx.n["act"] = act0 + 8
        ctx.n["dve"] = dve0 + 4
        ctx.n["st"] = st0 + 16

    nc.leave_named_scope("q_gu", _sid2, False)
    _sid3, _ = nc.enter_named_scope("q_wd", False)
    for dmt in range(16):
        ds0 = ctx.n["ds"] + 16
        with nc.Block() as blk:
            @blk.sync
            def _(sy, dmt=dmt):
                sy.dma_start(di32[:, 0:8192], dw_i[dmt * 128:(dmt + 1) * 128, :]
                             ).then_inc(ctx.h["ds"], 16)
        ctx.n["ds"] = ds0

        dve0 = ctx.n["dve"]
        with nc.Block() as blk:
            @blk.vector
            def _(v, dmt=dmt, ds0=ds0, dve0=dve0):
                v.wait_ge(ctx.h["ds"], ds0)
                v.tensor_scalar(wdb[:, :], di32[:, 0:8192], -128.0,
                                c_dsc[:, dmt:dmt + 1], ALU.add, ALU.mult)
                v.drain().then_inc(ctx.h["dve"], 1)

            @blk.scalar
            def _(s):
                pass
        ctx.n["dve"] = dve0 + 1

        pairs, evacs = [], []
        for j in range(16):
            ps = ps_tp if j % 2 == 0 else ps_tp2
            grp = [(ps[:, k4 * 128:(k4 + 1) * 128],
                    wdb[:, (4 * j + k4) * 128:(4 * j + k4 + 1) * 128])
                   for k4 in range(4)]
            pairs.append(grp)
            evacs.append((wdT[:, j * 512:(j + 1) * 512], ps[:, :]))
        emit_transpose(ctx, nc, pairs, c_ident[:, :], evacs,
                       pre_waits=[("dve", ctx.n["dve"])])

        st0 = ctx.n["st"]
        with nc.Block() as blk:
            @blk.sync
            def _(sy, dmt=dmt, st0=st0):
                sy.dma_start(wdtsp[dmt * 128:(dmt + 1) * 128, :], wdT[:, :]
                             ).then_inc(ctx.h["st"], 16)
                sy.wait_ge(ctx.h["st"], st0 + 16)
        ctx.n["st"] = st0 + 16

    nc.leave_named_scope("q_wd", _sid3, False)
    _sid4, _ = nc.enter_named_scope("q_down", False)
    for ch in range(4):
        ds0 = ctx.n["ds"] + 64 * 16
        with nc.Block() as blk:
            @blk.sync
            def _(sy, ch=ch):
                for it in range(64):
                    src = bass.AP(hspill, (it * 128) * 2048 + ch * 512,
                                  [[2048, 128], [1, 512]])
                    sy.dma_start(h64[:, it * 512:(it + 1) * 512], src
                                 ).then_inc(ctx.h["ds"], 16)
        ctx.n["ds"] = ds0

        p4sem = ["dg", "dk"]
        with nc.Block() as blk:
            @blk.gpsimd
            def _(g, ch=ch):
                g.dma_start(wdld[:, 0:8192], wdtsp[0:128, :]).then_inc(ctx.h["dg"], 16)
                ctx.n["dg"] += 16
        for dmt in range(16):
            sl = dmt % 2
            wd_tot = ctx.n[p4sem[sl]]
            pe0, act0 = ctx.n["pe"], ctx.n["act"]
            with nc.Block() as blk:
                @blk.gpsimd
                def _(g, dmt=dmt):
                    if dmt + 1 < 16:
                        s2 = (dmt + 1) % 2
                        g.dma_start(wdld[:, s2 * 8192:(s2 + 1) * 8192],
                                    wdtsp[(dmt + 1) * 128:(dmt + 2) * 128, :]
                                    ).then_inc(ctx.h[p4sem[s2]], 16)
                        ctx.n[p4sem[s2]] += 16

                @blk.tensor
                def _(t, ch=ch, dmt=dmt, sl=sl, pe0=pe0, wd_tot=wd_tot, ds0=ds0):
                    t.wait_ge(ctx.h["ds"], ds0)
                    t.wait_ge(ctx.h[p4sem[sl]], wd_tot)
                    ps = ps_a if dmt % 2 == 0 else ps_b
                    for it in range(64):
                        t.matmul(ps[:, :], wdld[:, sl * 8192 + it * 128: sl * 8192 + (it + 1) * 128],
                                 h64[:, it * 512:(it + 1) * 512],
                                 start=(it == 0), stop=(it == 63))
                    t.drain().then_inc(ctx.h["pe"], 1)

                @blk.scalar
                def _(s, dmt=dmt, pe0=pe0):
                    ps = ps_a if dmt % 2 == 0 else ps_b
                    s.wait_ge(ctx.h["pe"], pe0 + 1)
                    s.copy(yfm[:, dmt * 512:(dmt + 1) * 512], ps[:, :])
                    s.drain().then_inc(ctx.h["act"], 1)
            ctx.n["pe"] = pe0 + 1
            ctx.n["act"] = act0 + 1

        emit_epilogue_chunk(ctx, nc, yfm, resid_i, resid_o, ch * 512,
                            xraw, ytm, lny, stats, aggr, rstd, eps_col,
                            ctmp, ps_tp, ps_tp2, c_ident[:, :])

    with nc.Block() as blk:
        @blk.sync
        def _(sy):
            sy.wait_ge(ctx.h["st"], ctx.n["st"])
    nc.leave_named_scope("q_down", _sid4, False)

    stack.close()
    if own:
        ctx.close()
    return nc


def _prep_b3_inmaps(inputs, resids3):
    gw = np.ascontiguousarray(np.asarray(inputs["L1_ffn_gate_w"], dtype=np.int32))
    uw = np.ascontiguousarray(np.asarray(inputs["L1_ffn_up_w"], dtype=np.int32))
    dw = np.ascontiguousarray(np.asarray(inputs["L1_ffn_down_w"], dtype=np.int32))
    gsc = np.ascontiguousarray(np.asarray(inputs["L1_ffn_gate_scale"], dtype=np.float32
                                          ).reshape(64, 128).T)
    usc = np.ascontiguousarray(np.asarray(inputs["L1_ffn_up_scale"], dtype=np.float32
                                          ).reshape(64, 128).T)
    dsc = np.ascontiguousarray(np.asarray(inputs["L1_ffn_down_scale"], dtype=np.float32
                                          ).reshape(16, 128).T)
    ident = np.eye(128, dtype=BF)
    scl = host_scl()
    in_maps = []
    for c in range(NCORE):
        m = {
            "gate_w": gw, "up_w": uw, "down_w": dw,
            "gsc": gsc, "usc": usc, "dsc": dsc,
            "ident": ident, "scl": scl,
        }
        if resids3[c] is not None:
            m["resid"] = np.ascontiguousarray(resids3[c])
        in_maps.append(m)
    return in_maps


def run_b3(inputs, resids3):
    nc = build_b3()
    in_maps = _prep_b3_inmaps(inputs, resids3)
    res = run_bass_kernel_spmd(nc, in_maps, core_ids=list(range(8)))
    return [np.asarray(r["resid_o"]) for r in res.results]



def host_bidx(c):
    idx = np.zeros((128, 8), dtype=np.int32)
    blkA, blkB = core_blocks(c)
    for r, blk in enumerate([blkA, blkB]):
        if blk == 0:
            base = 8192
        else:
            prev = blk - 1
            jp = prev if prev <= 3 else 7 - prev
            o = (c // 4) * 4 + jp
            slab = 0 if prev <= 3 else 1
            base = o * 1024 + slab * 512
        for t in range(4):
            idx[:, r * 4 + t] = base + t * 128 + np.arange(128)
    return idx


def build_fused():
    nc = bass.Bass("TRN2", target_bir_lowering=False)
    ctx = Ctx(nc)

    x_full = nc.dram_tensor("xkv", [S, D], F32, kind="ExternalInput")
    x_own = nc.dram_tensor("xown", [T, D], F32, kind="ExternalInput")
    ident = nc.dram_tensor("ident", [128, 128], BF16, kind="ExternalInput")
    scl = nc.dram_tensor("scl", [128, 8], F32, kind="ExternalInput")
    ones32 = nc.dram_tensor("ones32", [128, 1], F32, kind="ExternalInput")
    onesr32 = nc.dram_tensor("onesr32", [1, 128], F32, kind="ExternalInput")
    masks2 = nc.dram_tensor("masks2", [128, 8 * 512], BF16, kind="ExternalInput")
    qgate_w = nc.dram_tensor("qgate_w", [8192, 2048], I32, kind="ExternalInput")
    bidx = nc.dram_tensor("bidx", [128, 8], I32, kind="ExternalInput")

    r1 = nc.dram_tensor("f_r1", [T, D], F32)
    r2 = nc.dram_tensor("f_r2", [T, D], F32)
    r3 = nc.dram_tensor("f_r3", [T, D], F32)
    ccb = nc.dram_tensor("f_ccb", [1024, D], F32)
    gb = nc.dram_tensor("f_gb", [9216, D], F32)

    resid_o = nc.dram_tensor("resid_o", [T, D], F32, kind="ExternalOutput")

    shared = {"ident": ident, "scl": scl, "ones32": ones32, "onesr32": onesr32}

    with nc.named_scope("ph_b0"):
        build_b0(nc, ctx, "a_", dict(shared, x_full=x_full, x_own=x_own, resid_out=r1))
    with nc.named_scope("ph_b1"):
        build_b1(nc, ctx, "b_", dict(shared, resid=r1, resid_out=r2))

    zstack = contextlib.ExitStack()
    zbuf = zstack.enter_context(nc.sbuf_tensor("f_zbuf", [128, 2048], F32))
    with nc.Block() as blk:
        @blk.vector
        def _(v):
            v.memset(zbuf[:, :], 0.0)
            v.drain().then_inc(ctx.h["dve"], 1)
            ctx.n["dve"] += 1

        @blk.scalar
        def _(s):
            pass

    with nc.Block() as blk:
        @blk.gpsimd
        def _(g):
            g.dma_start(ccb.ap()[0:512, :], r2.ap()[512:1024, :]).then_inc(ctx.h["dg"], 16)
            g.dma_start(ccb.ap()[512:1024, :], r2.ap()[1536:2048, :]).then_inc(ctx.h["dg"], 16)
            ctx.n["dg"] += 32
            g.wait_ge(ctx.h["dg"], ctx.n["dg"])
            g.collective_compute(
                "AllGather", ALU.bypass, replica_groups=[list(range(8))],
                ins=[ccb.ap().opt()], outs=[gb.ap()[0:8192, :].opt()],
            ).then_inc(ctx.h["cc"], 1)
            ctx.n["cc"] += 1
            g.wait_ge(ctx.h["cc"], ctx.n["cc"])

        @blk.sync
        def _(sy):
            sy.wait_ge(ctx.h["dve"], ctx.n["dve"])
            for t in range(4):
                sy.dma_start(gb.ap()[8192 + t * 128: 8192 + (t + 1) * 128, :],
                             zbuf[:, :]).then_inc(ctx.h["st"], 16)
            ctx.n["st"] += 64
            sy.wait_ge(ctx.h["st"], ctx.n["st"])
    zstack.close()

    with nc.named_scope("ph_b2"):
        build_b2(nc, ctx, "c_", dict(shared, resid=r2, resid_out=r3, gb=gb, bidx=bidx,
                                     masks2=masks2))
    with nc.named_scope("ph_b3"):
        build_b3(nc, ctx, "d_", dict(shared, resid=r3, resid_out=resid_o, qgate_w=qgate_w))

    ctx.close()
    return nc


def _prep_fused_inmaps(inputs):
    m0 = _prep_b0_inmaps(inputs)
    m1 = _prep_b1_inmaps(inputs, [None] * NCORE)
    m2 = _prep_b2_inmaps(inputs, [None] * NCORE)
    m3 = _prep_b3_inmaps(inputs, [None] * NCORE)
    in_maps = []
    for c in range(NCORE):
        m = {}
        m.update({k: v for k, v in m0[c].items()})
        m.update({k: v for k, v in m1[c].items() if k != "resid"})
        m.update({k: v for k, v in m2[c].items() if k not in ("resid", "xbnd", "masks")})
        m["masks2"] = m2[c]["masks"]
        m.update({k: v for k, v in m3[c].items() if k not in ("resid", "gate_w")})
        m["qgate_w"] = m3[c]["gate_w"]
        m["bidx"] = host_bidx(c)
        in_maps.append(m)
    return in_maps


# version 04 (final)
# speedup vs baseline: 1.0891x; 1.0734x over previous
import contextlib
import numpy as np
import ml_dtypes
import concourse.bass as bass
import concourse.mybir as mybir
from concourse.bass_utils import run_bass_kernel_spmd
from concourse.bass_core_rust import IndirectOffsetOnAxis

F32 = mybir.dt.float32
BF16 = mybir.dt.bfloat16
I32 = mybir.dt.int32
AF = mybir.ActivationFunctionType
ALU = mybir.AluOpType
BF = ml_dtypes.bfloat16

B, S, D = 2, 8192, 2048
DEBUG_B0 = False
T = 2048
NCORE = 8
KTMAX = [28, 32, 60, 64]
KT2 = [k // 2 for k in KTMAX]
NJOB2 = sum(KT2)
EXP_SHIFT = -30.0
SCALE_QK = float(1.0 / np.sqrt(128.0))
NEG_BIG = -1e30
EPS = 1e-5


def core_blocks(c):
    j = c % 4
    return [j, 7 - j]


def qt_q0(c, qt):
    blkA, blkB = core_blocks(c)
    blk = blkA if qt < 2 else blkB
    return 1024 * blk + 512 * (qt % 2)


def job2_col(qt, kp):
    return sum(KT2[:qt]) + kp



def host_causal_masks_paired():
    out = np.zeros((128, 2 * 1024), dtype=np.float32)
    k = np.arange(128)[:, None]
    q = np.arange(512)[None, :]
    for slot, (d0, d1) in enumerate([(0, 128), (256, 384)]):
        out[:, slot * 1024:slot * 1024 + 512] = (k + d0 <= q)
        out[:, slot * 1024 + 512:(slot + 1) * 1024] = (k + d1 <= q)
    return out.astype(BF)


def host_b0_tables(c):
    bias = np.zeros((128, NJOB2), dtype=np.float32)
    kdiag = np.zeros((128, 4), dtype=np.int32)
    for qt in range(4):
        p0 = (qt_q0(c, qt) + 512) // 128
        kdiag[:, qt] = np.arange(128) * 64 + (p0 - 4)
        for pi in range(KT2[qt]):
            col = job2_col(qt, pi)
            if pi < 2:
                bias[:, col] = EXP_SHIFT
            else:
                bias[:, col] = EXP_SHIFT if (2 * (pi - 2) + 1) < (p0 - 4) else NEG_BIG
    return bias, kdiag


def host_scl():
    scl = np.zeros((128, 8), dtype=np.float32)
    scl[:, 0] = SCALE_QK
    scl[:, 1] = EPS
    scl[:, 2] = 1.0 / 8192.0
    scl[:, 3] = -128.0
    return scl



class Ctx:
    def __init__(self, nc):
        self.nc = nc
        self._cms = []
        self.h = {}
        self.n = {}
        for nm in ["ds", "dg", "dk", "cc", "pe", "act", "dve", "st"]:
            cm = nc.semaphore(nm)
            self.h[nm] = cm.__enter__()
            self._cms.append(cm)
            self.n[nm] = 0

    def close(self):
        for cm in reversed(self._cms):
            cm.__exit__(None, None, None)


def ln_vec_emit(ctx, v, x_ap, ntt, stats, aggr, rstd, dve0, act0, out_ap=None,
                dg_wait=None, scratch32=None):
    if out_ap is None:
        out_ap = x_ap
    if dg_wait is not None:
        v.wait_ge(ctx.h[dg_wait[0]], dg_wait[1])
    for t in range(ntt):
        for ch in range(4):
            src = x_ap[:, t * 2048 + 512 * ch: t * 2048 + 512 * (ch + 1)]
            if scratch32 is not None:
                v.tensor_copy(scratch32[:, 0:512], src)
                v.drain()
                src = scratch32[:, 0:512]
            v.bn_stats(stats[:, t * 24 + 6 * ch: t * 24 + 6 * (ch + 1)], src)
            v.drain()
        v.bn_aggr(aggr[:, 2 * t:2 * t + 2], stats[:, t * 24:(t + 1) * 24])
        v.drain()
    v.drain().then_inc(ctx.h["dve"], 1)
    v.wait_ge(ctx.h["act"], act0 + 1)
    for t in range(ntt):
        v.reciprocal(rstd[:, t:t + 1], rstd[:, t:t + 1])
    v.drain()
    for t in range(ntt):
        v.tensor_scalar(out_ap[:, t * 2048:(t + 1) * 2048],
                        x_ap[:, t * 2048:(t + 1) * 2048],
                        aggr[:, 2 * t:2 * t + 1], rstd[:, t:t + 1],
                        ALU.subtract, ALU.mult)
    v.drain().then_inc(ctx.h["dve"], 1)


def ln_scl_emit(ctx, s, ntt, aggr, rstd, eps_col, dve0, act0):
    s.wait_ge(ctx.h["dve"], dve0 + 1)
    for t in range(ntt):
        s.activation(rstd[:, t:t + 1], aggr[:, 2 * t + 1:2 * t + 2], AF.Sqrt,
                     bias=eps_col, scale=1.0)
    s.drain().then_inc(ctx.h["act"], 1)


def emit_ln_block(ctx, nc, x_ap, ntt, stats, aggr, rstd, eps_col, out_ap=None,
                  dg_wait=None, extra_gp=None, scratch32=None):
    dve0, act0 = ctx.n["dve"], ctx.n["act"]
    with nc.Block() as blk:
        @blk.vector
        def _(v):
            ln_vec_emit(ctx, v, x_ap, ntt, stats, aggr, rstd, dve0, act0,
                        out_ap=out_ap, dg_wait=dg_wait, scratch32=scratch32)

        @blk.scalar
        def _(s):
            ln_scl_emit(ctx, s, ntt, aggr, rstd, eps_col, dve0, act0)

        if extra_gp is not None:
            @blk.gpsimd
            def _(g):
                extra_gp(g)
    ctx.n["dve"] = dve0 + 2
    ctx.n["act"] = act0 + 1


def emit_transpose(ctx, nc, pairs, ident, evac_list, pre_waits=()):
    pe0, act0 = ctx.n["pe"], ctx.n["act"]
    ngrp = len(evac_list)
    with nc.Block() as blk:
        @blk.tensor
        def _(t):
            for sem, val in pre_waits:
                t.wait_ge(ctx.h[sem], val)
            for gi, (grp, _dst) in enumerate(zip(pairs, evac_list)):
                if gi >= 2:
                    t.wait_ge(ctx.h["act"], act0 + gi - 1)
                for (dst_ps, src) in grp:
                    t.matmul(dst_ps, src, ident, start=True, stop=True)
                t.drain().then_inc(ctx.h["pe"], 1)

        @blk.scalar
        def _(s):
            for gi, (dst, ps_ap) in enumerate(evac_list):
                s.wait_ge(ctx.h["pe"], pe0 + gi + 1)
                s.copy(dst, ps_ap)
                s.drain().then_inc(ctx.h["act"], 1)
    ctx.n["pe"] = pe0 + ngrp
    ctx.n["act"] = act0 + ngrp


def emit_tm2fm(ctx, nc, src_tm, dst_fm, ident, ps_tp, ps_tp2, ntt=4):
    pairs, evacs = [], []
    for dt in range(16):
        ps = ps_tp if dt % 2 == 0 else ps_tp2
        grp = []
        for tt in range(ntt):
            grp.append((ps[:, tt * 128:(tt + 1) * 128],
                        src_tm[:, tt * 2048 + dt * 128: tt * 2048 + (dt + 1) * 128]))
        pairs.append(grp)
        evacs.append((dst_fm[:, dt * ntt * 128:(dt + 1) * ntt * 128],
                      ps[:, 0:ntt * 128]))
    emit_transpose(ctx, nc, pairs, ident, evacs)


def emit_proj(ctx, nc, wpool, w_off, x_fm, out_sb, nout, psA, psB,
              bias_sb=None, act_func=None, pre_waits=(), ndm=16, xstride=512):
    pe0, act0 = ctx.n["pe"], ctx.n["act"]
    with nc.Block() as blk:
        @blk.tensor
        def _(t):
            for sem, val in pre_waits:
                t.wait_ge(ctx.h[sem], val)
            for u in range(nout):
                ps = psA if u % 2 == 0 else psB
                if u >= 2:
                    t.wait_ge(ctx.h["act"], act0 + u - 1)
                ins = None
                for dm in range(ndm):
                    ins = t.matmul(ps[:, :],
                                   wpool[:, w_off + dm * 2048 + u * 128: w_off + dm * 2048 + (u + 1) * 128],
                                   x_fm[:, dm * xstride:dm * xstride + 512],
                                   start=(dm == 0), stop=(dm == ndm - 1))
                ins.then_inc(ctx.h["pe"], 1)

        @blk.scalar
        def _(s):
            for u in range(nout):
                ps = psA if u % 2 == 0 else psB
                s.wait_ge(ctx.h["pe"], pe0 + u + 1)
                if act_func is not None:
                    s.activation(out_sb[:, u * 512:(u + 1) * 512], ps[:, :],
                                 act_func,
                                 bias=(bias_sb[:, u:u + 1] if bias_sb is not None else 0.0),
                                 scale=1.0)
                else:
                    s.copy(out_sb[:, u * 512:(u + 1) * 512], ps[:, :])
                s.drain().then_inc(ctx.h["act"], 1)
    ctx.n["pe"] = pe0 + nout
    ctx.n["act"] = act0 + nout


def st_grp(st0, g):
    tot = st0
    for i in range(4 * (g + 1)):
        tot += 16 if i < 16 else 64
    return tot




def emit_tm2fm_gen(ctx, nc, src_tm, dst_fn, ident, ps_tp, ps_tp2, ntt=4):
    pairs, evacs = [], []
    for dt in range(16):
        ps = ps_tp if dt % 2 == 0 else ps_tp2
        grp = [(ps[:, tt * 128:(tt + 1) * 128],
                src_tm[:, tt * 2048 + dt * 128: tt * 2048 + (dt + 1) * 128])
               for tt in range(ntt)]
        pairs.append(grp)
        evacs.append((dst_fn(dt), ps[:, 0:ntt * 128]))
    emit_transpose(ctx, nc, pairs, ident, evacs)


def emit_xln_phase(ctx, nc, resid_dram, row0, nrows, xfm_dst_fn, rbuf, tbuf,
                   stats, aggr, rstd, eps_col, ident, ps_tp, ps_tp2, ntt=4):
    for g in range(nrows // (128 * ntt)):
        ds0 = ctx.n["ds"] + ntt * 16
        with nc.Block() as blk:
            @blk.sync
            def _(sy, g=g):
                for tt in range(ntt):
                    r = row0 + g * 128 * ntt + tt * 128
                    sy.dma_start(rbuf[:, tt * 2048:(tt + 1) * 2048],
                                 resid_dram[r:r + 128, :]).then_inc(ctx.h["ds"], 16)
        ctx.n["ds"] = ds0
        emit_ln_block(ctx, nc, rbuf, ntt, stats, aggr, rstd, eps_col,
                      out_ap=tbuf, dg_wait=("ds", ds0))
        emit_tm2fm_gen(ctx, nc, tbuf, lambda dt, g=g: xfm_dst_fn(g, dt),
                       ident, ps_tp, ps_tp2, ntt=ntt)


def emit_epilogue_chunk(ctx, nc, yfm, resid_in, resid_out, row0,
                        xraw, ytm, lny, stats, aggr, rstd, eps_col, ctmp,
                        ps_tp, ps_tp2, ident):
    for tt in range(4):
        pairs, evacs = [], []
        for g in range(4):
            ps = ps_tp if g % 2 == 0 else ps_tp2
            grp = []
            for k4 in range(4):
                dt = g * 4 + k4
                grp.append((ps[:, k4 * 128:(k4 + 1) * 128],
                            yfm[:, dt * 512 + tt * 128: dt * 512 + (tt + 1) * 128]))
            pairs.append(grp)
            evacs.append((ytm[:, g * 512:(g + 1) * 512], ps[:, :]))
        emit_transpose(ctx, nc, pairs, ident, evacs)
        ds_xr = ctx.n["ds"] + 16
        dve0e, act0e = ctx.n["dve"], ctx.n["act"]
        with nc.Block() as blk:
            @blk.vector
            def _(v, tt=tt, dve0e=dve0e, act0e=act0e, ds_xr=ds_xr):
                ln_vec_emit(ctx, v, ytm[:, 0:2048], 1, stats, aggr, rstd,
                            dve0e, act0e, out_ap=lny[:, 0:2048], scratch32=ctmp)
                v.wait_ge(ctx.h["ds"], ds_xr)
                v.tensor_add(xraw[:, :], xraw[:, :], lny[:, 0:2048])
                v.drain().then_inc(ctx.h["dve"], 1)

            @blk.scalar
            def _(s, dve0e=dve0e, act0e=act0e):
                ln_scl_emit(ctx, s, 1, aggr, rstd, eps_col, dve0e, act0e)

            @blk.sync
            def _(sy, tt=tt, dve0e=dve0e):
                r = row0 + tt * 128
                sy.dma_start(xraw[:, :], resid_in[r:r + 128, :]).then_inc(ctx.h["ds"], 16)
                sy.wait_ge(ctx.h["dve"], dve0e + 3)
                sy.dma_start(resid_out[r:r + 128, :], xraw[:, :]).then_inc(ctx.h["st"], 16)
                sy.wait_ge(ctx.h["st"], ctx.n["st"] + 16)
        ctx.n["ds"] = ds_xr
        ctx.n["dve"] = dve0e + 3
        ctx.n["act"] = act0e + 1
        ctx.n["st"] += 16


def host_swa_masks():
    out = np.zeros((128, 8 * 512), dtype=np.float32)
    k = np.arange(128)[:, None]
    q = np.arange(512)[None, :]
    for kt in range(8):
        dlt = -512 + 128 * kt
        m = ((k + dlt <= q) & (k + dlt >= q - 512))
        out[:, kt * 512:(kt + 1) * 512] = m
    return out.astype(BF)


def host_b2_bias(c):
    bias = np.full((128, 16), EXP_SHIFT, dtype=np.float32)
    blkA, blkB = core_blocks(c)
    for run, blk in enumerate([blkA, blkB]):
        for pos in range(2):
            gchunk = 2 * blk + pos
            if gchunk == 0:
                for kp in range(2):
                    bias[:, (run * 2 + pos) * 4 + kp] = NEG_BIG
    return bias



def _io(io, key, mk):
    return io[key] if io is not None and key in io else mk()


def build_b0(nc=None, ctx=None, pfx="", io=None):
    own = nc is None
    if own:
        nc = bass.Bass("TRN2", target_bir_lowering=False)
        ctx = Ctx(nc)

    xkv = _io(io, "x_full", lambda: nc.dram_tensor("xkv", [S, D], F32, kind="ExternalInput"))
    xown = _io(io, "x_own", lambda: nc.dram_tensor("xown", [T, D], F32, kind="ExternalInput"))
    wq_i = nc.dram_tensor("wq", [D, D], F32, kind="ExternalInput")
    wk_i = nc.dram_tensor("wk", [D, D], F32, kind="ExternalInput")
    wv_i = nc.dram_tensor("wv", [D, D], F32, kind="ExternalInput")
    wo_i = nc.dram_tensor("wo", [D, D], F32, kind="ExternalInput")
    bo_i = nc.dram_tensor("bo_t", [128, 16], F32, kind="ExternalInput")
    ident_i = _io(io, "ident", lambda: nc.dram_tensor("ident", [128, 128], BF16, kind="ExternalInput"))
    masks_i = nc.dram_tensor("masks", [128, 2 * 1024], BF16, kind="ExternalInput")
    bias_i = nc.dram_tensor("bias_tbl", [128, NJOB2], F32, kind="ExternalInput")
    kdiag_i = nc.dram_tensor("kdiag", [128, 4], I32, kind="ExternalInput")
    scl_i = _io(io, "scl", lambda: nc.dram_tensor("scl", [128, 8], F32, kind="ExternalInput"))
    ones_i = _io(io, "ones32", lambda: nc.dram_tensor("ones32", [128, 1], F32, kind="ExternalInput"))
    onesr_i = _io(io, "onesr32", lambda: nc.dram_tensor("onesr32", [1, 128], F32, kind="ExternalInput"))

    kspill = nc.dram_tensor(pfx + "kspill", [16 * 128, 64 * 128], BF16)
    vspill = nc.dram_tensor(pfx + "vspill", [16 * 128, 64 * 128], BF16)
    resid1 = _io(io, "resid_out", lambda: nc.dram_tensor("resid1", [T, D], F32, kind="ExternalOutput"))
    if DEBUG_B0:
        dbg_xfm = nc.dram_tensor("dbg_xfm", [128, 16 * 512], F32, kind="ExternalOutput")
        dbg_q = nc.dram_tensor("dbg_q", [128, 16 * 512], F32, kind="ExternalOutput")
        dbg_ctx = nc.dram_tensor("dbg_ctx", [128, 16 * 512], F32, kind="ExternalOutput")
        dbg_y = nc.dram_tensor("dbg_y", [128, 16 * 512], F32, kind="ExternalOutput")
        dbg_kbuf = nc.dram_tensor("dbg_kbuf", [128, 8192], F32, kind="ExternalOutput")
        dbg_xtm = nc.dram_tensor("dbg_xtm", [128, 8192], F32, kind="ExternalOutput")
        dbg_st = nc.dram_tensor("dbg_st", [128, 112], F32, kind="ExternalOutput")
        dbg_xfm2 = nc.dram_tensor("dbg_xfm2", [128, 2048], F32, kind="ExternalOutput")
        dbg_xfm3 = nc.dram_tensor("dbg_xfm3", [128, 2048], F32, kind="ExternalOutput")
        dbg_vbuf = nc.dram_tensor("dbg_vbuf", [128, 8192], F32, kind="ExternalOutput")

    stack = contextlib.ExitStack()

    def sb(name, shape, dt):
        return stack.enter_context(nc.sbuf_tensor(pfx + name, shape, dt))

    c_ident = sb("c_ident", [128, 128], BF16)
    c_masks = sb("c_masks", [128, 2 * 1024], BF16)
    c_bias = sb("c_bias", [128, NJOB2], F32)
    c_kdiag = sb("c_kdiag", [128, 4], I32)
    c_scl = sb("c_scl", [128, 8], F32)
    c_ones = sb("c_ones", [128, 1], F32)
    c_onesr = sb("c_onesr", [1, 128], F32)
    c_bo = sb("c_bo", [128, 16], F32)
    wpool = sb("wpool", [128, 32 * 2048], BF16)
    xtm = sb("xtm", [128, 8192], BF16)
    xfm = sb("xfm", [128, 16 * 512], BF16)
    qctx = sb("qctx", [128, 16 * 512], BF16)
    yfm = xfm
    ytm = xtm[:, 0:2048]
    lny = xtm[:, 2048:4096]
    xraw = sb("xraw", [128, 2048], F32)
    kvev = sb("kvev", [128, 2048], BF16)
    expp = kvev[:, 0:1024]
    exppm = kvev[:, 1024:2048]
    lpart = sb("lpart", [128, 512], F32)
    linv = lpart[0:1, :]
    ctmp = sb("ctmp", [128, 512], F32)
    stats = sb("stats", [128, 96], F32)
    aggr = sb("aggr", [128, 8], F32)
    rstd = sb("rstd", [128, 4], F32)
    ps_a = stack.enter_context(nc.psum_tensor(pfx + "ps_a", [128, 512], F32))
    ps_b = stack.enter_context(nc.psum_tensor(pfx + "ps_b", [128, 512], F32))
    if True:
        WQO = 0
        KBUF = [16 * 2048, 16 * 2048 + 8192]
        VBUF = [16 * 2048 + 2 * 8192, 16 * 2048 + 3 * 8192]
        eps_col = c_scl[:, 1:2]

        _sid_p0, _ = nc.enter_named_scope("wkv", False)
        with nc.Block() as blk:
            @blk.sync
            def _(sy):
                for dst, src in [(c_ident, ident_i), (c_masks, masks_i),
                                 (c_bias, bias_i), (c_kdiag, kdiag_i),
                                 (c_scl, scl_i), (c_ones, ones_i),
                                 (c_onesr, onesr_i), (c_bo, bo_i)]:
                    sy.dma_start(dst[:, :], src.ap()).then_inc(ctx.h["ds"], 16)
                ctx.n["ds"] += 8 * 16
                sy.wait_ge(ctx.h["ds"], ctx.n["ds"])

            @blk.gpsimd
            def _(g):
                for t in range(16):
                    g.dma_start(wpool[:, t * 2048:(t + 1) * 2048],
                                wk_i[128 * t:128 * (t + 1), :]).then_inc(ctx.h["dg"], 16)
                    g.dma_start(wpool[:, (16 + t) * 2048:(17 + t) * 2048],
                                wv_i[128 * t:128 * (t + 1), :]).then_inc(ctx.h["dg"], 16)
                for tt in range(4):
                    g.dma_start(xtm[:, tt * 2048:(tt + 1) * 2048],
                                xkv[tt * 128:(tt + 1) * 128, :]).then_inc(ctx.h["dg"], 16)
                ctx.n["dg"] += 36 * 16
                g.wait_ge(ctx.h["dg"], ctx.n["dg"])


        tp_stack = contextlib.ExitStack()
        ps_tp = tp_stack.enter_context(nc.psum_tensor(pfx + "ps_tp_0", [128, 512], F32))
        ps_tp2 = tp_stack.enter_context(nc.psum_tensor(pfx + "ps_tp2_0", [128, 512], F32))
        nc.leave_named_scope("wkv", _sid_p0, False)
        if True:
            _sid_kv, _ = nc.enter_named_scope("kv", False)
            for sc in range(16):
                xb = xtm[:, 0:8192]

                def loadx(g, sc=sc):
                    if sc > 0:
                        for tt in range(4):
                            g.dma_start(xtm[:, tt * 2048:(tt + 1) * 2048],
                                        xkv[sc * 512 + tt * 128: sc * 512 + (tt + 1) * 128, :]
                                        ).then_inc(ctx.h["dg"], 16)

                dgw = ctx.n["dg"] + (4 * 16 if sc > 0 else 0)
                ctx.n["dg"] = dgw
                emit_ln_block(ctx, nc, xb, 4, stats, aggr, rstd, eps_col,
                              dg_wait=("dg", dgw),
                              extra_gp=loadx, scratch32=ctmp)
                emit_tm2fm(ctx, nc, xb, xfm, c_ident[:, :], ps_tp, ps_tp2)

                pe0, act0, st0 = ctx.n["pe"], ctx.n["act"], ctx.n["st"]


                with nc.Block() as blk:
                    @blk.tensor
                    def _(t, sc=sc, pe0=pe0, act0=act0):
                        for i in range(32):
                            ps = ps_a if i % 2 == 0 else ps_b
                            if i >= 2:
                                t.wait_ge(ctx.h["act"], act0 + i - 1)
                            ins = None
                            if i < 16:
                                dk = i
                                for dm in range(16):
                                    ins = t.matmul(ps[:, :],
                                                   wpool[:, dm * 2048 + dk * 128: dm * 2048 + (dk + 1) * 128],
                                                   xfm[:, dm * 512:(dm + 1) * 512],
                                                   start=(dm == 0), stop=(dm == 15))
                            else:
                                tt, dvc = (i - 16) // 4, (i - 16) % 4
                                for dm in range(16):
                                    ins = t.matmul(ps[:, :],
                                                   xfm[:, dm * 512 + tt * 128: dm * 512 + (tt + 1) * 128],
                                                   wpool[:, (16 + dm) * 2048 + dvc * 512: (16 + dm) * 2048 + (dvc + 1) * 512],
                                                   start=(dm == 0), stop=(dm == 15))
                            ins.then_inc(ctx.h["pe"], 1)

                    @blk.scalar
                    def _(s, sc=sc, pe0=pe0, act0=act0, st0=st0):
                        for i in range(32):
                            ps = ps_a if i % 2 == 0 else ps_b
                            s.wait_ge(ctx.h["pe"], pe0 + i + 1)
                            if i >= 4:
                                s.wait_ge(ctx.h["st"], st_grp(st0, (i - 4) // 4))
                            slot = kvev[:, (i % 4) * 512:(i % 4 + 1) * 512]
                            s.copy(slot, ps[:, :])
                            s.drain().then_inc(ctx.h["act"], 1)

                    @blk.sync
                    def _(sy, sc=sc, act0=act0, st0=st0):
                        for g in range(8):
                            sy.wait_ge(ctx.h["act"], act0 + 4 * (g + 1))
                            for i in range(4 * g, 4 * g + 4):
                                slot = kvev[:, (i % 4) * 512:(i % 4 + 1) * 512]
                                if i < 16:
                                    dkk = i
                                    dst = bass.AP(kspill, dkk * 128 * 8192 + 4 * sc * 128,
                                                  [[8192, 128], [1, 512]])
                                    sy.dma_start(dst, slot[:, :]).then_inc(ctx.h["st"], 16)
                                else:
                                    tt, dvc = (i - 16) // 4, (i - 16) % 4
                                    for u in range(4):
                                        hh = 4 * dvc + u
                                        dst = bass.AP(vspill,
                                                      hh * 128 * 8192 + (4 * sc + tt) * 128,
                                                      [[8192, 128], [1, 128]])
                                        sy.dma_start(dst, slot[:, u * 128:(u + 1) * 128]
                                                     ).then_inc(ctx.h["st"], 16)
                        sy.wait_ge(ctx.h["st"], st_grp(st0, 7))
                ctx.n["pe"] = pe0 + 32
                ctx.n["act"] = act0 + 32
                ctx.n["st"] = st_grp(st0, 7)

            nc.leave_named_scope("kv", _sid_kv, False)
            _sid_at, _ = nc.enter_named_scope("attn", False)
            for qt in range(4):
                ktm, kpn = KTMAX[qt], KT2[qt]

                dg_wq = ctx.n["dg"] + 20 * 16
                with nc.Block() as blk:
                    @blk.gpsimd
                    def _(g, qt=qt):
                        for t in range(16):
                            g.dma_start(wpool[:, WQO + t * 2048: WQO + (t + 1) * 2048],
                                        wq_i[128 * t:128 * (t + 1), :]).then_inc(ctx.h["dg"], 16)
                        for tt in range(4):
                            g.dma_start(xtm[:, tt * 2048:(tt + 1) * 2048],
                                        xown[qt * 512 + tt * 128: qt * 512 + (tt + 1) * 128, :]
                                        ).then_inc(ctx.h["dg"], 16)
                ctx.n["dg"] = dg_wq

                emit_ln_block(ctx, nc, xtm[:, 0:8192], 4, stats, aggr, rstd,
                              eps_col, dg_wait=("dg", dg_wq), scratch32=ctmp)
                if DEBUG_B0 and qt == 0:
                    with nc.Block() as blk:
                        @blk.vector
                        def _(v):
                            v.tensor_copy(xraw[:, 0:8], aggr[:, :])
                            v.tensor_copy(xraw[:, 8:12], rstd[:, :])
                            v.tensor_copy(xraw[:, 16:112], stats[:, :])
                        @blk.scalar
                        def _(s2):
                            pass
                    with nc.Block() as blk:
                        @blk.sync
                        def _(sy):
                            sy.dma_start(dbg_st.ap(), xraw[:, 0:112]).then_inc(ctx.h["st"], 16)
                            ctx.n["st"] += 16
                            sy.wait_ge(ctx.h["st"], ctx.n["st"])
                    for pp in range(4):
                        with nc.Block() as blk:
                            @blk.vector
                            def _(v, pp=pp):
                                v.tensor_copy(xraw[:, :], xtm[:, pp * 2048:(pp + 1) * 2048])
                            @blk.scalar
                            def _(s2):
                                pass
                        with nc.Block() as blk:
                            @blk.sync
                            def _(sy, pp=pp):
                                sy.dma_start(dbg_xtm.ap()[:, pp * 2048:(pp + 1) * 2048],
                                             xraw[:, :]).then_inc(ctx.h["st"], 16)
                                ctx.n["st"] += 16
                                sy.wait_ge(ctx.h["st"], ctx.n["st"])
                emit_tm2fm(ctx, nc, xtm[:, 0:8192], xfm, c_ident[:, :], ps_tp, ps_tp2)

                if DEBUG_B0 and qt == 0:
                    with nc.Block() as blk:
                        @blk.vector
                        def _(v):
                            v.tensor_copy(xraw[:, :], xfm[:, 0:2048])
                        @blk.scalar
                        def _(s2):
                            pass
                    with nc.Block() as blk:
                        @blk.sync
                        def _(sy):
                            sy.dma_start(dbg_xfm2.ap(), xraw[:, :]).then_inc(ctx.h["st"], 16)
                            ctx.n["st"] += 16
                            sy.wait_ge(ctx.h["st"], ctx.n["st"])

                ds_h0 = ctx.n["ds"] + 32
                with nc.Block() as blk:
                    @blk.sync
                    def _(sy, ktm=ktm):
                        sy.dma_start(wpool[:, KBUF[0] + 512:KBUF[0] + 512 + (ktm - 4) * 128],
                                     bass.AP(kspill, 0, [[8192, 128], [1, (ktm - 4) * 128]])
                                     ).then_inc(ctx.h["ds"], 16)
                        sy.dma_start(wpool[:, VBUF[0] + 512:VBUF[0] + 512 + (ktm - 4) * 128],
                                     bass.AP(vspill, 0, [[8192, 128], [1, (ktm - 4) * 128]])
                                     ).then_inc(ctx.h["ds"], 16)

                    @blk.gpsimd
                    def _(g, qt=qt):
                        g.indirect_dma_start(
                            wpool[:, KBUF[0]:KBUF[0] + 512], None,
                            bass.AP(kspill, 0, [[128, 131072], [1, 128]]),
                            IndirectOffsetOnAxis(c_kdiag[:, qt:qt + 1], 0),
                        ).then_inc(ctx.h["dk"], 16)
                        g.indirect_dma_start(
                            wpool[:, VBUF[0]:VBUF[0] + 512], None,
                            bass.AP(vspill, 0, [[128, 131072], [1, 128]]),
                            IndirectOffsetOnAxis(c_kdiag[:, qt:qt + 1], 0),
                        ).then_inc(ctx.h["dk"], 16)
                ctx.n["ds"] = ds_h0
                ctx.n["dk"] += 32
                dk_h0 = ctx.n["dk"]

                if DEBUG_B0 and qt == 0:
                    with nc.Block() as blk:
                        @blk.vector
                        def _(v):
                            v.wait_ge(ctx.h["ds"], ds_h0)
                            v.wait_ge(ctx.h["dk"], dk_h0)
                            v.tensor_copy(xraw[:, :], xfm[:, 0:2048])
                        @blk.scalar
                        def _(s2):
                            pass
                    with nc.Block() as blk:
                        @blk.sync
                        def _(sy):
                            sy.dma_start(dbg_xfm3.ap(), xraw[:, :]).then_inc(ctx.h["st"], 16)
                            ctx.n["st"] += 16
                            sy.wait_ge(ctx.h["st"], ctx.n["st"])

                emit_proj(ctx, nc, wpool, WQO, xfm, qctx, 16, ps_a, ps_b,
                          pre_waits=[("dg", dg_wq)])

                if DEBUG_B0 and qt == 0:
                    with nc.Block() as blk:
                        @blk.vector
                        def _(v):
                            v.wait_ge(ctx.h["ds"], ds_h0)
                            v.wait_ge(ctx.h["dk"], dk_h0)
                            v.tensor_copy(xraw[:, :], xfm[:, 0:2048])
                        @blk.sync
                        def _(sy):
                            pass
                    with nc.Block() as blk:
                        @blk.sync
                        def _(sy):
                            sy.dma_start(dbg_xfm.ap()[:, 0:2048], xraw[:, :]).then_inc(ctx.h["st"], 16)
                            ctx.n["st"] += 16
                            sy.wait_ge(ctx.h["st"], ctx.n["st"])
                    for part in range(8):
                        with nc.Block() as blk:
                            @blk.vector
                            def _(v, part=part):
                                v.tensor_copy(xraw[:, 0:1024], qctx[:, part * 1024:(part + 1) * 1024])
                            @blk.scalar
                            def _(s2):
                                pass
                        with nc.Block() as blk:
                            @blk.sync
                            def _(sy, part=part):
                                sy.dma_start(dbg_q.ap()[:, part * 1024:(part + 1) * 1024],
                                             xraw[:, 0:1024]).then_inc(ctx.h["st"], 16)
                                ctx.n["st"] += 16
                                sy.wait_ge(ctx.h["st"], ctx.n["st"])
                    for part in range(4):
                        with nc.Block() as blk:
                            @blk.vector
                            def _(v, part=part):
                                v.tensor_copy(xraw[:, :], wpool[:, KBUF[0] + part * 2048: KBUF[0] + (part + 1) * 2048])
                            @blk.scalar
                            def _(s2):
                                pass
                        with nc.Block() as blk:
                            @blk.sync
                            def _(sy, part=part):
                                sy.dma_start(dbg_kbuf.ap()[:, part * 2048:(part + 1) * 2048],
                                             xraw[:, :]).then_inc(ctx.h["st"], 16)
                                ctx.n["st"] += 16
                                sy.wait_ge(ctx.h["st"], ctx.n["st"])
                    for part in range(4):
                        with nc.Block() as blk:
                            @blk.vector
                            def _(v, part=part):
                                v.tensor_copy(xraw[:, :], wpool[:, VBUF[0] + part * 2048: VBUF[0] + (part + 1) * 2048])
                            @blk.scalar
                            def _(s2):
                                pass
                        with nc.Block() as blk:
                            @blk.sync
                            def _(sy, part=part):
                                sy.dma_start(dbg_vbuf.ap()[:, part * 2048:(part + 1) * 2048],
                                             xraw[:, :]).then_inc(ctx.h["st"], 16)
                                ctx.n["st"] += 16
                                sy.wait_ge(ctx.h["st"], ctx.n["st"])

                dg_wo = ctx.n["dg"] + 16 * 16
                with nc.Block() as blk:
                    @blk.gpsimd
                    def _(g):
                        for t in range(16):
                            g.dma_start(wpool[:, WQO + t * 2048: WQO + (t + 1) * 2048],
                                        wo_i[128 * t:128 * (t + 1), :]).then_inc(ctx.h["dg"], 16)
                ctx.n["dg"] = dg_wo

                tp_stack.close()
                hd_stack = contextlib.ExitStack()
                ps_s0 = hd_stack.enter_context(nc.psum_tensor(pfx + "ps_s0_%d" % qt, [128, 1024], F32))
                ps_s1 = hd_stack.enter_context(nc.psum_tensor(pfx + "ps_s1_%d" % qt, [128, 1024], F32))
                ps_o = hd_stack.enter_context(nc.psum_tensor(pfx + "ps_o_%d" % qt, [128, 512], F32))
                ps_l = hd_stack.enter_context(nc.psum_tensor(pfx + "ps_l_%d" % qt, [1, 512], F32))
                if True:
                    for hh in range(16):
                        kb, vb = KBUF[hh % 2], VBUF[hh % 2]
                        pe0, act0, dve0 = ctx.n["pe"], ctx.n["act"], ctx.n["dve"]
                        ds_need = ds_h0 + 32 * hh
                        dk_need = dk_h0 + 32 * hh
                        with nc.Block() as blk:
                            @blk.sync
                            def _(sy, hh=hh, ktm=ktm, pe0=pe0):
                                if hh < 15:
                                    sy.wait_ge(ctx.h["pe"], pe0 + 1)
                                    nkb, nvb = KBUF[(hh + 1) % 2], VBUF[(hh + 1) % 2]
                                    sy.dma_start(wpool[:, nkb + 512:nkb + 512 + (ktm - 4) * 128],
                                                 bass.AP(kspill, (hh + 1) * 128 * 8192,
                                                         [[8192, 128], [1, (ktm - 4) * 128]])
                                                 ).then_inc(ctx.h["ds"], 16)
                                    sy.dma_start(wpool[:, nvb + 512:nvb + 512 + (ktm - 4) * 128],
                                                 bass.AP(vspill, (hh + 1) * 128 * 8192,
                                                         [[8192, 128], [1, (ktm - 4) * 128]])
                                                 ).then_inc(ctx.h["ds"], 16)

                            @blk.gpsimd
                            def _(g, qt=qt, hh=hh, pe0=pe0):
                                if hh < 15:
                                    nkb, nvb = KBUF[(hh + 1) % 2], VBUF[(hh + 1) % 2]
                                    g.wait_ge(ctx.h["pe"], pe0 + 1)
                                    g.indirect_dma_start(
                                        wpool[:, nkb:nkb + 512], None,
                                        bass.AP(kspill, 0, [[128, 131072], [1, 128]]),
                                        IndirectOffsetOnAxis(c_kdiag[:, qt:qt + 1], 0),
                                        element_offset=(hh + 1) * 1048576,
                                    ).then_inc(ctx.h["dk"], 16)
                                    g.indirect_dma_start(
                                        wpool[:, nvb:nvb + 512], None,
                                        bass.AP(vspill, 0, [[128, 131072], [1, 128]]),
                                        IndirectOffsetOnAxis(c_kdiag[:, qt:qt + 1], 0),
                                        element_offset=(hh + 1) * 1048576,
                                    ).then_inc(ctx.h["dk"], 16)

                            @blk.tensor
                            def _(t, hh=hh, kb=kb, vb=vb, kpn=kpn,
                                  pe0=pe0, act0=act0, dve0=dve0,
                                  ds_need=ds_need, dk_need=dk_need):
                                t.wait_ge(ctx.h["ds"], ds_need)
                                t.wait_ge(ctx.h["dk"], dk_need)
                                for kp in range(kpn):
                                    ps_s = ps_s0 if kp % 2 == 0 else ps_s1
                                    if kp >= 2:
                                        t.wait_ge(ctx.h["act"], act0 + kp - 1)
                                    ins = None
                                    for half in range(2):
                                        co = kp * 256 + half * 128
                                        ins = t.matmul(ps_s[:, half * 512:(half + 1) * 512],
                                                       wpool[:, kb + co: kb + co + 128],
                                                       qctx[:, hh * 512:(hh + 1) * 512],
                                                       start=True, stop=True)
                                    ins.then_inc(ctx.h["pe"], 1)
                                    if kp < 2:
                                        t.wait_ge(ctx.h["dve"], dve0 + kp + 1)
                                    else:
                                        t.wait_ge(ctx.h["act"], act0 + kp + 1)
                                    em = exppm if kp < 2 else expp
                                    for half in range(2):
                                        co = kp * 256 + half * 128
                                        ins = t.matmul(ps_o[:, :],
                                                       wpool[:, vb + co: vb + co + 128],
                                                       em[:, half * 512:(half + 1) * 512],
                                                       start=(kp == 0 and half == 0),
                                                       stop=(kp == kpn - 1 and half == 1))
                                    ins.then_inc(ctx.h["pe"], 1)
                                t.wait_ge(ctx.h["dve"], dve0 + kpn)
                                t.matmul(ps_l[:, :], c_ones[:, :], lpart[:, :],
                                         start=True, stop=True)
                                t.drain().then_inc(ctx.h["pe"], 1)
                                t.wait_ge(ctx.h["dve"], dve0 + kpn + 1)
                                t.matmul(ps_a[:, :], c_onesr[:, :], linv[:, :],
                                         start=True, stop=True)
                                t.drain().then_inc(ctx.h["pe"], 1)

                            @blk.scalar
                            def _(s, qt=qt, kpn=kpn, pe0=pe0, act0=act0, dve0=dve0):
                                col0 = job2_col(qt, 0)
                                for kp in range(kpn):
                                    ps_s = ps_s0 if kp % 2 == 0 else ps_s1
                                    s.wait_ge(ctx.h["pe"], pe0 + 2 * kp + 1)
                                    if kp >= 1:
                                        s.wait_ge(ctx.h["dve"], dve0 + kp)
                                    s.activation(expp[:, :], ps_s[:, :], AF.Exp,
                                                 bias=c_bias[:, col0 + kp:col0 + kp + 1],
                                                 scale=c_scl[:, 0:1])
                                    s.drain().then_inc(ctx.h["act"], 1)
                                s.wait_ge(ctx.h["pe"], pe0 + 2 * kpn)
                                s.copy(ctmp[:, :], ps_o[:, :])
                                s.drain().then_inc(ctx.h["act"], 1)

                            @blk.vector
                            def _(v, qt=qt, hh=hh, kpn=kpn, pe0=pe0, act0=act0, dve0=dve0):
                                col0 = job2_col(qt, 0)
                                v.memset(lpart[:, :], 0.0)
                                v.drain()

                                for kp in range(kpn):
                                    v.wait_ge(ctx.h["act"], act0 + kp + 1)
                                    if kp < 2:
                                        if kp >= 1:
                                            v.wait_ge(ctx.h["pe"], pe0 + 2 * kp)
                                        v.tensor_tensor(exppm[:, :], expp[:, :],
                                                        c_masks[:, kp * 1024:(kp + 1) * 1024],
                                                        ALU.mult)
                                        v.tensor_add(lpart[:, :], lpart[:, :], exppm[:, 0:512])
                                        v.tensor_add(lpart[:, :], lpart[:, :], exppm[:, 512:1024])
                                    else:
                                        v.tensor_add(lpart[:, :], lpart[:, :], expp[:, 0:512])
                                        v.tensor_add(lpart[:, :], lpart[:, :], expp[:, 512:1024])
                                    v.drain().then_inc(ctx.h["dve"], 1)
                                v.wait_ge(ctx.h["pe"], pe0 + 2 * kpn + 1)
                                v.reciprocal(linv[:, :], ps_l[:, :])
                                v.drain().then_inc(ctx.h["dve"], 1)
                                v.wait_ge(ctx.h["pe"], pe0 + 2 * kpn + 2)
                                v.wait_ge(ctx.h["act"], act0 + kpn + 1)
                                v.tensor_mul(qctx[:, hh * 512:(hh + 1) * 512],
                                             ctmp[:, :], ps_a[:, :])
                                v.drain().then_inc(ctx.h["dve"], 1)

                        ctx.n["pe"] = pe0 + 2 * kpn + 2
                        ctx.n["act"] = act0 + kpn + 1
                        ctx.n["dve"] = dve0 + kpn + 2
                        if hh < 15:
                            ctx.n["ds"] += 32
                            ctx.n["dk"] += 32

                hd_stack.close()
                tp_stack = contextlib.ExitStack()
                ps_tp = tp_stack.enter_context(nc.psum_tensor(pfx + "ps_tp_%d" % (qt + 1), [128, 512], F32))
                ps_tp2 = tp_stack.enter_context(nc.psum_tensor(pfx + "ps_tp2_%d" % (qt + 1), [128, 512], F32))
                emit_proj(ctx, nc, wpool, WQO, qctx, yfm, 16, ps_a, ps_b,
                          bias_sb=c_bo, act_func=AF.Identity,
                          pre_waits=[("dg", dg_wo)])

                if DEBUG_B0 and qt == 0:
                    for part in range(8):
                        with nc.Block() as blk:
                            @blk.vector
                            def _(v, part=part):
                                v.tensor_copy(xraw[:, 0:1024], qctx[:, part * 1024:(part + 1) * 1024])
                            @blk.scalar
                            def _(s2):
                                pass
                        with nc.Block() as blk:
                            @blk.sync
                            def _(sy, part=part):
                                sy.dma_start(dbg_ctx.ap()[:, part * 1024:(part + 1) * 1024],
                                             xraw[:, 0:1024]).then_inc(ctx.h["st"], 16)
                                ctx.n["st"] += 16
                                sy.wait_ge(ctx.h["st"], ctx.n["st"])
                    for part in range(8):
                        with nc.Block() as blk:
                            @blk.vector
                            def _(v, part=part):
                                v.tensor_copy(xraw[:, 0:1024], yfm[:, part * 1024:(part + 1) * 1024])
                            @blk.scalar
                            def _(s2):
                                pass
                        with nc.Block() as blk:
                            @blk.sync
                            def _(sy, part=part):
                                sy.dma_start(dbg_y.ap()[:, part * 1024:(part + 1) * 1024],
                                             xraw[:, 0:1024]).then_inc(ctx.h["st"], 16)
                                ctx.n["st"] += 16
                                sy.wait_ge(ctx.h["st"], ctx.n["st"])

                for tt in range(4):
                    pairs, evacs = [], []
                    for g in range(4):
                        ps = ps_tp if g % 2 == 0 else ps_tp2
                        grp = []
                        for k4 in range(4):
                            dt = g * 4 + k4
                            grp.append((ps[:, k4 * 128:(k4 + 1) * 128],
                                        yfm[:, dt * 512 + tt * 128: dt * 512 + (tt + 1) * 128]))
                        pairs.append(grp)
                        evacs.append((ytm[:, g * 512:(g + 1) * 512], ps[:, :]))
                    emit_transpose(ctx, nc, pairs, c_ident[:, :], evacs)
                    ds_xr = ctx.n["ds"] + 16

                    dve0e, act0e = ctx.n["dve"], ctx.n["act"]
                    with nc.Block() as blk:
                        @blk.vector
                        def _(v, qt=qt, tt=tt, dve0e=dve0e, act0e=act0e, ds_xr=ds_xr):
                            ln_vec_emit(ctx, v, ytm[:, 0:2048], 1, stats, aggr, rstd,
                                        dve0e, act0e, out_ap=lny[:, 0:2048],
                                        scratch32=ctmp)
                            v.wait_ge(ctx.h["ds"], ds_xr)
                            v.tensor_add(xraw[:, :], xraw[:, :], lny[:, 0:2048])
                            v.drain().then_inc(ctx.h["dve"], 1)

                        @blk.scalar
                        def _(s, dve0e=dve0e, act0e=act0e):
                            ln_scl_emit(ctx, s, 1, aggr, rstd, eps_col, dve0e, act0e)

                        @blk.sync
                        def _(sy, qt=qt, tt=tt, dve0e=dve0e):
                            sy.dma_start(xraw[:, :],
                                         xown[(qt * 4 + tt) * 128:(qt * 4 + tt + 1) * 128, :]
                                         ).then_inc(ctx.h["ds"], 16)
                            sy.wait_ge(ctx.h["dve"], dve0e + 3)
                            sy.dma_start(resid1.ap()[(qt * 4 + tt) * 128:(qt * 4 + tt + 1) * 128, :],
                                         xraw[:, :]).then_inc(ctx.h["st"], 16)
                            sy.wait_ge(ctx.h["st"], ctx.n["st"] + 16)
                    ctx.n["ds"] = ds_xr
                    ctx.n["dve"] = dve0e + 3
                    ctx.n["act"] = act0e + 1
                    ctx.n["st"] += 16

        tp_stack.close()
        with nc.Block() as blk:
            @blk.sync
            def _(sy):
                sy.wait_ge(ctx.h["st"], ctx.n["st"])
        nc.leave_named_scope("attn", _sid_at, False)

    stack.close()
    if own:
        ctx.close()
    return nc



def _prep_b0_inmaps(inputs):
    x = np.asarray(inputs["L0_attn_x"], dtype=np.float32)
    wq = np.ascontiguousarray(np.asarray(inputs["L0_attn_wq"], dtype=np.float32))
    wk = np.ascontiguousarray(np.asarray(inputs["L0_attn_wk"], dtype=np.float32))
    wv = np.ascontiguousarray(np.asarray(inputs["L0_attn_wv"], dtype=np.float32))
    wo = np.ascontiguousarray(np.asarray(inputs["L0_attn_wo"], dtype=np.float32))
    bo = np.asarray(inputs["L0_attn_bo"], dtype=np.float32)
    bo_t = np.ascontiguousarray(bo.reshape(16, 128).T).astype(np.float32)
    ident = np.eye(128, dtype=BF)
    masks = host_causal_masks_paired()
    scl = host_scl()
    ones32 = np.ones((128, 1), dtype=np.float32)
    onesr32 = np.ones((1, 128), dtype=np.float32)

    in_maps = []
    for c in range(NCORE):
        b = c // 4
        blkA, blkB = core_blocks(c)
        xb = np.ascontiguousarray(x[b])
        xo = np.ascontiguousarray(np.concatenate(
            [xb[1024 * blkA:1024 * blkA + 1024], xb[1024 * blkB:1024 * blkB + 1024]], axis=0))
        bias_tbl, kdiag = host_b0_tables(c)
        in_maps.append({
            "xkv": xb, "xown": xo,
            "wq": wq, "wk": wk, "wv": wv, "wo": wo, "bo_t": bo_t,
            "ident": ident, "masks": masks,
            "bias_tbl": bias_tbl, "kdiag": kdiag,
            "scl": scl, "ones32": ones32, "onesr32": onesr32,
        })
    return in_maps


def run_b0(inputs):
    nc = build_b0()
    in_maps = _prep_b0_inmaps(inputs)
    res = run_bass_kernel_spmd(nc, in_maps, core_ids=list(range(8)))
    if DEBUG_B0:
        return res.results
    return [np.asarray(r["resid1"]) for r in res.results]


def assemble(resids):
    out = np.zeros((B, S, D), dtype=np.float32)
    for c in range(NCORE):
        b = c // 4
        blkA, blkB = core_blocks(c)
        out[b, 1024 * blkA:1024 * blkA + 1024] = resids[c][0:1024]
        out[b, 1024 * blkB:1024 * blkB + 1024] = resids[c][1024:2048]
    return out


def run_b1(inputs, resids1):
    nc = build_b1()
    in_maps = _prep_b1_inmaps(inputs, resids1)
    res = run_bass_kernel_spmd(nc, in_maps, core_ids=list(range(8)))
    return [np.asarray(r["resid_o"]) for r in res.results]


N_LAUNCH = 1


def run_fused(inputs):
    nc = build_fused()
    in_maps = _prep_fused_inmaps(inputs)
    res = run_bass_kernel_spmd(nc, in_maps, core_ids=list(range(8)))
    return [np.asarray(r["resid_o"]) for r in res.results]


def kernel(**inputs):
    if N_LAUNCH == 1:
        return assemble(run_fused(inputs))
    resids = run_b0(inputs)
    if N_LAUNCH >= 2:
        resids = run_b1(inputs, resids)
    if N_LAUNCH >= 3:
        resids = run_b2(inputs, resids)
    if N_LAUNCH >= 4:
        resids = run_b3(inputs, resids)
    return assemble(resids)



B1CUT = 99
B1_ACT = AF.Gelu
B1_HN = 99
B1_EVAC = 1



def _b1_finish(nc, ctx, stack):
    with nc.Block() as blk:
        @blk.sync
        def _(sy):
            sy.wait_ge(ctx.h["st"], ctx.n["st"])
    stack.close()
    if own:
        ctx.close()
    return nc


def _cut(n):
    return B1CUT < n


def build_b1(nc=None, ctx=None, pfx="", io=None):
    own = nc is None
    if own:
        nc = bass.Bass("TRN2", target_bir_lowering=False)
        ctx = Ctx(nc)

    resid_i = _io(io, "resid", lambda: nc.dram_tensor("resid", [T, D], F32, kind="ExternalInput"))
    gw_i = nc.dram_tensor("gate_w", [6, D], F32, kind="ExternalInput")
    w1_i = nc.dram_tensor("w1", [6 * 2048, 1408], F32, kind="ExternalInput")
    b1c_i = nc.dram_tensor("b1c", [128, 66], F32, kind="ExternalInput")
    w2_i = nc.dram_tensor("w2", [6 * 1408, 2048], F32, kind="ExternalInput")
    b2c_i = nc.dram_tensor("b2c", [128, 96], F32, kind="ExternalInput")
    ident_i = _io(io, "ident", lambda: nc.dram_tensor("ident", [128, 128], BF16, kind="ExternalInput"))
    ident32_i = nc.dram_tensor("ident32", [128, 128], F32, kind="ExternalInput")
    scl_i = _io(io, "scl", lambda: nc.dram_tensor("scl", [128, 8], F32, kind="ExternalInput"))
    onesr_i = _io(io, "onesr32", lambda: nc.dram_tensor("onesr32", [1, 128], F32, kind="ExternalInput"))
    iota_i = nc.dram_tensor("iota", [128, 1], F32, kind="ExternalInput")
    bsel_i = nc.dram_tensor("bsel", [128, 2], F32, kind="ExternalInput")

    cc_in = nc.dram_tensor(pfx + "cc_in", [128, 32], F32)
    cc_out = nc.dram_tensor(pfx + "cc_out", [128, 32], F32)
    y0sp = nc.dram_tensor(pfx + "y0sp", [64 * 128, 512], BF16)

    resid_o = _io(io, "resid_out", lambda: nc.dram_tensor("resid_o", [T, D], F32, kind="ExternalOutput"))

    stack = contextlib.ExitStack()

    def sb(name, shape, dt):
        return stack.enter_context(nc.sbuf_tensor(pfx + name, shape, dt))

    c_ident = sb("c_ident", [128, 128], BF16)
    c_id32 = sb("c_id32", [128, 128], F32)
    c_scl = sb("c_scl", [128, 8], F32)
    c_onesr = sb("c_onesr", [1, 128], F32)
    c_iota = sb("c_iota", [128, 1], F32)
    c_bsel = sb("c_bsel", [128, 2], F32)
    xfm = sb("xfm", [128, 16 * 2048], BF16)
    wbuf = sb("wbuf", [128, 16 * 1408], BF16)
    hbuf = sb("hbuf", [128, 11 * 2048], BF16)
    rbuf = sb("rbuf", [128, 2 * 2048], F32)
    tbuf = sb("tbuf", [128, 2 * 2048], BF16)
    gbuf = sb("gbuf", [128, 2048], F32)
    yfm = sb("yfm", [128, 16 * 512], BF16)
    y0ld = xfm[:, 0:16 * 512]
    ytm = tbuf[:, 0:2048]
    lny = tbuf[:, 2048:4096]
    xraw = rbuf[:, 0:2048]
    ctmp = sb("ctmp", [128, 512], F32)
    stats = sb("stats", [128, 96], F32)
    aggr = sb("aggr", [128, 8], F32)
    rstd = sb("rstd", [128, 4], F32)
    xbar = sb("xbar", [128, 48], F32)
    gwT = sb("gwT", [128, 96], F32)
    gsm = sb("gsm", [1, 96], F32)
    b1all = sb("b1all", [128, 66], F32)
    b2all = sb("b2all", [128, 96], F32)
    indc = sb("indc", [128, 16], F32)
    gidx1 = sb("gidx1", [1, 8], mybir.dt.uint32)
    gidx2 = sb("gidx2", [1, 8], mybir.dt.uint32)
    idxt = sb("idxt", [128, 2], I32)
    brow = gbuf[0:1, 0:2048]
    bcol = sb("bcol", [128, 40], F32)
    ps_a = stack.enter_context(nc.psum_tensor(pfx + "ps_a", [128, 512], F32))
    ps_b = stack.enter_context(nc.psum_tensor(pfx + "ps_b", [128, 512], F32))
    ps_tp = stack.enter_context(nc.psum_tensor(pfx + "ps_tp", [128, 512], F32))
    ps_tp2 = stack.enter_context(nc.psum_tensor(pfx + "ps_tp2", [128, 512], F32))
    ps_s = stack.enter_context(nc.psum_tensor(pfx + "ps_s", [128, 16], F32))
    ps_g = stack.enter_context(nc.psum_tensor(pfx + "ps_g", [1, 16], F32))
    eps_col = c_scl[:, 1:2]

    with nc.Block() as blk:
        @blk.sync
        def _(sy):
            for dst, src in [(c_ident, ident_i), (c_id32, ident32_i),
                             (c_scl, scl_i), (c_onesr, onesr_i),
                             (c_iota, iota_i), (c_bsel, bsel_i),
                             (b1all, b1c_i), (b2all, b2c_i)]:
                sy.dma_start(dst[:, :], src.ap()).then_inc(ctx.h["ds"], 16)
            ctx.n["ds"] += 8 * 16
            sy.wait_ge(ctx.h["ds"], ctx.n["ds"])

    if _cut(1):
        return _b1_finish(nc, ctx, stack)
    emit_xln_phase(ctx, nc, resid_i, 0, T,
                   lambda g, dt: xfm[:, dt * 2048 + g * 256: dt * 2048 + (g + 1) * 256],
                   rbuf, tbuf, stats, aggr, rstd, eps_col, c_ident[:, :],
                   ps_tp, ps_tp2, ntt=2)

    if _cut(2):
        return _b1_finish(nc, ctx, stack)
    with nc.Block() as blk:
        @blk.vector
        def _(v):
            for dm in range(16):
                v.tensor_reduce(xbar[:, dm:dm + 1], xfm[:, dm * 2048:(dm + 1) * 2048],
                                mybir.AxisListType.X, ALU.add)
                v.drain()
            v.tensor_scalar(xbar[:, 32:48], xbar[:, 0:16], c_bsel[:, 1:2], None, ALU.mult)
            v.drain()
            v.tensor_scalar(xbar[:, 16:32], xbar[:, 0:16], c_bsel[:, 0:1], None, ALU.mult)
            v.drain().then_inc(ctx.h["dve"], 1)
            ctx.n["dve"] += 1

        @blk.scalar
        def _(s):
            pass

    if _cut(3):
        return _b1_finish(nc, ctx, stack)
    with nc.Block() as blk:
        @blk.gpsimd
        def _(g):
            g.wait_ge(ctx.h["dve"], ctx.n["dve"])
            g.dma_start(cc_in[:, :], xbar[:, 16:48]).then_inc(ctx.h["dg"], 16)
            ctx.n["dg"] += 16
            g.wait_ge(ctx.h["dg"], ctx.n["dg"])
            g.collective_compute(
                "AllReduce", ALU.add, replica_groups=[list(range(8))],
                ins=[cc_in.ap().opt()], outs=[cc_out.ap().opt()],
            ).then_inc(ctx.h["cc"], 1)
            ctx.n["cc"] += 1
            g.wait_ge(ctx.h["cc"], ctx.n["cc"])
            g.dma_start(xbar[:, 16:48], cc_out[:, :]).then_inc(ctx.h["dg"], 16)
            ctx.n["dg"] += 16
            g.wait_ge(ctx.h["dg"], ctx.n["dg"])

    if _cut(4):
        return _b1_finish(nc, ctx, stack)
    with nc.Block() as blk:
        @blk.vector
        def _(v):
            v.tensor_scalar(xbar[:, 0:16], xbar[:, 16:32], c_bsel[:, 0:1], None, ALU.mult)
            v.drain()
            v.scalar_tensor_tensor(xbar[:, 0:16], xbar[:, 32:48], c_bsel[:, 1:2],
                                   xbar[:, 0:16], ALU.mult, ALU.add)
            v.drain()
            v.tensor_scalar(xbar[:, 0:16], xbar[:, 0:16], c_scl[:, 2:3], None, ALU.mult)
            v.drain().then_inc(ctx.h["dve"], 1)
            ctx.n["dve"] += 1

        @blk.scalar
        def _(s):
            pass

    if _cut(5):
        return _b1_finish(nc, ctx, stack)
    with nc.Block() as blk:
        @blk.sync
        def _(sy):
            sy.dma_start(gbuf[0:6, 0:2048], gw_i[:, :]).then_inc(ctx.h["ds"], 16)
            ctx.n["ds"] += 16
            sy.wait_ge(ctx.h["ds"], ctx.n["ds"])

    pairs = []
    evacs = []
    for dm in range(16):
        ps = ps_tp if dm % 2 == 0 else ps_tp2
        pairs.append([(ps[:, 0:6], gbuf[0:6, dm * 128:(dm + 1) * 128])])
        evacs.append((gwT[:, dm * 6:(dm + 1) * 6], ps[:, 0:6]))
    emit_transpose(ctx, nc, pairs, c_id32[0:6, 0:6], evacs)

    pe0 = ctx.n["pe"]
    with nc.Block() as blk:
        @blk.tensor
        def _(t):
            t.wait_ge(ctx.h["dve"], ctx.n["dve"])
            for dm in range(16):
                t.matmul(ps_g[:, 0:6], xbar[:, dm:dm + 1], gwT[:, dm * 6:(dm + 1) * 6],
                         start=(dm == 0), stop=(dm == 15))
            t.drain().then_inc(ctx.h["pe"], 1)
    ctx.n["pe"] = pe0 + 1

    if _cut(7):
        return _b1_finish(nc, ctx, stack)
    dve0, act0 = ctx.n["dve"], ctx.n["act"]
    with nc.Block() as blk:
        @blk.vector
        def _(v, dve0=dve0, act0=act0):
            v.memset(gsm[:, 0:8], NEG_BIG)
            v.drain()
            v.wait_ge(ctx.h["pe"], ctx.n["pe"])
            v.tensor_copy(gsm[:, 0:6], ps_g[:, 0:6])
            v.drain()
            v.reduce_max(gsm[:, 8:9], gsm[:, 0:8], mybir.AxisListType.X)
            v.drain()
            v.tensor_scalar(gsm[:, 16:24], gsm[:, 0:8], gsm[:, 8:9], None, ALU.subtract)
            v.drain().then_inc(ctx.h["dve"], 1)
            v.wait_ge(ctx.h["act"], act0 + 1)
            v.reduce_sum(gsm[:, 9:10], gsm[:, 24:32], mybir.AxisListType.X)
            v.drain()
            v.reciprocal(gsm[:, 10:11], gsm[:, 9:10])
            v.drain()
            v.tensor_scalar(gsm[:, 24:32], gsm[:, 24:32], gsm[:, 10:11], None, ALU.mult)
            v.drain()
            v.reduce_max(gsm[:, 32:33], gsm[:, 24:32], mybir.AxisListType.X)
            v.drain()
            v.tensor_copy(gsm[:, 33:40], gsm[:, 32:33].broadcast_to((1, 7)))
            v.drain()
            v.max_index(gidx1[:, :], gsm[:, 32:40], gsm[:, 24:32])
            v.drain()
            v.match_replace(gsm[:, 40:48], gsm[:, 32:40], gsm[:, 24:32], -1e30)
            v.drain()
            v.reduce_max(gsm[:, 48:49], gsm[:, 40:48], mybir.AxisListType.X)
            v.drain()
            v.tensor_copy(gsm[:, 49:56], gsm[:, 48:49].broadcast_to((1, 7)))
            v.drain()
            v.max_index(gidx2[:, :], gsm[:, 48:56], gsm[:, 40:48])
            v.drain()
            v.tensor_scalar(gsm[:, 56:57], gsm[:, 32:33], gsm[:, 48:49], None, ALU.add)
            v.drain()
            v.reciprocal(gsm[:, 57:58], gsm[:, 56:57])
            v.drain()
            v.tensor_scalar(gsm[:, 58:59], gsm[:, 32:33], gsm[:, 57:58], None, ALU.mult)
            v.drain()
            v.tensor_scalar(gsm[:, 59:60], gsm[:, 48:49], gsm[:, 57:58], None, ALU.mult)
            v.drain()
            v.tensor_scalar(gsm[:, 64:72], gsm[:, 24:32], gsm[:, 32:33], None, ALU.is_ge)
            v.drain()
            v.tensor_scalar(gsm[:, 72:80], gsm[:, 40:48], gsm[:, 48:49], None, ALU.is_ge)
            v.drain().then_inc(ctx.h["dve"], 1)

        @blk.scalar
        def _(s, dve0=dve0, act0=act0):
            s.wait_ge(ctx.h["dve"], dve0 + 1)
            s.activation(gsm[:, 24:32], gsm[:, 16:24], AF.Exp)
            s.drain().then_inc(ctx.h["act"], 1)
    ctx.n["dve"] = dve0 + 2
    ctx.n["act"] = act0 + 1


    if _cut(8):
        return _b1_finish(nc, ctx, stack)
    pe0 = ctx.n["pe"]
    act0 = ctx.n["act"]
    dve0 = ctx.n["dve"]
    with nc.Block() as blk:
        @blk.tensor
        def _(t, pe0=pe0):
            for e in range(6):
                t.matmul(ps_s[:, e:e + 1], c_onesr[:, :], gsm[:, 64 + e:65 + e],
                         start=True, stop=True)
                t.matmul(ps_s[:, 6 + e:7 + e], c_onesr[:, :], gsm[:, 72 + e:73 + e],
                         start=True, stop=True)
            t.matmul(ps_s[:, 12:13], c_onesr[:, :], gsm[:, 58:59], start=True, stop=True)
            t.matmul(ps_s[:, 13:14], c_onesr[:, :], gsm[:, 59:60], start=True, stop=True)
            t.drain().then_inc(ctx.h["pe"], 1)

        @blk.scalar
        def _(s, pe0=pe0):
            s.wait_ge(ctx.h["pe"], pe0 + 1)
            s.copy(indc[:, 0:14], ps_s[:, 0:14])
            s.drain().then_inc(ctx.h["act"], 1)
    ctx.n["pe"] = pe0 + 1
    ctx.n["act"] = act0 + 1

    with nc.Block() as blk:
        @blk.vector
        def _(v, dve0=dve0):
            v.tensor_scalar(bcol[:, 34:40], indc[:, 0:6], indc[:, 12:13], None, ALU.mult)
            v.drain()
            v.scalar_tensor_tensor(bcol[:, 34:40], indc[:, 6:12], indc[:, 13:14],
                                   bcol[:, 34:40], ALU.mult, ALU.add)
            v.drain()
            v.memset(bcol[:, 16:32], 0.0)
            v.drain()
            for e in range(6):
                v.scalar_tensor_tensor(bcol[:, 16:32], b2all[:, e * 16:(e + 1) * 16],
                                       bcol[:, 34 + e:35 + e], bcol[:, 16:32],
                                       ALU.mult, ALU.add)
                v.drain()
            v.drain().then_inc(ctx.h["dve"], 1)

        @blk.scalar
        def _(s):
            pass
    ctx.n["dve"] = dve0 + 1

    if _cut(10):
        return _b1_finish(nc, ctx, stack)
    for sidx, gidx in [(0, gidx1), (1, gidx2)]:
        pe0 = ctx.n["pe"]
        dve0 = ctx.n["dve"]
        with nc.Block() as blk:
            @blk.vector
            def _(v, sidx=sidx, gidx=gidx, pe0=pe0):
                v.tensor_copy(gsm[:, 60:61], gidx[:, 0:1])
                v.drain()
                v.tensor_scalar(gsm[:, 61:62], gsm[:, 60:61], 2048.0, None, ALU.mult)
                v.drain()
                v.tensor_scalar(gsm[:, 62:63], gsm[:, 60:61], 1408.0, None, ALU.mult)
                v.drain()
                v.drain().then_inc(ctx.h["dve"], 1)
                v.wait_ge(ctx.h["pe"], pe0 + 1)
                v.tensor_tensor(idxt[:, 0:1], ps_a[:, 0:1], c_iota[:, :], ALU.add)
                v.drain()
                v.tensor_tensor(idxt[:, 1:2], ps_b[:, 0:1], c_iota[:, :], ALU.add)
                v.drain().then_inc(ctx.h["dve"], 1)

            @blk.tensor
            def _(t, sidx=sidx, dve0=dve0):
                t.wait_ge(ctx.h["dve"], dve0 + 1)
                t.matmul(ps_a[:, 0:1], c_onesr[:, :], gsm[:, 61:62], start=True, stop=True)
                t.matmul(ps_b[:, 0:1], c_onesr[:, :], gsm[:, 62:63], start=True, stop=True)
                t.matmul(ps_s[:, 0:1], c_onesr[:, :], gsm[:, 58 + sidx:59 + sidx],
                         start=True, stop=True)
                t.drain().then_inc(ctx.h["pe"], 1)
        ctx.n["pe"] = pe0 + 1
        ctx.n["dve"] = dve0 + 2

        if _cut(11 + 10 * sidx):
            return _b1_finish(nc, ctx, stack)
        act0 = ctx.n["act"]
        dve0 = ctx.n["dve"]
        with nc.Block() as blk:
            @blk.scalar
            def _(s, sidx=sidx, act0=act0):
                s.copy(bcol[:, 32 + sidx:33 + sidx], ps_s[:, 0:1])
                s.drain().then_inc(ctx.h["act"], 1)

            @blk.vector
            def _(v, sidx=sidx, dve0=dve0):
                v.memset(bcol[:, 0:11], 0.0)
                v.drain()
                for e in range(6):
                    icol = indc[:, e:e + 1] if sidx == 0 else indc[:, 6 + e:7 + e]
                    v.scalar_tensor_tensor(bcol[:, 0:11], b1all[:, e * 11:(e + 1) * 11],
                                           icol, bcol[:, 0:11], ALU.mult, ALU.add)
                    v.drain()
                v.drain().then_inc(ctx.h["dve"], 1)
        ctx.n["act"] = act0 + 1
        ctx.n["dve"] = dve0 + 1

        if _cut(12 + 10 * sidx):
            return _b1_finish(nc, ctx, stack)
        for dm in range(16):
            dk0 = ctx.n["dk"]
            dve0 = ctx.n["dve"]
            with nc.Block() as blk:
                @blk.gpsimd
                def _(g, dm=dm, dk0=dk0):
                    g.indirect_dma_start(
                        gbuf[:, 0:1408], None, w1_i.ap(),
                        IndirectOffsetOnAxis(idxt[:, 0:1], 0),
                        element_offset=dm * 128 * 1408,
                    ).then_inc(ctx.h["dk"], 16)

                @blk.vector
                def _(v, dm=dm, dk0=dk0):
                    v.wait_ge(ctx.h["dk"], dk0 + 16)
                    v.tensor_copy(wbuf[:, dm * 1408:(dm + 1) * 1408], gbuf[:, 0:1408])
                    v.drain().then_inc(ctx.h["dve"], 1)
            ctx.n["dk"] = dk0 + 16
            ctx.n["dve"] = dve0 + 1

        if _cut(13 + 10 * sidx):
            return _b1_finish(nc, ctx, stack)
        pe0 = ctx.n["pe"]
        act0 = ctx.n["act"]
        with nc.Block() as blk:
            @blk.tensor
            def _(t, pe0=pe0, act0=act0):
                i = 0
                for ch in range(4):
                    for ft in range(11):
                        if i >= B1_HN:
                            break
                        ps = ps_a if i % 2 == 0 else ps_b
                        if i >= 2:
                            t.wait_ge(ctx.h["act"], act0 + i - 1)
                        for dm in range(16):
                            t.matmul(ps[:, :],
                                     wbuf[:, dm * 1408 + ft * 128: dm * 1408 + (ft + 1) * 128],
                                     xfm[:, dm * 2048 + ch * 512: dm * 2048 + (ch + 1) * 512],
                                     start=(dm == 0), stop=(dm == 15))
                        t.drain().then_inc(ctx.h["pe"], 1)
                        i += 1

            @blk.scalar
            def _(s, pe0=pe0, act0=act0):
                i = 0
                for ch in range(4):
                    for ft in range(11):
                        if i >= B1_HN:
                            break
                        ps = ps_a if i % 2 == 0 else ps_b
                        s.wait_ge(ctx.h["pe"], pe0 + i + 1)
                        if B1_EVAC == 1:
                            s.activation(hbuf[:, ft * 2048 + ch * 512: ft * 2048 + (ch + 1) * 512],
                                         ps[:, :], B1_ACT, bias=bcol[:, ft:ft + 1], scale=1.0)
                        elif B1_EVAC == 2:
                            s.copy(hbuf[:, ft * 2048 + ch * 512: ft * 2048 + (ch + 1) * 512],
                                   ps[:, :])
                        s.drain().then_inc(ctx.h["act"], 1)
                        i += 1
        nh = min(44, B1_HN)
        ctx.n["pe"] = pe0 + nh
        ctx.n["act"] = act0 + nh

        if _cut(14 + 10 * sidx):
            return _b1_finish(nc, ctx, stack)
        for ft in range(11):
            dk0 = ctx.n["dk"]
            dve0 = ctx.n["dve"]
            with nc.Block() as blk:
                @blk.gpsimd
                def _(g, ft=ft, dk0=dk0):
                    g.indirect_dma_start(
                        gbuf[:, 0:2048], None, w2_i.ap(),
                        IndirectOffsetOnAxis(idxt[:, 1:2], 0),
                        element_offset=ft * 128 * 2048,
                    ).then_inc(ctx.h["dk"], 16)

                @blk.vector
                def _(v, ft=ft, dk0=dk0):
                    v.wait_ge(ctx.h["dk"], dk0 + 16)
                    v.tensor_scalar(wbuf[:, ft * 2048:(ft + 1) * 2048], gbuf[:, 0:2048],
                                    bcol[:, 32 + sidx:33 + sidx], None, ALU.mult)
                    v.drain().then_inc(ctx.h["dve"], 1)
            ctx.n["dk"] = dk0 + 16
            ctx.n["dve"] = dve0 + 1

        if _cut(15 + 10 * sidx):
            return _b1_finish(nc, ctx, stack)
        if sidx == 0:
            for ch in range(4):
                pe0 = ctx.n["pe"]
                act0 = ctx.n["act"]
                st0 = ctx.n["st"]
                with nc.Block() as blk:
                    @blk.tensor
                    def _(t, ch=ch, pe0=pe0, act0=act0):
                        for u in range(16):
                            ps = ps_a if u % 2 == 0 else ps_b
                            if u >= 2:
                                t.wait_ge(ctx.h["act"], act0 + u - 1)
                            for ft in range(11):
                                t.matmul(ps[:, :],
                                         wbuf[:, ft * 2048 + u * 128: ft * 2048 + (u + 1) * 128],
                                         hbuf[:, ft * 2048 + ch * 512: ft * 2048 + (ch + 1) * 512],
                                         start=(ft == 0), stop=(ft == 10))
                            t.drain().then_inc(ctx.h["pe"], 1)

                    @blk.scalar
                    def _(s, ch=ch, pe0=pe0, act0=act0):
                        for u in range(16):
                            ps = ps_a if u % 2 == 0 else ps_b
                            s.wait_ge(ctx.h["pe"], pe0 + u + 1)
                            s.copy(yfm[:, u * 512:(u + 1) * 512], ps[:, :])
                            s.drain().then_inc(ctx.h["act"], 1)

                    @blk.sync
                    def _(sy, ch=ch, act0=act0, st0=st0):
                        sy.wait_ge(ctx.h["act"], act0 + 16)
                        dst = bass.AP(y0sp, ch * 16 * 128 * 512,
                                      [[512, 128], [128 * 512, 16], [1, 512]])
                        sy.dma_start(dst, yfm[:, :]).then_inc(ctx.h["st"], 16)
                        sy.wait_ge(ctx.h["st"], st0 + 16)
                ctx.n["pe"] = pe0 + 16
                ctx.n["act"] = act0 + 16
                ctx.n["st"] = st0 + 16
        else:
            for ch in range(4):
                ds0 = ctx.n["ds"]
                pe0 = ctx.n["pe"]
                dve0 = ctx.n["dve"]
                with nc.Block() as blk:
                    @blk.sync
                    def _(sy, ch=ch, ds0=ds0):
                        src = bass.AP(y0sp, ch * 16 * 128 * 512,
                                      [[512, 128], [128 * 512, 16], [1, 512]])
                        sy.dma_start(y0ld[:, :], src).then_inc(ctx.h["ds"], 16)

                    @blk.tensor
                    def _(t, ch=ch, pe0=pe0, dve0=dve0):
                        for u in range(16):
                            ps = ps_a if u % 2 == 0 else ps_b
                            if u >= 2:
                                t.wait_ge(ctx.h["dve"], dve0 + u - 1)
                            for ft in range(11):
                                t.matmul(ps[:, :],
                                         wbuf[:, ft * 2048 + u * 128: ft * 2048 + (u + 1) * 128],
                                         hbuf[:, ft * 2048 + ch * 512: ft * 2048 + (ch + 1) * 512],
                                         start=(ft == 0), stop=(ft == 10))
                            t.drain().then_inc(ctx.h["pe"], 1)

                    @blk.vector
                    def _(v, ch=ch, pe0=pe0, dve0=dve0, ds0=ds0):
                        v.wait_ge(ctx.h["ds"], ds0 + 16)
                        for u in range(16):
                            ps = ps_a if u % 2 == 0 else ps_b
                            v.wait_ge(ctx.h["pe"], pe0 + u + 1)
                            v.scalar_tensor_tensor(ctmp[:, :], ps[:, :], 1.0,
                                                   y0ld[:, u * 512:(u + 1) * 512],
                                                   ALU.mult, ALU.add)
                            v.drain()
                            v.tensor_scalar(yfm[:, u * 512:(u + 1) * 512], ctmp[:, :],
                                            bcol[:, 16 + u:17 + u], None, ALU.add)
                            v.drain().then_inc(ctx.h["dve"], 1)

                    @blk.scalar
                    def _(s):
                        pass
                ctx.n["ds"] = ds0 + 16
                ctx.n["pe"] = pe0 + 16
                ctx.n["dve"] = dve0 + 16

                emit_epilogue_chunk(ctx, nc, yfm, resid_i, resid_o, ch * 512,
                                    xraw, ytm, lny, stats, aggr, rstd, eps_col,
                                    ctmp, ps_tp, ps_tp2, c_ident[:, :])

    with nc.Block() as blk:
        @blk.sync
        def _(sy):
            sy.wait_ge(ctx.h["st"], ctx.n["st"])

    stack.close()
    if own:
        ctx.close()
    return nc


def _prep_b1_inmaps(inputs, resids1):
    gw = np.ascontiguousarray(np.asarray(inputs["L0_ffn_gate_W"], dtype=np.float32))
    w1 = np.ascontiguousarray(np.asarray(inputs["L0_ffn_W1"], dtype=np.float32).reshape(6 * 2048, 1408))
    b1 = np.asarray(inputs["L0_ffn_b1"], dtype=np.float32)
    b1c = np.ascontiguousarray(b1.reshape(6, 11, 128).transpose(2, 0, 1).reshape(128, 66))
    w2 = np.ascontiguousarray(np.asarray(inputs["L0_ffn_W2"], dtype=np.float32).reshape(6 * 1408, 2048))
    b2 = np.asarray(inputs["L0_ffn_b2"], dtype=np.float32)
    b2c = np.ascontiguousarray(b2.reshape(6, 16, 128).transpose(2, 0, 1).reshape(128, 96))
    ident = np.eye(128, dtype=BF)
    ident32 = np.eye(128, dtype=np.float32)
    scl = host_scl()
    onesr32 = np.ones((1, 128), dtype=np.float32)
    iota = np.arange(128, dtype=np.float32).reshape(128, 1)
    in_maps = []
    for c in range(NCORE):
        b = c // 4
        bsel = np.zeros((128, 2), dtype=np.float32)
        bsel[:, b] = 1.0
        m = {
            "gate_w": gw, "w1": w1, "b1c": b1c, "w2": w2, "b2c": b2c,
            "ident": ident, "ident32": ident32, "scl": scl,
            "onesr32": onesr32, "iota": iota, "bsel": bsel,
        }
        if resids1[c] is not None:
            m["resid"] = np.ascontiguousarray(resids1[c])
        in_maps.append(m)
    return in_maps



def build_b2(nc=None, ctx=None, pfx="", io=None):
    own = nc is None
    if own:
        nc = bass.Bass("TRN2", target_bir_lowering=False)
        ctx = Ctx(nc)

    resid_i = _io(io, "resid", lambda: nc.dram_tensor("resid", [T, D], F32, kind="ExternalInput"))
    gb_ap = io.get("gb") if io is not None else None
    bidx_i = io.get("bidx") if io is not None else None
    if gb_ap is None:
        xbnd_i = nc.dram_tensor("xbnd", [1024, D], F32, kind="ExternalInput")
    else:
        xbnd_i = None
    wqkv_i = nc.dram_tensor("w_qkv", [3 * D, D], F32, kind="ExternalInput")
    wout_i = nc.dram_tensor("w_out", [D, D], F32, kind="ExternalInput")
    ident_i = _io(io, "ident", lambda: nc.dram_tensor("ident", [128, 128], BF16, kind="ExternalInput"))
    masks_i = _io(io, "masks2", lambda: nc.dram_tensor("masks", [128, 8 * 512], BF16, kind="ExternalInput"))
    bias_i = nc.dram_tensor("bias16", [128, 16], F32, kind="ExternalInput")
    scl_i = _io(io, "scl", lambda: nc.dram_tensor("scl", [128, 8], F32, kind="ExternalInput"))
    ones_i = _io(io, "ones32", lambda: nc.dram_tensor("ones32", [128, 1], F32, kind="ExternalInput"))
    onesr_i = _io(io, "onesr32", lambda: nc.dram_tensor("onesr32", [1, 128], F32, kind="ExternalInput"))

    resid_o = _io(io, "resid_out", lambda: nc.dram_tensor("resid_o", [T, D], F32, kind="ExternalOutput"))

    stack = contextlib.ExitStack()

    def sb(name, shape, dt):
        return stack.enter_context(nc.sbuf_tensor(pfx + name, shape, dt))

    c_ident = sb("c_ident", [128, 128], BF16)
    c_masks = sb("c_masks", [128, 8 * 512], BF16)
    c_bias = sb("c_bias", [128, 16], F32)
    c_scl = sb("c_scl", [128, 8], F32)
    c_ones = sb("c_ones", [128, 1], F32)
    c_onesr = sb("c_onesr", [1, 128], F32)
    c_bidx = sb("c_bidx", [128, 8], I32)
    xkv = sb("xkv", [128, 16 * 1536], BF16)
    qb = sb("qb", [128, 16 * 1024], BF16)
    kb = sb("kb", [128, 16 * 1536], BF16)
    vb = sb("vb", [128, 12 * 2048], BF16)
    rbuf = sb("rbuf", [128, 2048], F32)
    tbuf = sb("tbuf", [128, 4096], BF16)
    lpart = sb("lpart", [128, 512], F32)
    linv = lpart[0:1, :]
    ctmp = sb("ctmp", [128, 512], F32)
    stats = sb("stats", [128, 96], F32)
    aggr = sb("aggr", [128, 8], F32)
    rstd = sb("rstd", [128, 4], F32)
    yfm = xkv[:, 0:8192]
    expp = xkv[:, 8192:9216]
    exppm = xkv[:, 9216:10240]
    wqb = tbuf[:, 0:2048]
    wT = tbuf[:, 2048:4096]
    ytm = tbuf[:, 0:2048]
    lny = tbuf[:, 2048:4096]
    xraw = rbuf[:, 0:2048]
    ps_a = stack.enter_context(nc.psum_tensor(pfx + "ps_a", [128, 512], F32))
    ps_b = stack.enter_context(nc.psum_tensor(pfx + "ps_b", [128, 512], F32))
    eps_col = c_scl[:, 1:2]

    with nc.Block() as blk:
        @blk.sync
        def _(sy):
            consts = [(c_ident, ident_i), (c_masks, masks_i),
                      (c_bias, bias_i), (c_scl, scl_i),
                      (c_ones, ones_i), (c_onesr, onesr_i)]
            if bidx_i is not None:
                consts.append((c_bidx, bidx_i))
            for dst, src in consts:
                sy.dma_start(dst[:, :], src.ap()).then_inc(ctx.h["ds"], 16)
            ctx.n["ds"] += len(consts) * 16
            sy.wait_ge(ctx.h["ds"], ctx.n["ds"])

    for run in range(2):
        _sid_pr, _ = nc.enter_named_scope("prj%d" % run, False)
        tp_stack = contextlib.ExitStack()
        ps_tp = tp_stack.enter_context(nc.psum_tensor(pfx + "b2tp_%d" % (run * 10), [128, 512], F32))
        ps_tp2 = tp_stack.enter_context(nc.psum_tensor(pfx + "b2tp2_%d" % (run * 10), [128, 512], F32))

        for g in range(12):
            if gb_ap is not None and g < 4:
                dk0 = ctx.n["dk"] + 16
                with nc.Block() as blk:
                    @blk.gpsimd
                    def _(gp, run=run, g=g):
                        gp.indirect_dma_start(
                            rbuf[:, :], None, gb_ap.ap(),
                            IndirectOffsetOnAxis(c_bidx[:, run * 4 + g: run * 4 + g + 1], 0),
                        ).then_inc(ctx.h["dk"], 16)
                ctx.n["dk"] = dk0
                ld_wait = ("dk", dk0)
            else:
                ds0 = ctx.n["ds"] + 16
                with nc.Block() as blk:
                    @blk.sync
                    def _(sy, run=run, g=g):
                        if g < 4:
                            src = xbnd_i[run * 512 + g * 128: run * 512 + (g + 1) * 128, :]
                        else:
                            r0 = run * 1024 + (g - 4) * 128
                            src = resid_i[r0:r0 + 128, :]
                        sy.dma_start(rbuf[:, :], src).then_inc(ctx.h["ds"], 16)
                ctx.n["ds"] = ds0
                ld_wait = ("ds", ds0)
            emit_ln_block(ctx, nc, rbuf, 1, stats, aggr, rstd, eps_col,
                          out_ap=tbuf[:, 0:2048], dg_wait=ld_wait)
            emit_tm2fm_gen(ctx, nc, tbuf[:, 0:2048],
                           lambda dt, g=g: xkv[:, dt * 1536 + g * 128: dt * 1536 + (g + 1) * 128],
                           c_ident[:, :], ps_tp, ps_tp2, ntt=1)

        for ft in range(48):
            dg0 = ctx.n["dg"] + 16
            with nc.Block() as blk:
                @blk.gpsimd
                def _(g, ft=ft):
                    g.dma_start(wqb[:, :], wqkv_i[ft * 128:(ft + 1) * 128, :]
                                ).then_inc(ctx.h["dg"], 16)
            ctx.n["dg"] = dg0

            pairs, evacs = [], []
            for j in range(4):
                ps = ps_tp if j % 2 == 0 else ps_tp2
                grp = [(ps[:, k4 * 128:(k4 + 1) * 128],
                        wqb[:, (4 * j + k4) * 128:(4 * j + k4 + 1) * 128])
                       for k4 in range(4)]
                pairs.append(grp)
                evacs.append((wT[:, j * 512:(j + 1) * 512], ps[:, :]))
            emit_transpose(ctx, nc, pairs, c_ident[:, :], evacs,
                           pre_waits=[("dg", dg0)])

            pe0, act0 = ctx.n["pe"], ctx.n["act"]
            with nc.Block() as blk:
                @blk.tensor
                def _(t, ft=ft, pe0=pe0, act0=act0):
                    if ft < 32:
                        chunks = [1, 2] if ft < 16 else [0, 1, 2]
                        for i, c in enumerate(chunks):
                            ps = ps_a if i % 2 == 0 else ps_b
                            if i >= 2:
                                t.wait_ge(ctx.h["act"], act0 + i - 1)
                            for dm in range(16):
                                t.matmul(ps[:, :], wT[:, dm * 128:(dm + 1) * 128],
                                         xkv[:, dm * 1536 + c * 512: dm * 1536 + (c + 1) * 512],
                                         start=(dm == 0), stop=(dm == 15))
                            t.drain().then_inc(ctx.h["pe"], 1)
                    else:
                        for kt in range(12):
                            ps = ps_a if kt % 2 == 0 else ps_b
                            if kt >= 2:
                                t.wait_ge(ctx.h["act"], act0 + kt - 1)
                            for dm in range(16):
                                t.matmul(ps[:, 0:128],
                                         xkv[:, dm * 1536 + kt * 128: dm * 1536 + (kt + 1) * 128],
                                         wT[:, dm * 128:(dm + 1) * 128],
                                         start=(dm == 0), stop=(dm == 15))
                            t.drain().then_inc(ctx.h["pe"], 1)

                @blk.scalar
                def _(s, ft=ft, pe0=pe0, act0=act0):
                    if ft < 32:
                        chunks = [1, 2] if ft < 16 else [0, 1, 2]
                        for i, c in enumerate(chunks):
                            ps = ps_a if i % 2 == 0 else ps_b
                            s.wait_ge(ctx.h["pe"], pe0 + i + 1)
                            if ft < 16:
                                dst = qb[:, ft * 1024 + (c - 1) * 512: ft * 1024 + c * 512]
                            else:
                                dst = kb[:, (ft - 16) * 1536 + c * 512: (ft - 16) * 1536 + (c + 1) * 512]
                            s.copy(dst, ps[:, :])
                            s.drain().then_inc(ctx.h["act"], 1)
                    else:
                        for kt in range(12):
                            ps = ps_a if kt % 2 == 0 else ps_b
                            s.wait_ge(ctx.h["pe"], pe0 + kt + 1)
                            s.copy(vb[:, kt * 2048 + (ft - 32) * 128: kt * 2048 + (ft - 31) * 128],
                                   ps[:, 0:128])
                            s.drain().then_inc(ctx.h["act"], 1)
            ngr = (2 if ft < 16 else 3) if ft < 32 else 12
            ctx.n["pe"] = pe0 + ngr
            ctx.n["act"] = act0 + ngr

        tp_stack.close()

        nc.leave_named_scope("prj%d" % run, _sid_pr, False)
        _sid_at2, _ = nc.enter_named_scope("att%d" % run, False)
        for pos in range(2):
            at_stack = contextlib.ExitStack()
            ps_s0 = at_stack.enter_context(nc.psum_tensor(pfx + "b2s0_%d" % (run * 10 + pos), [128, 1024], F32))
            ps_s1 = at_stack.enter_context(nc.psum_tensor(pfx + "b2s1_%d" % (run * 10 + pos), [128, 1024], F32))
            ps_o = at_stack.enter_context(nc.psum_tensor(pfx + "b2o_%d" % (run * 10 + pos), [128, 512], F32))
            ps_l = at_stack.enter_context(nc.psum_tensor(pfx + "b2l_%d" % (run * 10 + pos), [1, 512], F32))

            for hh in range(16):
                pe0, act0, dve0 = ctx.n["pe"], ctx.n["act"], ctx.n["dve"]
                kpn = 4
                with nc.Block() as blk:
                    @blk.tensor
                    def _(t, run=run, pos=pos, hh=hh, pe0=pe0, act0=act0, dve0=dve0):
                        for kp in range(kpn):
                            ps_s = ps_s0 if kp % 2 == 0 else ps_s1
                            if kp >= 2:
                                t.wait_ge(ctx.h["act"], act0 + kp - 1)
                            ins = None
                            for half in range(2):
                                kt = 2 * kp + half
                                ins = t.matmul(ps_s[:, half * 512:(half + 1) * 512],
                                               kb[:, hh * 1536 + pos * 512 + kt * 128:
                                                  hh * 1536 + pos * 512 + (kt + 1) * 128],
                                               qb[:, hh * 1024 + pos * 512:
                                                  hh * 1024 + (pos + 1) * 512],
                                               start=True, stop=True)
                            ins.then_inc(ctx.h["pe"], 1)
                            t.wait_ge(ctx.h["dve"], dve0 + kp + 1)
                            for half in range(2):
                                kt = 2 * kp + half
                                ins = t.matmul(ps_o[:, :],
                                               vb[:, (pos * 4 + kt) * 2048 + hh * 128:
                                                  (pos * 4 + kt) * 2048 + (hh + 1) * 128],
                                               exppm[:, half * 512:(half + 1) * 512],
                                               start=(kp == 0 and half == 0),
                                               stop=(kp == kpn - 1 and half == 1))
                            ins.then_inc(ctx.h["pe"], 1)
                        t.wait_ge(ctx.h["dve"], dve0 + kpn)
                        t.matmul(ps_l[:, :], c_ones[:, :], lpart[:, :],
                                 start=True, stop=True)
                        t.drain().then_inc(ctx.h["pe"], 1)
                        t.wait_ge(ctx.h["dve"], dve0 + kpn + 1)
                        t.matmul(ps_a[:, :], c_onesr[:, :], linv[:, :],
                                 start=True, stop=True)
                        t.drain().then_inc(ctx.h["pe"], 1)

                    @blk.scalar
                    def _(s, run=run, pos=pos, pe0=pe0, act0=act0, dve0=dve0):
                        col0 = (run * 2 + pos) * 4
                        for kp in range(kpn):
                            ps_s = ps_s0 if kp % 2 == 0 else ps_s1
                            s.wait_ge(ctx.h["pe"], pe0 + 2 * kp + 1)
                            if kp >= 1:
                                s.wait_ge(ctx.h["dve"], dve0 + kp)
                            s.activation(expp[:, :], ps_s[:, :], AF.Exp,
                                         bias=c_bias[:, col0 + kp:col0 + kp + 1],
                                         scale=c_scl[:, 0:1])
                            s.drain().then_inc(ctx.h["act"], 1)
                        s.wait_ge(ctx.h["pe"], pe0 + 2 * kpn)
                        s.copy(ctmp[:, :], ps_o[:, :])
                        s.drain().then_inc(ctx.h["act"], 1)

                    @blk.vector
                    def _(v, pos=pos, hh=hh, pe0=pe0, act0=act0, dve0=dve0):
                        v.memset(lpart[:, :], 0.0)
                        v.drain()

                        for kp in range(kpn):
                            v.wait_ge(ctx.h["act"], act0 + kp + 1)
                            if kp >= 1:
                                v.wait_ge(ctx.h["pe"], pe0 + 2 * kp)
                            v.tensor_tensor(exppm[:, :], expp[:, :],
                                            c_masks[:, kp * 1024:(kp + 1) * 1024],
                                            ALU.mult)
                            v.drain()
                            v.tensor_add(lpart[:, :], lpart[:, :], exppm[:, 0:512])
                            v.drain()
                            v.tensor_add(lpart[:, :], lpart[:, :], exppm[:, 512:1024])
                            v.drain().then_inc(ctx.h["dve"], 1)
                        v.wait_ge(ctx.h["pe"], pe0 + 2 * kpn + 1)
                        v.reciprocal(linv[:, :], ps_l[:, :])
                        v.drain().then_inc(ctx.h["dve"], 1)
                        v.wait_ge(ctx.h["pe"], pe0 + 2 * kpn + 2)
                        v.wait_ge(ctx.h["act"], act0 + kpn + 1)
                        v.tensor_mul(qb[:, hh * 1024 + pos * 512: hh * 1024 + (pos + 1) * 512],
                                     ctmp[:, :], ps_a[:, :])
                        v.drain().then_inc(ctx.h["dve"], 1)

                ctx.n["pe"] = pe0 + 2 * kpn + 2
                ctx.n["act"] = act0 + kpn + 1
                ctx.n["dve"] = dve0 + kpn + 2

            at_stack.close()
            tp_stack = contextlib.ExitStack()
            ps_tp = tp_stack.enter_context(nc.psum_tensor(pfx + "b2tpo_%d" % (run * 10 + pos), [128, 512], F32))
            ps_tp2 = tp_stack.enter_context(nc.psum_tensor(pfx + "b2tpo2_%d" % (run * 10 + pos), [128, 512], F32))

            for u in range(16):
                dg0 = ctx.n["dg"] + 16
                with nc.Block() as blk:
                    @blk.gpsimd
                    def _(g, u=u):
                        g.dma_start(wqb[:, :], wout_i[u * 128:(u + 1) * 128, :]
                                    ).then_inc(ctx.h["dg"], 16)
                ctx.n["dg"] = dg0
                pairs, evacs = [], []
                for j in range(4):
                    ps = ps_tp if j % 2 == 0 else ps_tp2
                    grp = [(ps[:, k4 * 128:(k4 + 1) * 128],
                            wqb[:, (4 * j + k4) * 128:(4 * j + k4 + 1) * 128])
                           for k4 in range(4)]
                    pairs.append(grp)
                    evacs.append((wT[:, j * 512:(j + 1) * 512], ps[:, :]))
                emit_transpose(ctx, nc, pairs, c_ident[:, :], evacs,
                               pre_waits=[("dg", dg0)])

                pe0, act0 = ctx.n["pe"], ctx.n["act"]
                with nc.Block() as blk:
                    @blk.tensor
                    def _(t, pos=pos, u=u, pe0=pe0):
                        ps = ps_a if u % 2 == 0 else ps_b
                        for dm in range(16):
                            t.matmul(ps[:, :], wT[:, dm * 128:(dm + 1) * 128],
                                     qb[:, dm * 1024 + pos * 512: dm * 1024 + (pos + 1) * 512],
                                     start=(dm == 0), stop=(dm == 15))
                        t.drain().then_inc(ctx.h["pe"], 1)

                    @blk.scalar
                    def _(s, u=u, pe0=pe0):
                        s.wait_ge(ctx.h["pe"], pe0 + 1)
                        s.copy(yfm[:, u * 512:(u + 1) * 512], ps_a[:, :] if u % 2 == 0 else ps_b[:, :])
                        s.drain().then_inc(ctx.h["act"], 1)
                ctx.n["pe"] = pe0 + 1
                ctx.n["act"] = act0 + 1

            emit_epilogue_chunk(ctx, nc, yfm, resid_i, resid_o,
                                run * 1024 + pos * 512,
                                xraw, ytm, lny, stats, aggr, rstd, eps_col,
                                ctmp, ps_tp, ps_tp2, c_ident[:, :])
            tp_stack.close()
        nc.leave_named_scope("att%d" % run, _sid_at2, False)

    with nc.Block() as blk:
        @blk.sync
        def _(sy):
            sy.wait_ge(ctx.h["st"], ctx.n["st"])

    stack.close()
    if own:
        ctx.close()
    return nc


def _prep_b2_inmaps(inputs, resids2):
    wqkv = np.ascontiguousarray(np.asarray(inputs["L1_attn_w_qkv"], dtype=np.float32))
    wout = np.ascontiguousarray(np.asarray(inputs["L1_attn_w_out"], dtype=np.float32))
    ident = np.eye(128, dtype=BF)
    masks = host_swa_masks()
    scl = host_scl()
    ones32 = np.ones((128, 1), dtype=np.float32)
    onesr32 = np.ones((1, 128), dtype=np.float32)
    have_r = resids2[0] is not None
    if have_r:
        full = np.zeros((B, S, D), dtype=np.float32)
        for c in range(NCORE):
            b = c // 4
            blkA, blkB = core_blocks(c)
            full[b, 1024 * blkA:1024 * blkA + 1024] = resids2[c][0:1024]
            full[b, 1024 * blkB:1024 * blkB + 1024] = resids2[c][1024:2048]
    in_maps = []
    for c in range(NCORE):
        b = c // 4
        blkA, blkB = core_blocks(c)
        m = {
            "w_qkv": wqkv, "w_out": wout,
            "ident": ident, "masks": masks, "bias16": host_b2_bias(c),
            "scl": scl, "ones32": ones32, "onesr32": onesr32,
        }
        if have_r:
            xbnd = np.zeros((1024, D), dtype=np.float32)
            for r, blk in enumerate([blkA, blkB]):
                if blk > 0:
                    xbnd[r * 512:(r + 1) * 512] = full[b, 1024 * blk - 512:1024 * blk]
            m["resid"] = np.ascontiguousarray(resids2[c])
            m["xbnd"] = xbnd
        in_maps.append(m)
    return in_maps


def run_b2(inputs, resids2):
    nc = build_b2()
    in_maps = _prep_b2_inmaps(inputs, resids2)
    res = run_bass_kernel_spmd(nc, in_maps, core_ids=list(range(8)))
    return [np.asarray(r["resid_o"]) for r in res.results]



def build_b3(nc=None, ctx=None, pfx="", io=None):
    own = nc is None
    if own:
        nc = bass.Bass("TRN2", target_bir_lowering=False)
        ctx = Ctx(nc)

    resid_i = _io(io, "resid", lambda: nc.dram_tensor("resid", [T, D], F32, kind="ExternalInput"))
    gw_i = _io(io, "qgate_w", lambda: nc.dram_tensor("gate_w", [8192, 2048], I32, kind="ExternalInput"))
    uw_i = nc.dram_tensor("up_w", [8192, 2048], I32, kind="ExternalInput")
    dw_i = nc.dram_tensor("down_w", [2048, 8192], I32, kind="ExternalInput")
    gsc_i = nc.dram_tensor("gsc", [128, 64], F32, kind="ExternalInput")
    usc_i = nc.dram_tensor("usc", [128, 64], F32, kind="ExternalInput")
    dsc_i = nc.dram_tensor("dsc", [128, 16], F32, kind="ExternalInput")
    ident_i = _io(io, "ident", lambda: nc.dram_tensor("ident", [128, 128], BF16, kind="ExternalInput"))
    scl_i = _io(io, "scl", lambda: nc.dram_tensor("scl", [128, 8], F32, kind="ExternalInput"))

    hspill = nc.dram_tensor(pfx + "hspill", [64 * 128, 2048], BF16)
    wdtsp = nc.dram_tensor(pfx + "wdtsp", [16 * 128, 8192], BF16)

    resid_o = _io(io, "resid_out", lambda: nc.dram_tensor("resid_o", [T, D], F32, kind="ExternalOutput"))

    stack = contextlib.ExitStack()

    def sb(name, shape, dt):
        return stack.enter_context(nc.sbuf_tensor(pfx + name, shape, dt))

    c_ident = sb("c_ident", [128, 128], BF16)
    c_gsc = sb("c_gsc", [128, 64], F32)
    c_usc = sb("c_usc", [128, 64], F32)
    c_dsc = sb("c_dsc", [128, 16], F32)
    c_scl = sb("c_scl", [128, 8], F32)
    xfm = sb("xfm", [128, 16 * 2048], BF16)
    gi32 = sb("gi32", [128, 4096], I32)
    ui32 = sb("ui32", [128, 4096], I32)
    wgb = sb("wgb", [128, 2048], BF16)
    wub = sb("wub", [128, 2048], BF16)
    wgT = sb("wgT", [128, 2048], BF16)
    wuT = sb("wuT", [128, 2048], BF16)
    hb = sb("hb", [128, 2048], BF16)
    wdld = sb("wdld", [128, 2 * 8192], BF16)
    rbuf = sb("rbuf", [128, 2 * 2048], F32)
    tbuf = sb("tbuf", [128, 2 * 2048], BF16)
    yfm = sb("yfm", [128, 16 * 512], BF16)
    ctmp = sb("ctmp", [128, 512], F32)
    stats = sb("stats", [128, 96], F32)
    aggr = sb("aggr", [128, 8], F32)
    rstd = sb("rstd", [128, 4], F32)
    ytm = tbuf[:, 0:2048]
    lny = tbuf[:, 2048:4096]
    xraw = rbuf[:, 0:2048]
    sg = tbuf[:, 0:512]
    su = tbuf[:, 512:1024]
    di32 = xfm.bitcast(I32)
    wdb = xfm[:, 16384:24576]
    wdT = xfm[:, 24576:32768]
    h64 = xfm
    ps_a = stack.enter_context(nc.psum_tensor(pfx + "ps_a", [128, 512], F32))
    ps_b = stack.enter_context(nc.psum_tensor(pfx + "ps_b", [128, 512], F32))
    ps_tp = stack.enter_context(nc.psum_tensor(pfx + "ps_tp", [128, 512], F32))
    ps_tp2 = stack.enter_context(nc.psum_tensor(pfx + "ps_tp2", [128, 512], F32))
    eps_col = c_scl[:, 1:2]

    with nc.Block() as blk:
        @blk.sync
        def _(sy):
            for dst, src in [(c_ident, ident_i), (c_gsc, gsc_i), (c_usc, usc_i),
                             (c_dsc, dsc_i), (c_scl, scl_i)]:
                sy.dma_start(dst[:, :], src.ap()).then_inc(ctx.h["ds"], 16)
            ctx.n["ds"] += 5 * 16
            sy.wait_ge(ctx.h["ds"], ctx.n["ds"])

    _sid1, _ = nc.enter_named_scope("q_xln", False)
    emit_xln_phase(ctx, nc, resid_i, 0, T,
                   lambda g, dt: xfm[:, dt * 2048 + g * 256: dt * 2048 + (g + 1) * 256],
                   rbuf, tbuf, stats, aggr, rstd, eps_col, c_ident[:, :],
                   ps_tp, ps_tp2, ntt=2)

    nc.leave_named_scope("q_xln", _sid1, False)
    _sid2, _ = nc.enter_named_scope("q_gu", False)
    p2sem = ["ds", "dg"]
    with nc.Block() as blk:
        @blk.sync
        def _(sy):
            sy.dma_start(gi32[:, 0:2048], gw_i[0:128, :]).then_inc(ctx.h["ds"], 16)
            sy.dma_start(ui32[:, 0:2048], uw_i[0:128, :]).then_inc(ctx.h["ds"], 16)
            ctx.n["ds"] += 32
    for it in range(64):
        sl = it % 2
        ld_tot = ctx.n[p2sem[sl]]
        dve0 = ctx.n["dve"]
        with nc.Block() as blk:
            @blk.sync
            def _(sy, it=it):
                if it + 1 < 64:
                    s2 = (it + 1) % 2
                    sy.dma_start(gi32[:, s2 * 2048:(s2 + 1) * 2048],
                                 gw_i[(it + 1) * 128:(it + 2) * 128, :]
                                 ).then_inc(ctx.h[p2sem[s2]], 16)
                    sy.dma_start(ui32[:, s2 * 2048:(s2 + 1) * 2048],
                                 uw_i[(it + 1) * 128:(it + 2) * 128, :]
                                 ).then_inc(ctx.h[p2sem[s2]], 16)
                    ctx.n[p2sem[s2]] += 32

            @blk.vector
            def _(v, it=it, sl=sl, ld_tot=ld_tot, dve0=dve0):
                v.wait_ge(ctx.h[p2sem[sl]], ld_tot)
                v.tensor_scalar(wgb[:, :], gi32[:, sl * 2048:(sl + 1) * 2048],
                                -128.0, c_gsc[:, it:it + 1], ALU.add, ALU.mult)
                v.drain()
                v.tensor_scalar(wub[:, :], ui32[:, sl * 2048:(sl + 1) * 2048],
                                -128.0, c_usc[:, it:it + 1], ALU.add, ALU.mult)
                v.drain().then_inc(ctx.h["dve"], 1)

            @blk.scalar
            def _(s):
                pass
        ctx.n["dve"] = dve0 + 1

        pairs, evacs = [], []
        for m, (srcb, dstb) in enumerate([(wgb, wgT), (wub, wuT)]):
            for j in range(4):
                ps = ps_tp if (m * 4 + j) % 2 == 0 else ps_tp2
                grp = [(ps[:, k4 * 128:(k4 + 1) * 128],
                        srcb[:, (4 * j + k4) * 128:(4 * j + k4 + 1) * 128])
                       for k4 in range(4)]
                pairs.append(grp)
                evacs.append((dstb[:, j * 512:(j + 1) * 512], ps[:, :]))
        emit_transpose(ctx, nc, pairs, c_ident[:, :], evacs,
                       pre_waits=[("dve", ctx.n["dve"])])

        st0 = ctx.n["st"]
        pe0, act0, dve0 = ctx.n["pe"], ctx.n["act"], ctx.n["dve"]
        with nc.Block() as blk:
            @blk.tensor
            def _(t, it=it, pe0=pe0, act0=act0):
                for ch in range(4):
                    if ch >= 1:
                        t.wait_ge(ctx.h["act"], act0 + 2 * ch - 1)
                    for dm in range(16):
                        t.matmul(ps_a[:, :], wgT[:, dm * 128:(dm + 1) * 128],
                                 xfm[:, dm * 2048 + ch * 512: dm * 2048 + (ch + 1) * 512],
                                 start=(dm == 0), stop=(dm == 15))
                    t.drain().then_inc(ctx.h["pe"], 1)
                    if ch >= 1:
                        t.wait_ge(ctx.h["act"], act0 + 2 * ch)
                    for dm in range(16):
                        t.matmul(ps_b[:, :], wuT[:, dm * 128:(dm + 1) * 128],
                                 xfm[:, dm * 2048 + ch * 512: dm * 2048 + (ch + 1) * 512],
                                 start=(dm == 0), stop=(dm == 15))
                    t.drain().then_inc(ctx.h["pe"], 1)

            @blk.scalar
            def _(s, it=it, pe0=pe0, act0=act0, dve0=dve0):
                for ch in range(4):
                    s.wait_ge(ctx.h["pe"], pe0 + 2 * ch + 1)
                    if ch >= 1:
                        s.wait_ge(ctx.h["dve"], dve0 + ch)
                    s.activation(sg[:, :], ps_a[:, :], AF.Silu)
                    s.drain().then_inc(ctx.h["act"], 1)
                    s.wait_ge(ctx.h["pe"], pe0 + 2 * ch + 2)
                    s.copy(su[:, :], ps_b[:, :])
                    s.drain().then_inc(ctx.h["act"], 1)

            @blk.vector
            def _(v, it=it, act0=act0, dve0=dve0):
                for ch in range(4):
                    v.wait_ge(ctx.h["act"], act0 + 2 * ch + 2)
                    v.tensor_tensor(hb[:, ch * 512:(ch + 1) * 512], sg[:, :], su[:, :],
                                    ALU.mult)
                    v.drain().then_inc(ctx.h["dve"], 1)

            @blk.sync
            def _(sy, it=it, dve0=dve0, st0=st0):
                sy.wait_ge(ctx.h["dve"], dve0 + 4)
                sy.dma_start(hspill[it * 128:(it + 1) * 128, :], hb[:, :]
                             ).then_inc(ctx.h["st"], 16)
                sy.wait_ge(ctx.h["st"], st0 + 16)
        ctx.n["pe"] = pe0 + 8
        ctx.n["act"] = act0 + 8
        ctx.n["dve"] = dve0 + 4
        ctx.n["st"] = st0 + 16

    nc.leave_named_scope("q_gu", _sid2, False)
    _sid3, _ = nc.enter_named_scope("q_wd", False)
    for dmt in range(16):
        ds0 = ctx.n["ds"] + 16
        with nc.Block() as blk:
            @blk.sync
            def _(sy, dmt=dmt):
                sy.dma_start(di32[:, 0:8192], dw_i[dmt * 128:(dmt + 1) * 128, :]
                             ).then_inc(ctx.h["ds"], 16)
        ctx.n["ds"] = ds0

        dve0 = ctx.n["dve"]
        with nc.Block() as blk:
            @blk.vector
            def _(v, dmt=dmt, ds0=ds0, dve0=dve0):
                v.wait_ge(ctx.h["ds"], ds0)
                v.tensor_scalar(wdb[:, :], di32[:, 0:8192], -128.0,
                                c_dsc[:, dmt:dmt + 1], ALU.add, ALU.mult)
                v.drain().then_inc(ctx.h["dve"], 1)

            @blk.scalar
            def _(s):
                pass
        ctx.n["dve"] = dve0 + 1

        pairs, evacs = [], []
        for j in range(16):
            ps = ps_tp if j % 2 == 0 else ps_tp2
            grp = [(ps[:, k4 * 128:(k4 + 1) * 128],
                    wdb[:, (4 * j + k4) * 128:(4 * j + k4 + 1) * 128])
                   for k4 in range(4)]
            pairs.append(grp)
            evacs.append((wdT[:, j * 512:(j + 1) * 512], ps[:, :]))
        emit_transpose(ctx, nc, pairs, c_ident[:, :], evacs,
                       pre_waits=[("dve", ctx.n["dve"])])

        st0 = ctx.n["st"]
        with nc.Block() as blk:
            @blk.sync
            def _(sy, dmt=dmt, st0=st0):
                sy.dma_start(wdtsp[dmt * 128:(dmt + 1) * 128, :], wdT[:, :]
                             ).then_inc(ctx.h["st"], 16)
                sy.wait_ge(ctx.h["st"], st0 + 16)
        ctx.n["st"] = st0 + 16

    nc.leave_named_scope("q_wd", _sid3, False)
    _sid4, _ = nc.enter_named_scope("q_down", False)
    for ch in range(4):
        ds0 = ctx.n["ds"] + 64 * 16
        with nc.Block() as blk:
            @blk.sync
            def _(sy, ch=ch):
                for it in range(64):
                    src = bass.AP(hspill, (it * 128) * 2048 + ch * 512,
                                  [[2048, 128], [1, 512]])
                    sy.dma_start(h64[:, it * 512:(it + 1) * 512], src
                                 ).then_inc(ctx.h["ds"], 16)
        ctx.n["ds"] = ds0

        p4sem = ["dg", "dk"]
        with nc.Block() as blk:
            @blk.gpsimd
            def _(g, ch=ch):
                g.dma_start(wdld[:, 0:8192], wdtsp[0:128, :]).then_inc(ctx.h["dg"], 16)
                ctx.n["dg"] += 16
        for dmt in range(16):
            sl = dmt % 2
            wd_tot = ctx.n[p4sem[sl]]
            pe0, act0 = ctx.n["pe"], ctx.n["act"]
            with nc.Block() as blk:
                @blk.gpsimd
                def _(g, dmt=dmt):
                    if dmt + 1 < 16:
                        s2 = (dmt + 1) % 2
                        g.dma_start(wdld[:, s2 * 8192:(s2 + 1) * 8192],
                                    wdtsp[(dmt + 1) * 128:(dmt + 2) * 128, :]
                                    ).then_inc(ctx.h[p4sem[s2]], 16)
                        ctx.n[p4sem[s2]] += 16

                @blk.tensor
                def _(t, ch=ch, dmt=dmt, sl=sl, pe0=pe0, wd_tot=wd_tot, ds0=ds0):
                    t.wait_ge(ctx.h["ds"], ds0)
                    t.wait_ge(ctx.h[p4sem[sl]], wd_tot)
                    ps = ps_a if dmt % 2 == 0 else ps_b
                    for it in range(64):
                        t.matmul(ps[:, :], wdld[:, sl * 8192 + it * 128: sl * 8192 + (it + 1) * 128],
                                 h64[:, it * 512:(it + 1) * 512],
                                 start=(it == 0), stop=(it == 63))
                    t.drain().then_inc(ctx.h["pe"], 1)

                @blk.scalar
                def _(s, dmt=dmt, pe0=pe0):
                    ps = ps_a if dmt % 2 == 0 else ps_b
                    s.wait_ge(ctx.h["pe"], pe0 + 1)
                    s.copy(yfm[:, dmt * 512:(dmt + 1) * 512], ps[:, :])
                    s.drain().then_inc(ctx.h["act"], 1)
            ctx.n["pe"] = pe0 + 1
            ctx.n["act"] = act0 + 1

        emit_epilogue_chunk(ctx, nc, yfm, resid_i, resid_o, ch * 512,
                            xraw, ytm, lny, stats, aggr, rstd, eps_col,
                            ctmp, ps_tp, ps_tp2, c_ident[:, :])

    with nc.Block() as blk:
        @blk.sync
        def _(sy):
            sy.wait_ge(ctx.h["st"], ctx.n["st"])
    nc.leave_named_scope("q_down", _sid4, False)

    stack.close()
    if own:
        ctx.close()
    return nc


def _prep_b3_inmaps(inputs, resids3):
    gw = np.ascontiguousarray(np.asarray(inputs["L1_ffn_gate_w"], dtype=np.int32))
    uw = np.ascontiguousarray(np.asarray(inputs["L1_ffn_up_w"], dtype=np.int32))
    dw = np.ascontiguousarray(np.asarray(inputs["L1_ffn_down_w"], dtype=np.int32))
    gsc = np.ascontiguousarray(np.asarray(inputs["L1_ffn_gate_scale"], dtype=np.float32
                                          ).reshape(64, 128).T)
    usc = np.ascontiguousarray(np.asarray(inputs["L1_ffn_up_scale"], dtype=np.float32
                                          ).reshape(64, 128).T)
    dsc = np.ascontiguousarray(np.asarray(inputs["L1_ffn_down_scale"], dtype=np.float32
                                          ).reshape(16, 128).T)
    ident = np.eye(128, dtype=BF)
    scl = host_scl()
    in_maps = []
    for c in range(NCORE):
        m = {
            "gate_w": gw, "up_w": uw, "down_w": dw,
            "gsc": gsc, "usc": usc, "dsc": dsc,
            "ident": ident, "scl": scl,
        }
        if resids3[c] is not None:
            m["resid"] = np.ascontiguousarray(resids3[c])
        in_maps.append(m)
    return in_maps


def run_b3(inputs, resids3):
    nc = build_b3()
    in_maps = _prep_b3_inmaps(inputs, resids3)
    res = run_bass_kernel_spmd(nc, in_maps, core_ids=list(range(8)))
    return [np.asarray(r["resid_o"]) for r in res.results]



def host_bidx(c):
    idx = np.zeros((128, 8), dtype=np.int32)
    blkA, blkB = core_blocks(c)
    for r, blk in enumerate([blkA, blkB]):
        if blk == 0:
            base = 8192
        else:
            prev = blk - 1
            jp = prev if prev <= 3 else 7 - prev
            o = (c // 4) * 4 + jp
            slab = 0 if prev <= 3 else 1
            base = o * 1024 + slab * 512
        for t in range(4):
            idx[:, r * 4 + t] = base + t * 128 + np.arange(128)
    return idx


def build_fused():
    nc = bass.Bass("TRN2", target_bir_lowering=False)
    ctx = Ctx(nc)

    x_full = nc.dram_tensor("xkv", [S, D], F32, kind="ExternalInput")
    x_own = nc.dram_tensor("xown", [T, D], F32, kind="ExternalInput")
    ident = nc.dram_tensor("ident", [128, 128], BF16, kind="ExternalInput")
    scl = nc.dram_tensor("scl", [128, 8], F32, kind="ExternalInput")
    ones32 = nc.dram_tensor("ones32", [128, 1], F32, kind="ExternalInput")
    onesr32 = nc.dram_tensor("onesr32", [1, 128], F32, kind="ExternalInput")
    masks2 = nc.dram_tensor("masks2", [128, 8 * 512], BF16, kind="ExternalInput")
    qgate_w = nc.dram_tensor("qgate_w", [8192, 2048], I32, kind="ExternalInput")
    bidx = nc.dram_tensor("bidx", [128, 8], I32, kind="ExternalInput")

    r1 = nc.dram_tensor("f_r1", [T, D], F32)
    r2 = nc.dram_tensor("f_r2", [T, D], F32)
    r3 = nc.dram_tensor("f_r3", [T, D], F32)
    ccb = nc.dram_tensor("f_ccb", [1024, D], F32)
    gb = nc.dram_tensor("f_gb", [9216, D], F32)

    resid_o = nc.dram_tensor("resid_o", [T, D], F32, kind="ExternalOutput")

    shared = {"ident": ident, "scl": scl, "ones32": ones32, "onesr32": onesr32}

    with nc.named_scope("ph_b0"):
        build_b0(nc, ctx, "a_", dict(shared, x_full=x_full, x_own=x_own, resid_out=r1))
    with nc.named_scope("ph_b1"):
        build_b1(nc, ctx, "b_", dict(shared, resid=r1, resid_out=r2))

    zstack = contextlib.ExitStack()
    zbuf = zstack.enter_context(nc.sbuf_tensor("f_zbuf", [128, 2048], F32))
    with nc.Block() as blk:
        @blk.vector
        def _(v):
            v.memset(zbuf[:, :], 0.0)
            v.drain().then_inc(ctx.h["dve"], 1)
            ctx.n["dve"] += 1

        @blk.scalar
        def _(s):
            pass

    with nc.Block() as blk:
        @blk.gpsimd
        def _(g):
            g.dma_start(ccb.ap()[0:512, :], r2.ap()[512:1024, :]).then_inc(ctx.h["dg"], 16)
            g.dma_start(ccb.ap()[512:1024, :], r2.ap()[1536:2048, :]).then_inc(ctx.h["dg"], 16)
            ctx.n["dg"] += 32
            g.wait_ge(ctx.h["dg"], ctx.n["dg"])
            g.collective_compute(
                "AllGather", ALU.bypass, replica_groups=[list(range(8))],
                ins=[ccb.ap().opt()], outs=[gb.ap()[0:8192, :].opt()],
            ).then_inc(ctx.h["cc"], 1)
            ctx.n["cc"] += 1
            g.wait_ge(ctx.h["cc"], ctx.n["cc"])

        @blk.sync
        def _(sy):
            sy.wait_ge(ctx.h["dve"], ctx.n["dve"])
            for t in range(4):
                sy.dma_start(gb.ap()[8192 + t * 128: 8192 + (t + 1) * 128, :],
                             zbuf[:, :]).then_inc(ctx.h["st"], 16)
            ctx.n["st"] += 64
            sy.wait_ge(ctx.h["st"], ctx.n["st"])
    zstack.close()

    with nc.named_scope("ph_b2"):
        build_b2(nc, ctx, "c_", dict(shared, resid=r2, resid_out=r3, gb=gb, bidx=bidx,
                                     masks2=masks2))
    with nc.named_scope("ph_b3"):
        build_b3(nc, ctx, "d_", dict(shared, resid=r3, resid_out=resid_o, qgate_w=qgate_w))

    ctx.close()
    return nc


def _prep_fused_inmaps(inputs):
    m0 = _prep_b0_inmaps(inputs)
    m1 = _prep_b1_inmaps(inputs, [None] * NCORE)
    m2 = _prep_b2_inmaps(inputs, [None] * NCORE)
    m3 = _prep_b3_inmaps(inputs, [None] * NCORE)
    in_maps = []
    for c in range(NCORE):
        m = {}
        m.update({k: v for k, v in m0[c].items()})
        m.update({k: v for k, v in m1[c].items() if k != "resid"})
        m.update({k: v for k, v in m2[c].items() if k not in ("resid", "xbnd", "masks")})
        m["masks2"] = m2[c]["masks"]
        m.update({k: v for k, v in m3[c].items() if k not in ("resid", "gate_w")})
        m["qgate_w"] = m3[c]["gate_w"]
        m["bidx"] = host_bidx(c)
        in_maps.append(m)
    return in_maps
